# version 06
# baseline (speedup 1.0000x reference)
import numpy as np
import ml_dtypes
from contextlib import ExitStack

import concourse.bass as bass
import concourse.mybir as mybir
from concourse.alu_op_type import AluOpType
from concourse.bass_utils import run_bass_kernel_spmd

F32 = mybir.dt.float32
BF16 = mybir.dt.bfloat16
AF = mybir.ActivationFunctionType
BB = ml_dtypes.bfloat16

P = 128
D = 2048
KC = 16
NH = 16
NKV = 4
WIN = 512
ISZ = 8192
NIC = 64
TILE = 512
NSUB = 4
HALO = 512
NT = 8
NSL = 16
MW = 640

f32 = np.float32
ENGS = ("sync", "gpsimd", "tensor", "scalar", "vector")


class Sem:
    def __init__(self, nc, es, name, step=1):
        self.h = es.enter_context(nc.semaphore(name))
        self.n = 0
        self.step = step
        self.name = name

    def reserve(self):
        self.n += self.step
        return self.n


class Stage:
    def __init__(self):
        self.lists = {e: [] for e in ENGS}

    def add(self, eng, fn):
        self.lists[eng].append(fn)


class MK:
    def __init__(self, n_tiles=NT, debug=False):
        self.n_tiles = n_tiles
        self.debug = debug
        self.dbg_names = []
        nc = bass.Bass("TRN2", target_bir_lowering=False)
        self.nc = nc
        self.es = ExitStack()
        es = self.es
        ntok = HALO + n_tiles * TILE

        dram = nc.dram_tensor
        self.x = dram("x", [ntok, D], F32, kind="ExternalInput")
        self.Wq = dram("L0_attn_Wq", [D, D], F32, kind="ExternalInput")
        self.Wk = dram("L0_attn_Wk", [D, 512], F32, kind="ExternalInput")
        self.Wv = dram("L0_attn_Wv", [D, 512], F32, kind="ExternalInput")
        self.Wo = dram("L0_attn_Wo", [D, D], F32, kind="ExternalInput")
        self.Wg = dram("L0_ffn_w_gate", [D, ISZ], F32, kind="ExternalInput")
        self.Wu = dram("L0_ffn_w_up", [D, ISZ], F32, kind="ExternalInput")
        self.Wd = dram("L0_ffn_w_down", [ISZ, D], F32, kind="ExternalInput")
        self.mask = dram("mask01", [P, 5 * MW], BF16, kind="ExternalInput")
        self.ident = dram("ident", [P, P], BF16, kind="ExternalInput")
        self.out = dram("out", [n_tiles * TILE, D], F32, kind="ExternalOutput")
        self.sWq = dram("sWq", [D, D], BF16)
        self.sWk = dram("sWk", [512, D], BF16)
        self.sWv = dram("sWv", [P, KC * 512], BF16)
        self.sWo = dram("sWo", [D, D], BF16)
        self.sWg = dram("sWg", [NSL * P, KC * 512], BF16)
        self.sWu = dram("sWu", [NSL * P, KC * 512], BF16)
        self.sWd = dram("sWd", [KC * P, NIC * P], BF16)

        self.s_pe = Sem(nc, es, "s_pe")
        self.s_act = Sem(nc, es, "s_act")
        self.s_dve = Sem(nc, es, "s_dve")
        self.s_pool = Sem(nc, es, "s_pool")
        self.s_cst = Sem(nc, es, "s_cst", 16)
        self.s_dbg = Sem(nc, es, "s_dbg", 16)
        self.dsems = {}
        self.waited = {}
        self.slot_wb = {}
        self.fence_sems = set()

        sb = lambda name, cols, dt: es.enter_context(nc.sbuf_tensor(name, [P, cols], dt))
        self.AR1 = sb("AR1", 50688, BF16)
        self.AR2 = sb("AR2", 8192, BF16)
        self.AR3 = sb("AR3", 4096, F32)
        self.x1_bf = sb("x1_bf", NSUB * D, BF16)
        self.x1_fm = sb("x1_fm", KC * TILE, BF16)
        self.x_tm_bf = sb("x_tm_bf", NSUB * D, BF16)
        self.ident_sb = sb("ident_sb", P, BF16)
        self.k_roll = sb("k_roll", 2 * NKV * 512, BF16)
        self.v_roll = sb("v_roll", 2 * NSUB * 512, BF16)
        self.den4 = sb("den4", 16, F32)
        self.rec4 = sb("rec4", 16, F32)
        self.stats = sb("stats", 24, F32)
        self.mv = sb("mv", 2, F32)
        self.lnv = sb("lnv", 1, F32)
        self.rstd = sb("rstd", 1, F32)

        self.o_xbf = 0
        self.o_xfm = 4096
        self.o_wq = 12288
        self.o_wk = 16384
        self.o_wv = 20480
        self.o_q = 28672
        self.o_pe_ = 36864
        self.o_pm = 38144
        self.o_pT = 45824
        self.o_wo = 0
        self.o_wg = 0
        self.o_wu = 8192
        self.o_hbuf = 16384
        self.o_habs = 30208
        self.o_thr = 32256
        self.o_hspt = 24576
        self.o_silu = 28672
        self.o_dfm = 29696
        self.o_r2 = 30208
        self.o_hspf = 34304
        self.o_mask = 0
        self.o_r1b = 34304

        ps = lambda name, cols, dt: es.enter_context(nc.psum_tensor(name, [P, cols], dt))
        self.sc0 = ps("sc0", 1024, F32)
        self.sc1 = ps("sc1", 1024, F32)
        self.tA = ps("tA", 1024, BF16)
        self.tB = ps("tB", 1024, BF16)
        self.pA = ps("pA", 512, F32)
        self.pB = ps("pB", 512, F32)

        self.wb_tok = {}
        self.slot = {}
        self.stage = None

    def v1(self, off, cols):
        return self.AR1[:, off:off + cols]

    def dsem(self, name):
        if name not in self.dsems:
            self.dsems[name] = Sem(self.nc, self.es, "sd_" + name, 16)
        return self.dsems[name]

    def wait(self, eng, sem, tok):
        if not tok:
            return
        key = (eng, sem.name)
        if self.waited.get(key, 0) >= tok:
            return
        self.waited[key] = tok
        self.stage.add(eng, lambda E, s=sem.h, t=tok: E.wait_ge(s, t))

    def op(self, eng, sem, builder):
        if sem is None:
            self.stage.add(eng, lambda E, b=builder: b(E))
            return None
        tok = sem.reserve()
        self.stage.add(eng, lambda E, b=builder, h=sem.h: b(E).then_inc(h, 1))
        return tok

    def dma(self, eng, sem, dst, src):
        tok = sem.reserve()
        self.stage.add(eng, lambda E, d=dst, a=src, h=sem.h: E.dma_start(d, a).then_inc(h, 16))
        return tok

    def open(self):
        self.stage = Stage()

    def close(self):
        st = self.stage
        for sem in sorted(self.fence_sems, key=lambda x: x.name):
            st.add("sync", lambda E, h=sem.h, n=sem.n: E.wait_ge(h, n))
            self.waited[("sync", sem.name)] = sem.n
        with self.nc.Block() as block:
            for e in ENGS:
                fns = st.lists[e]
                if fns:
                    def runner(E, fns=fns):
                        for f in fns:
                            f(E)
                    getattr(block, e)(runner)
        for sem in (self.s_pe, self.s_act, self.s_dve, self.s_pool):
            for e in ENGS:
                key = (e, sem.name)
                self.waited[key] = max(self.waited.get(key, 0), sem.n)
        names = {k[1] for k in self.waited}
        for nm in names:
            m = max(self.waited.get((e, nm), 0) for e in ENGS)
            for e in ENGS:
                self.waited[(e, nm)] = m
        self.fence_sems = set()
        self.stage = None

    def load_unit(self, key, slot_sem, dst, scratch, dep=None, queue="sync"):
        s = self
        eng = "sync" if queue == "sync" else "scalar"
        if dep:
            s.wait(eng, dep[0], dep[1])
        tok = s.dma(eng, slot_sem, dst, scratch)
        return (slot_sem, tok)

    def stage_precast(self):
        s = self
        s.open()
        chunks = []
        wqr = s.Wq.ap().rearrange("(kc p) (h m) -> p kc h m", p=P, m=P)
        for h in range(NH):
            chunks.append((s.sWq.ap()[h * P:(h + 1) * P, :], wqr[:, :, h, :]))
        wkr = s.Wk.ap().rearrange("(kc p) (g m) -> p kc g m", p=P, m=P)
        for g in range(NKV):
            chunks.append((s.sWk.ap()[g * P:(g + 1) * P, :], wkr[:, :, g, :]))
        wvr = s.Wv.ap().rearrange("(kc p) c -> p kc c", p=P)
        for q in range(4):
            chunks.append((s.sWv.ap()[:, q * 2048:(q + 1) * 2048],
                           wvr[:, q * 4:(q + 1) * 4, :]))
        wor = s.Wo.ap().rearrange("(ic p) (oc m) -> p ic oc m", p=P, m=P)
        for oc in range(KC):
            chunks.append((s.sWo.ap()[oc * P:(oc + 1) * P, :], wor[:, :, oc, :]))
        wgr = s.Wg.ap().rearrange("(kc p) i -> p kc i", p=P)
        wur = s.Wu.ap().rearrange("(kc p) i -> p kc i", p=P)
        for sl in range(NSL):
            for q in range(4):
                chunks.append((s.sWg.ap()[sl * P:(sl + 1) * P, q * 2048:(q + 1) * 2048],
                               wgr[:, q * 4:(q + 1) * 4, sl * 512:(sl + 1) * 512]))
        for sl in range(NSL):
            for q in range(4):
                chunks.append((s.sWu.ap()[sl * P:(sl + 1) * P, q * 2048:(q + 1) * 2048],
                               wur[:, q * 4:(q + 1) * 4, sl * 512:(sl + 1) * 512]))
        wdr = s.Wd.ap().rearrange("(ic p) (oc m) -> p ic oc m", p=P, m=P)
        for oc in range(KC):
            for q in range(4):
                chunks.append((s.sWd.ap()[oc * P:(oc + 1) * P, q * 2048:(q + 1) * 2048],
                               wdr[:, q * 16:(q + 1) * 16, oc, :]))
        ldsem = [s.dsem("pcl0"), s.dsem("pcl1")]
        wbsem = [s.dsem("pc0_wb"), s.dsem("pc1_wb")]
        cast_tok = [None, None]
        qs = ["sync", "scalar"]
        for i, (scr, gat) in enumerate(chunks):
            sl_ = i % 2
            qld = qs[sl_]
            qwb = qs[1 - sl_]
            stg = s.AR3[:, sl_ * 2048:(sl_ + 1) * 2048]
            bfs = s.v1(s.o_xbf + sl_ * D, D)
            if cast_tok[sl_]:
                s.wait(qld, s.s_dve, cast_tok[sl_])
            ltok = s.dma(qld, ldsem[sl_], stg, gat)
            s.wait("vector", ldsem[sl_], ltok)
            if i >= 2:
                s.wait("vector", wbsem[sl_], wbsem[sl_].n)
            ctok = s.op("vector", s.s_dve, lambda E, d=bfs, a=stg: E.tensor_copy(d, a))
            cast_tok[sl_] = ctok
            s.wait(qwb, s.s_dve, ctok)
            s.dma(qwb, wbsem[sl_], scr, bfs)
        s.fence_sems.add(wbsem[0])
        s.fence_sems.add(wbsem[1])
        s.close()

    def dbg(self, name, src_ap, cols, dt):
        if not self.debug:
            return
        t = self.nc.dram_tensor("dbg_" + name, [P, cols], dt, kind="ExternalOutput")
        self.dbg_names.append("dbg_" + name)
        self.open()
        self.dma("sync", self.s_dbg, t.ap(), src_ap)
        self.close()

    def stage_X(self, t):
        s = self
        s.open()
        row0 = 0 if t == -1 else HALO + t * TILE
        xfm = s.v1(s.o_xfm, KC * TILE).rearrange("p (kc q) -> p kc q", q=TILE)
        x_tok = {}

        def issue_x(sub):
            sl2 = sub % 2
            stg_ = s.AR3[:, sl2 * 2048:(sl2 + 1) * 2048]
            depc = s.slot.get(("xstg", sl2))
            if depc:
                s.wait("sync", *depc)
            dob = s.slot.get(("osb", sl2))
            if dob:
                s.wait("sync", *dob)
            x_tok[sub] = s.dma("sync", s.dsem("xb%d" % sl2), stg_,
                               s.x.ap()[row0 + sub * P:row0 + (sub + 1) * P, :])
        issue_x(0)
        issue_x(1)
        for sub in range(NSUB):
            sl_ = sub % 2
            xbf = s.x_tm_bf[:, sub * D:(sub + 1) * D]
            stg = s.AR3[:, sl_ * 2048:(sl_ + 1) * 2048]
            xsem = s.dsem("xb%d" % sl_)
            ltok = x_tok[sub]
            s.wait("vector", xsem, ltok)
            ctok = s.op("vector", s.s_dve, lambda E, d=xbf, a=stg: E.tensor_copy(d, a))
            s.slot[("xstg", sl_)] = (s.s_dve, ctok)
            if sub + 2 < NSUB:
                issue_x(sub + 2)
            s.wait("tensor", s.s_dve, ctok)
            last_pe = None
            for half in range(2):
                tps = s.tA if half == 0 else s.tB
                dep2 = s.slot.get(("tps", half))
                if dep2:
                    s.wait("tensor", *dep2)
                srcs = [xbf[:, (half * 8 + j) * P:(half * 8 + j + 1) * P] for j in range(8)]
                dsts = [tps[:, j * P:(j + 1) * P] for j in range(8)]
                def ftr(E, srcs=srcs, dsts=dsts):
                    for d_, a_ in zip(dsts, srcs):
                        tr = E.transpose(d_, a_, s.ident_sb[:, :])
                    return tr
                petok = s.op("tensor", s.s_pe, ftr)
                last_pe = petok
                s.wait("vector", s.s_pe, petok)
                dst = xfm[:, half * 8:(half + 1) * 8, sub * P:(sub + 1) * P]
                src = tps.ap().rearrange("p (j q) -> p j q", q=P)[:, 0:8, :]
                dtok = s.op("vector", s.s_dve, lambda E, d=dst, a=src: E.tensor_copy(d, a))
                s.slot[("tps", half)] = (s.s_dve, dtok)
        s.close()

    def stage_QKV(self, t):
        s = self
        s.open()
        halo = (t == -1)
        par = t % 2
        xfm = s.v1(s.o_xfm, KC * TILE).rearrange("p (kc q) -> p kc q", q=TILE)
        wv_sb = s.v1(s.o_wv, KC * 512)

        if t == 0:
            v_rdy = None
        else:
            v_rdy = s.load_unit(("Wv", 0), s.dsem("wv"), wv_sb, s.sWv.ap())

        if not halo:
            q_rdy = {}

            def issue_q(h):
                sl_ = h % 2
                wq_ = s.v1(s.o_wq + sl_ * D, D)
                dep_ = s.slot.get(("wq", sl_))
                q_rdy[h] = s.load_unit(("Wq", h), s.dsem("wq%d" % sl_), wq_,
                                       s.sWq.ap()[h * P:(h + 1) * P, :], dep=dep_,
                                       queue="sync" if h % 2 == 0 else "scalar")
            issue_q(0)
            issue_q(1)
            for h in range(NH):
                sl_ = h % 2
                wq = s.v1(s.o_wq + sl_ * D, D)
                rdy = q_rdy[h]
                qps = s.pA if h % 2 == 0 else s.pB
                depq = s.slot.get(("qps", h % 2))
                if depq:
                    s.wait("tensor", *depq)
                s.wait("tensor", rdy[0], rdy[1])
                lhs = [wq[:, kc * P:(kc + 1) * P] for kc in range(KC)]
                rhs = [xfm[:, kc, :] for kc in range(KC)]
                def fmm(E, lhs=lhs, rhs=rhs, qps=qps):
                    for kc in range(KC):
                        mm = E.matmul(qps[:, :], lhs[kc], rhs[kc],
                                      start=(kc == 0), stop=(kc == KC - 1))
                    return mm
                petok = s.op("tensor", s.s_pe, fmm)
                s.slot[("wq", sl_)] = (s.s_pe, petok)
                if h + 2 < NH:
                    issue_q(h + 2)
                s.wait("scalar", s.s_pe, petok)
                qdst = s.v1(s.o_q + h * TILE, TILE)
                atok = s.op("scalar", s.s_act, lambda E, d=qdst, a=qps: E.copy(d, a[:, :]))
                s.slot[("qps", h % 2)] = (s.s_act, atok)

        k_rdy = {}

        def issue_k(g):
            sl2 = g % 2
            wk_ = s.v1(s.o_wk + sl2 * D, D)
            dep_ = s.slot.get(("wk", sl2))
            k_rdy[g] = s.load_unit(("Wk", g), s.dsem("wk%d" % sl2), wk_,
                                   s.sWk.ap()[g * P:(g + 1) * P, :], dep=dep_)
        issue_k(0)
        issue_k(1)
        for g in range(NKV):
            sl_ = g % 2
            wk = s.v1(s.o_wk + sl_ * D, D)
            rdy = k_rdy[g]
            kps = s.sc0[:, (g % 2) * 512:(g % 2) * 512 + 512]
            depk = s.slot.get(("kps", g % 2))
            if depk:
                s.wait("tensor", *depk)
            s.wait("tensor", rdy[0], rdy[1])
            lhs = [wk[:, kc * P:(kc + 1) * P] for kc in range(KC)]
            rhs = [xfm[:, kc, :] for kc in range(KC)]
            def fmm(E, lhs=lhs, rhs=rhs, kps=kps):
                for kc in range(KC):
                    mm = E.matmul(kps, lhs[kc], rhs[kc], start=(kc == 0), stop=(kc == KC - 1))
                return mm
            petok = s.op("tensor", s.s_pe, fmm)
            s.slot[("wk", sl_)] = (s.s_pe, petok)
            if g + 2 < NKV:
                issue_k(g + 2)
            s.wait("scalar", s.s_pe, petok)
            kdst = s.k_roll[:, (par * NKV + g) * 512:(par * NKV + g) * 512 + 512]
            atok = s.op("scalar", s.s_act, lambda E, d=kdst, a=kps: E.copy(d, a))
            s.slot[("kps", g % 2)] = (s.s_act, atok)

        if v_rdy:
            s.wait("tensor", v_rdy[0], v_rdy[1])
        for sub in range(NSUB):
            vps = s.sc1[:, (sub % 2) * 512:(sub % 2) * 512 + 512]
            depv = s.slot.get(("vps", sub % 2))
            if depv:
                s.wait("tensor", *depv)
            lhs = [xfm[:, kc, sub * P:(sub + 1) * P] for kc in range(KC)]
            rhs = [wv_sb[:, kc * 512:(kc + 1) * 512] for kc in range(KC)]
            def fmm(E, lhs=lhs, rhs=rhs, vps=vps):
                for kc in range(KC):
                    mm = E.matmul(vps, lhs[kc], rhs[kc], start=(kc == 0), stop=(kc == KC - 1))
                return mm
            petok = s.op("tensor", s.s_pe, fmm)
            s.wait("scalar", s.s_pe, petok)
            vdst = s.v_roll[:, (par * NSUB + sub) * 512:(par * NSUB + sub) * 512 + 512]
            atok = s.op("scalar", s.s_act, lambda E, d=vdst, a=vps: E.copy(d, a))
            s.slot[("vps", sub % 2)] = (s.s_act, atok)
        s.close()

    def stage_A(self, t):
        s = self
        s.open()
        par = t % 2
        ppar = (t - 1) % 2
        mtok = s.dma("sync", s.dsem("mask"), s.v1(s.o_mask, 5 * MW), s.mask.ap())
        s.wait("vector", s.dsem("mask"), mtok)

        def pass1(c, g):
            gc = t * 4 + c
            mslot = gc if gc < 4 else 4
            mask = s.v1(s.o_mask + mslot * MW, MW)
            cgp = (c * NKV + g) % 3
            stt_toks = []
            for hl in range(4):
                h = 4 * g + hl
                scps = s.sc0 if h % 2 == 0 else s.sc1
                dep = s.slot.get(("scps", h % 2))
                if dep:
                    s.wait("tensor", *dep)
                q_l = s.v1(s.o_q + h * TILE + c * P, P)
                w1 = 512 - c * P
                kp = s.k_roll[:, (ppar * NKV + g) * 512 + c * P:(ppar * NKV + g) * 512 + 512]
                kc2 = s.k_roll[:, (par * NKV + g) * 512:(par * NKV + g) * 512 + c * P + P]
                def fsc(E, q_l=q_l, kp=kp, kc2=kc2, scps=scps, c=c, w1=w1):
                    mm = E.matmul(scps[:, 0:w1], q_l, kp, start=True, stop=True)
                    if c == 0:
                        mm = E.matmul(scps[:, 512:640], q_l, kc2, start=True, stop=True)
                    else:
                        mm = E.matmul(scps[:, w1:512], q_l, kc2[:, 0:c * P], start=True, stop=True)
                        mm = E.matmul(scps[:, 512:640], q_l, kc2[:, c * P:c * P + P], start=True, stop=True)
                    return mm
                petok = s.op("tensor", s.s_pe, fsc)
                peb = s.v1(s.o_pe_ + (h % 2) * MW, MW)
                depe = s.slot.get(("peb", h % 2))
                if depe:
                    s.wait("scalar", *depe)
                s.wait("scalar", s.s_pe, petok)
                atok = s.op("scalar", s.s_act,
                            lambda E, d=peb, a=scps: E.activation(d, a[:, 0:MW], AF.Exp,
                                                                  scale=f32(1.0 / 128.0)))
                s.slot[("scps", h % 2)] = (s.s_act, atok)
                pm = s.v1(s.o_pm + cgp * 4 * MW + hl * MW, MW)
                depm = s.slot.get(("pm", cgp, hl))
                if depm:
                    s.wait("vector", *depm)
                s.wait("vector", s.s_act, atok)
                den = s.den4[:, cgp * 4 + hl:cgp * 4 + hl + 1]
                dtok = s.op("vector", s.s_dve,
                            lambda E, d=pm, a=peb, m=mask, acc=den:
                            E.scalar_tensor_tensor(d, a, 1.0, m, AluOpType.mult,
                                                   AluOpType.mult, accum_out=acc))
                s.slot[("peb", h % 2)] = (s.s_dve, dtok)
                stt_toks.append(dtok)
            return dict(c=c, g=g, gc=gc, cgp=cgp, stt=stt_toks)

        def pass2(ctx):
            c, g, gc, cgp = ctx["c"], ctx["g"], ctx["gc"], ctx["cgp"]
            cg2 = (c * NKV + g) % 2
            pT = s.v1(s.o_pT, 2560)
            deppT = s.slot.get(("pT",))
            s.wait("vector", s.s_dve, ctx["stt"][-1])
            rec = s.rec4[:, cgp * 4:cgp * 4 + 4]
            den = s.den4[:, cgp * 4:cgp * 4 + 4]
            rtok = s.op("vector", s.s_dve, lambda E, r=rec, d=den: E.reciprocal(r, d))
            ev_toks = []
            for hl in range(4):
                pm = s.v1(s.o_pm + cgp * 4 * MW + hl * MW, MW)
                s.wait("vector", s.s_dve, rtok)
                rsc = s.rec4[:, cgp * 4 + hl:cgp * 4 + hl + 1]
                ntok = s.op("vector", s.s_dve,
                            lambda E, d=pm, a=pm, r=rsc:
                            E.tensor_scalar(d, a, r, None, AluOpType.mult))
                tps = s.tA if hl % 2 == 0 else s.tB
                dept = s.slot.get(("tpsA", hl % 2))
                if dept:
                    s.wait("tensor", *dept)
                s.wait("tensor", s.s_dve, ntok)
                srcs = [pm[:, j * P:(j + 1) * P] for j in range(5)]
                dsts = [tps[:, j * P:(j + 1) * P] for j in range(5)]
                def ftr(E, srcs=srcs, dsts=dsts):
                    for d_, a_ in zip(dsts, srcs):
                        tr = E.transpose(d_, a_, s.ident_sb[:, :])
                    return tr
                petok = s.op("tensor", s.s_pe, ftr)
                s.slot[("pm", cgp, hl)] = (s.s_pe, petok)
                s.wait("scalar", s.s_pe, petok)
                if deppT:
                    s.wait("scalar", *deppT)
                dst = pT.rearrange("p (j q) -> p j q", q=512)[:, 0:5, hl * P:(hl + 1) * P]
                src = tps.ap().rearrange("p (j q) -> p j q", q=P)[:, 0:5, :]
                etok = s.op("scalar", s.s_act, lambda E, d=dst, a=src: E.copy(d, a))
                s.slot[("tpsA", hl % 2)] = (s.s_act, etok)
                ev_toks.append(etok)
            aps = s.pA if cg2 == 0 else s.pB
            depa = s.slot.get(("aps", cg2))
            if depa:
                s.wait("tensor", *depa)
            s.wait("tensor", s.s_act, ev_toks[-1])
            vls = []
            for j in range(5):
                kcg = gc - 4 + j
                pj = (kcg // 4) % 2
                sj = kcg % 4
                o = (pj * NSUB + sj) * 512 + g * P
                vls.append(s.v_roll[:, o:o + P])
            pTs = [pT[:, j * 512:(j + 1) * 512] for j in range(5)]
            def fpv(E, vls=vls, pTs=pTs, aps=aps):
                for j in range(5):
                    mm = E.matmul(aps[:, :], vls[j], pTs[j], start=(j == 0), stop=(j == 4))
                return mm
            pvtok = s.op("tensor", s.s_pe, fpv)
            s.slot[("pT",)] = (s.s_pe, pvtok)
            s.wait("vector", s.s_pe, pvtok)
            dst = s.AR2.ap().rearrange("p (h q) -> p h q", q=TILE)[:, 4 * g:4 * g + 4, c * P:(c + 1) * P]
            src = aps.ap().rearrange("p (h q) -> p h q", q=P)[:, 0:4, :]
            dtok = s.op("vector", s.s_dve, lambda E, d=dst, a=src: E.tensor_copy(d, a))
            s.slot[("aps", cg2)] = (s.s_dve, dtok)

        pend = []
        for c in range(NSUB):
            for g in range(NKV):
                pend.append(pass1(c, g))
                if len(pend) > 2:
                    pass2(pend.pop(0))
        for ctx in pend:
            pass2(ctx)
        s.close()

    def emit_ln(self, src, dst):
        s = self
        s.wait("vector", s.s_dve, s.s_dve.n)
        atok = None
        for r in range(4):
            sv = src[:, r * 512:(r + 1) * 512]
            dv = s.stats[:, 6 * r:6 * r + 6]
            atok = s.op("vector", s.s_dve, lambda E, d=dv, a=sv: E.bn_stats(d, a))
        s.wait("vector", s.s_dve, atok)
        gtok = s.op("vector", s.s_dve, lambda E: E.bn_aggr(s.mv[:, :], s.stats[:, :]))
        s.wait("scalar", s.s_dve, gtok)
        ltok = s.op("scalar", s.s_act,
                    lambda E: E.activation(s.lnv[:, :], s.mv[:, 1:2], AF.Ln, bias=f32(1e-5)))
        s.wait("scalar", s.s_act, ltok)
        stok = s.op("scalar", s.s_act,
                    lambda E: E.activation(s.rstd[:, :], s.lnv[:, :], AF.Exp, scale=f32(-0.5)))
        s.wait("vector", s.s_act, stok)
        s.wait("vector", s.s_dve, gtok)
        tok = s.op("vector", s.s_dve,
                   lambda E, a=src, d=dst: E.tensor_scalar(d, a, s.mv[:, 0:1], s.rstd[:, 0:1],
                                                           AluOpType.subtract, AluOpType.mult))
        return tok

    def stage_O(self, t):
        s = self
        s.open()
        attn = s.AR2.ap().rearrange("p (h q) -> p h q", q=TILE)
        x1fm = s.x1_fm.ap().rearrange("p (kc q) -> p kc q", q=TILE)
        xtm = s.x_tm_bf.ap().rearrange("p (u c) -> p u c", c=D)
        r1b = s.v1(s.o_r1b, NSUB * D)
        r1bv = r1b.rearrange("p (u c) -> p u c", c=D)
        wo_rdy = {}

        def issue_wo(oc):
            ring_ = oc % 2
            wo_ = s.v1(s.o_wo + ring_ * D, D)
            dep_ = s.slot.get(("wo", ring_))
            wo_rdy[oc] = s.load_unit(("Wo", oc), s.dsem("wo%d" % ring_), wo_,
                                     s.sWo.ap()[oc * P:(oc + 1) * P, :], dep=dep_,
                                     queue="sync" if ring_ == 0 else "scalar")
        issue_wo(0)
        issue_wo(1)
        for oc in range(KC):
            ring = oc % 2
            wo = s.v1(s.o_wo + ring * D, D)
            rdy = wo_rdy[oc]
            s.wait("tensor", rdy[0], rdy[1])
            dps = s.pA if oc % 2 == 0 else s.pB
            depo = s.slot.get(("dpsO", oc % 2))
            if depo:
                s.wait("tensor", *depo)
            lhs = [wo[:, ic * P:(ic + 1) * P] for ic in range(KC)]
            rhs = [attn[:, ic, :] for ic in range(KC)]
            def fmm(E, lhs=lhs, rhs=rhs, dps=dps):
                for ic in range(KC):
                    mm = E.matmul(dps[:, :], lhs[ic], rhs[ic],
                                  start=(ic == 0), stop=(ic == KC - 1))
                return mm
            petok = s.op("tensor", s.s_pe, fmm)
            s.slot[("wo", ring)] = (s.s_pe, petok)
            if oc + 2 < KC:
                issue_wo(oc + 2)
            wob = s.v1(s.o_silu + (oc % 2) * 512, 512)
            depf = s.slot.get(("wob", oc % 2))
            if depf:
                s.wait("scalar", *depf)
            s.wait("scalar", s.s_pe, petok)
            etok = s.op("scalar", s.s_act, lambda E, d=wob, a=dps: E.copy(d, a[:, :]))
            s.slot[("dpsO", oc % 2)] = (s.s_act, etok)
            tps = s.tA if oc % 2 == 0 else s.tB
            for k_ in (("tpsO", oc % 2), ("tps", oc % 2)):
                d_ = s.slot.get(k_)
                if d_:
                    s.wait("tensor", *d_)
            s.wait("tensor", s.s_act, etok)
            dsts = [tps[:, j * P:(j + 1) * P] for j in range(4)]
            srcs = [wob[:, j * P:(j + 1) * P] for j in range(4)]
            def ftr(E, srcs=srcs, dsts=dsts):
                for d_, a_ in zip(dsts, srcs):
                    tr = E.transpose(d_, a_, s.ident_sb[:, :])
                return tr
            ttok = s.op("tensor", s.s_pe, ftr)
            s.slot[("wob", oc % 2)] = (s.s_pe, ttok)
            s.wait("vector", s.s_pe, ttok)
            dst = r1bv[:, 0:4, oc * P:(oc + 1) * P]
            src = tps.ap().rearrange("p (u q) -> p u q", q=P)[:, 0:4, :]
            xb = xtm[:, 0:4, oc * P:(oc + 1) * P]
            rtok = s.op("vector", s.s_dve,
                        lambda E, d=dst, a=src, b=xb: E.tensor_tensor(d, a, b, AluOpType.add))
            s.slot[("tpsO", oc % 2)] = (s.s_dve, rtok)
        for sub in range(NSUB):
            x1b = s.x1_bf[:, sub * D:(sub + 1) * D]
            aptok = self.emit_ln(r1b[:, sub * D:(sub + 1) * D], x1b)
            for half in range(2):
                tps = s.tA if half == 0 else s.tB
                for k_ in (("tps", half), ("tpsO", half), ("wob", half)):
                    d_ = s.slot.get(k_)
                    if d_:
                        s.wait("tensor", *d_)
                s.wait("tensor", s.s_dve, aptok)
                srcs = [x1b[:, (half * 8 + j) * P:(half * 8 + j + 1) * P] for j in range(8)]
                dsts = [tps[:, j * P:(j + 1) * P] for j in range(8)]
                def ftr(E, srcs=srcs, dsts=dsts):
                    for d_, a_ in zip(dsts, srcs):
                        tr = E.transpose(d_, a_, s.ident_sb[:, :])
                    return tr
                petok = s.op("tensor", s.s_pe, ftr)
                s.wait("vector", s.s_pe, petok)
                dst = x1fm[:, half * 8:(half + 1) * 8, sub * P:(sub + 1) * P]
                src = tps.ap().rearrange("p (j q) -> p j q", q=P)[:, 0:8, :]
                dtok = s.op("vector", s.s_dve, lambda E, d=dst, a=src: E.tensor_copy(d, a))
                s.slot[("tps", half)] = (s.s_dve, dtok)
        s.close()

    def stage_F(self, t, half):
        s = self
        s.open()
        subs = (2 * half, 2 * half + 1)
        x1fm = s.x1_fm.ap().rearrange("p (kc q) -> p kc q", q=TILE)
        hspf = s.v1(s.o_hspf, NIC * 256).rearrange("p (ic q) -> p ic q", q=256)
        def gps_of(sub_l, sl):
            if sub_l == 1:
                return s.sc0[:, 0:512]
            return s.pA[:, :] if sl % 2 == 0 else s.sc1[:, 0:512]

        def ups_of(sub_l, sl):
            if sub_l == 1:
                return s.sc0[:, 512:1024]
            return s.pB[:, :] if sl % 2 == 0 else s.sc1[:, 512:1024]

        sp_pending = []
        sp_state = {"cur": None, "tok": None}

        def emit_hsp_transposes(grp, sub_l, sptok):
            s.wait("tensor", s.s_dve, sptok)
            hspt = s.v1(s.o_hspt + (grp % 2) * D, D)
            for tph in range(2):
                tps = s.tA if tph == 0 else s.tB
                deptp = s.slot.get(("tps", tph))
                if deptp:
                    s.wait("tensor", *deptp)
                srcs = [hspt[:, (tph * 8 + j) * P:(tph * 8 + j + 1) * P] for j in range(8)]
                dsts = [tps[:, j * P:(j + 1) * P] for j in range(8)]
                def ftr(E, srcs=srcs, dsts=dsts):
                    for d_, a_ in zip(dsts, srcs):
                        tr = E.transpose(d_, a_, s.ident_sb[:, :])
                    return tr
                petok = s.op("tensor", s.s_pe, ftr)
                s.wait("scalar", s.s_pe, petok)
                dst = hspf[:, grp * 16 + tph * 8:grp * 16 + (tph + 1) * 8, sub_l * P:(sub_l + 1) * P]
                src = tps.ap().rearrange("p (j q) -> p j q", q=P)[:, 0:8, :]
                etok = s.op("scalar", s.s_act, lambda E, d=dst, a=src: E.copy(d, a))
                s.slot[("tps", tph)] = (s.s_act, etok)
            s.slot[("hspt", grp % 2)] = (s.s_pe, s.s_pe.n)

        def emit_sparsify(k):
            while k > 0:
                if sp_state["cur"] is None:
                    if not sp_pending:
                        return
                    tag, grp, sub_l, builders = sp_pending.pop(0)
                    hw = s.slot.get(("hbw", grp % 2, sub_l))
                    if hw:
                        s.wait("vector", *hw)
                    dep = s.slot.get(("hspt", grp % 2))
                    if dep:
                        s.wait("vector", *dep)
                    sp_state["cur"] = (grp, sub_l, builders, 0)
                grp, sub_l, builders, i = sp_state["cur"]
                if sp_state["tok"]:
                    s.wait("vector", s.s_dve, sp_state["tok"])
                sp_state["tok"] = s.op("vector", s.s_dve, builders[i])
                i += 1
                k -= 1
                if i == len(builders):
                    emit_hsp_transposes(grp, sub_l, sp_state["tok"])
                    sp_state["cur"] = None
                else:
                    sp_state["cur"] = (grp, sub_l, builders, i)

        gu_rdy = {}

        def issue_gu(ci):
            sl_, q4_ = divmod(ci, 4)
            ring_ = ci % 4
            out = {}
            for (nm, o_ring, sW, q_) in (("Wg", s.o_wg, s.sWg, "sync"),
                                         ("Wu", s.o_wu, s.sWu, "scalar")):
                dst_ = s.v1(o_ring + ring_ * D, D)
                dep_ = s.slot.get((nm + "ring", ring_))
                out[nm] = s.load_unit((nm, sl_, q4_), s.dsem(nm + "r%d" % ring_), dst_,
                                      sW.ap()[sl_ * P:(sl_ + 1) * P, q4_ * 4 * 512:(q4_ + 1) * 4 * 512],
                                      dep=dep_, queue=q_)
            gu_rdy[ci] = out
        for ci0 in range(3):
            issue_gu(ci0)
        for sl in range(NSL):
            gtok = utok = None
            for q4 in range(4):
                ci = sl * 4 + q4
                ring = ci % 4
                if ci + 3 < NSL * 4:
                    issue_gu(ci + 3)
                rds = gu_rdy[ci]
                s.wait("tensor", rds["Wg"][0], rds["Wg"][1])
                s.wait("tensor", rds["Wu"][0], rds["Wu"][1])
                wg_c = s.v1(s.o_wg + ring * D, D)
                wu_c = s.v1(s.o_wu + ring * D, D)
                tok = None
                for sub_l in (1, 0):
                    sub = subs[sub_l]
                    if q4 == 0:
                        keys = ([("gps0", sl % 2), ("ups0", sl % 2)] if sub_l == 0
                                else [("gps1",), ("ups1",)])
                        for k_ in keys:
                            d_ = s.slot.get(k_)
                            if d_:
                                s.wait("tensor", *d_)
                    lhs = [x1fm[:, q4 * 4 + j, sub * P:(sub + 1) * P] for j in range(4)]
                    g_ps, u_ps = gps_of(sub_l, sl), ups_of(sub_l, sl)
                    def fmm(E, lhs=lhs, wg_c=wg_c, wu_c=wu_c, g_ps=g_ps, u_ps=u_ps, q4=q4):
                        for j in range(4):
                            st_f = (q4 == 0 and j == 0)
                            sp_f = (q4 == 3 and j == 3)
                            mm = E.matmul(g_ps, lhs[j], wg_c[:, j * 512:(j + 1) * 512],
                                          start=st_f, stop=sp_f)
                            mm2 = E.matmul(u_ps, lhs[j], wu_c[:, j * 512:(j + 1) * 512],
                                           start=st_f, stop=sp_f)
                        return mm2
                    tok = s.op("tensor", s.s_pe, fmm)
                    if q4 == 3:
                        if sub_l == 0:
                            gtok = tok
                        else:
                            utok = tok
                s.slot[("Wgring", ring)] = (s.s_pe, tok)
                s.slot[("Wuring", ring)] = (s.s_pe, tok)
            mm_done = {subs[0]: gtok, subs[1]: utok}
            sl_l = sl % 4
            grp = sl // 4
            for sub_l in (1, 0):
                sub = subs[sub_l]
                silu = s.v1(s.o_silu + sub_l * 512, 512)
                dw = s.slot.get(("silu_w", sub_l))
                if dw:
                    s.wait("scalar", *dw)
                s.wait("scalar", s.s_pe, mm_done[sub])
                g_ps, u_ps = gps_of(sub_l, sl), ups_of(sub_l, sl)
                stok = s.op("scalar", s.s_act, lambda E, d=silu, a=g_ps: E.activation(d, a, AF.Silu))
                s.slot[("gps0", sl % 2) if sub_l == 0 else ("gps1",)] = (s.s_act, stok)
                hb = s.v1(s.o_hbuf + (grp % 2) * 2 * D + sub_l * D + sl_l * 512, 512)
                s.wait("vector", s.s_act, stok)
                s.wait("vector", s.s_pe, mm_done[sub])
                htok = s.op("vector", s.s_dve,
                            lambda E, d=hb, a=u_ps, b=silu: E.tensor_tensor(d, a, b, AluOpType.mult))
                s.slot[("ups0", sl % 2) if sub_l == 0 else ("ups1",)] = (s.s_dve, htok)
                s.slot[("hbw", grp % 2, sub_l)] = (s.s_dve, htok)
                s.slot[("silu_w", sub_l)] = (s.s_dve, htok)
            if sl_l == 3:
                for sub_l in (1, 0):
                    sub = subs[sub_l]
                    hb = s.v1(s.o_hbuf + (grp % 2) * 2 * D + sub_l * D, D)
                    habs = s.v1(s.o_habs, D)
                    hspt = s.v1(s.o_hspt + (grp % 2) * D, D)
                    thr = s.v1(s.o_thr, 512)
                    t1_ = s.v1(s.o_thr + 512, 512)
                    t2_ = s.v1(s.o_thr + 1024, 512)
                    t3_ = s.v1(s.o_thr + 1536, 512)
                    h0, h1, h2, h3 = (habs[:, k:D:4] for k in range(4))
                    builders = [
                        lambda E, habs=habs, hb=hb: E.tensor_tensor(habs, hb, hb, AluOpType.mult),
                        lambda E, t1_=t1_, h0=h0, h1=h1: E.tensor_tensor(t1_, h0, h1, AluOpType.max),
                        lambda E, t2_=t2_, h0=h0, h1=h1: E.tensor_tensor(t2_, h0, h1, AluOpType.min),
                        lambda E, t3_=t3_, h2=h2, h3=h3: E.tensor_tensor(t3_, h2, h3, AluOpType.max),
                        lambda E, thr=thr, h2=h2, h3=h3: E.tensor_tensor(thr, h2, h3, AluOpType.min),
                        lambda E, t1_=t1_, t3_=t3_: E.tensor_tensor(t1_, t1_, t3_, AluOpType.min),
                        lambda E, t2_=t2_, thr=thr: E.tensor_tensor(t2_, t2_, thr, AluOpType.max),
                        lambda E, thr=thr, t1_=t1_, t2_=t2_: E.tensor_tensor(thr, t1_, t2_, AluOpType.max),
                    ]
                    for k in range(4):
                        tt = t1_ if k % 2 == 0 else t2_
                        hak = habs[:, k:D:4]
                        hbk = hb[:, k:D:4]
                        hok = hspt[:, k:D:4]
                        builders.append(lambda E, tt=tt, hak=hak, thr=thr:
                                        E.tensor_tensor(tt, hak, thr, AluOpType.is_ge))
                        builders.append(lambda E, hok=hok, tt=tt, hbk=hbk:
                                        E.tensor_tensor(hok, tt, hbk, AluOpType.mult))
                    sp_pending.append(("start", grp, sub_l, builders))
            emit_sparsify(9)

        while sp_pending or sp_state["cur"] is not None:
            emit_sparsify(8)

        for k_ in (("tps", 0), ("tps", 1), ("gps0", 0), ("gps0", 1), ("ups0", 0),
                   ("ups0", 1), ("gps1",), ("ups1",)):
            d_ = s.slot.get(k_)
            if d_:
                s.wait("tensor", *d_)
        r2 = s.AR1[:, s.o_r2:s.o_r2 + 2 * D]
        wd_rdy = {}

        def issue_wd(idx):
            oc_, hu_ = divmod(idx, 2)
            ring_ = idx % 2
            dep_ = s.slot.get(("wdring", ring_))
            sem_a = s.dsem("wd%da" % ring_)
            sem_b = s.dsem("wd%db" % ring_)
            base = ring_ * 4096
            scr = s.sWd.ap()[oc_ * P:(oc_ + 1) * P, hu_ * 4096:(hu_ + 1) * 4096]
            ta = s.load_unit(("Wd", oc_, hu_, 0), sem_a, s.AR2[:, base:base + 2048],
                             scr[:, 0:2048], dep=dep_, queue="sync")
            tb = s.load_unit(("Wd", oc_, hu_, 1), sem_b, s.AR2[:, base + 2048:base + 4096],
                             scr[:, 2048:4096], dep=dep_, queue="scalar")
            wd_rdy[idx] = (ta, tb)
        issue_wd(0)
        for oc in range(KC):
            dps = s.pA if oc % 2 == 0 else s.pB
            dtok = None
            for hu in range(2):
                idx = oc * 2 + hu
                ring = idx % 2
                wd = s.AR2[:, ring * 4096:(ring + 1) * 4096]
                if idx + 1 < 2 * KC:
                    issue_wd(idx + 1)
                ta, tb = wd_rdy[idx]
                s.wait("tensor", ta[0], ta[1])
                s.wait("tensor", tb[0], tb[1])
                if hu == 0:
                    depo = s.slot.get(("dps", oc % 2))
                    if depo:
                        s.wait("tensor", *depo)
                lhs = [wd[:, k * P:(k + 1) * P] for k in range(32)]
                rhs = [hspf[:, hu * 32 + k, :] for k in range(32)]
                def fdn(E, lhs=lhs, rhs=rhs, dps=dps, hu=hu):
                    for k in range(32):
                        mm = E.matmul(dps[:, 0:256], lhs[k], rhs[k],
                                      start=(hu == 0 and k == 0), stop=(hu == 1 and k == 31))
                    return mm
                dtok = s.op("tensor", s.s_pe, fdn)
                s.slot[("wdring", ring)] = (s.s_pe, dtok)
            dfm = s.v1(s.o_dfm + (oc % 2) * 256, 256)
            s.wait("scalar", s.s_pe, dtok)
            depf = s.slot.get(("dfm", oc % 2))
            if depf:
                s.wait("scalar", *depf)
            etok = s.op("scalar", s.s_act, lambda E, d=dfm, a=dps: E.copy(d, a[:, 0:256]))
            s.slot[("dps", oc % 2)] = (s.s_act, etok)
            tps = s.tA if oc % 2 == 0 else s.tB
            for k_ in (("tpsF3", oc % 2), ("tps", oc % 2)):
                d_ = s.slot.get(k_)
                if d_:
                    s.wait("tensor", *d_)
            s.wait("tensor", s.s_act, etok)
            dsts = [tps[:, j * P:(j + 1) * P] for j in range(2)]
            srcs = [dfm[:, j * P:(j + 1) * P] for j in range(2)]
            def ftr(E, srcs=srcs, dsts=dsts):
                for d_, a_ in zip(dsts, srcs):
                    tr = E.transpose(d_, a_, s.ident_sb[:, :])
                return tr
            petok = s.op("tensor", s.s_pe, ftr)
            s.slot[("dfm", oc % 2)] = (s.s_pe, petok)
            s.wait("vector", s.s_pe, petok)
            dst = r2.rearrange("p (u c) -> p u c", c=D)[:, 0:2, oc * P:(oc + 1) * P]
            src = tps.ap().rearrange("p (u q) -> p u q", q=P)[:, 0:2, :]
            xb = s.x1_bf.ap().rearrange("p (u c) -> p u c", c=D)[:, 2 * half:2 * half + 2, oc * P:(oc + 1) * P]
            rtok = s.op("vector", s.s_dve,
                        lambda E, d=dst, a=src, b=xb: E.tensor_tensor(d, a, b, AluOpType.add))
            s.slot[("tpsF3", oc % 2)] = (s.s_dve, rtok)
        for sub_l, sub in enumerate(subs):
            osb = s.AR3[:, sub_l * D:(sub_l + 1) * D]
            depo = s.slot.get(("osb_v", sub_l))
            if depo:
                s.wait("vector", *depo)
            aptok = self.emit_ln(r2[:, sub_l * D:(sub_l + 1) * D], osb)
            s.wait("sync", s.s_dve, aptok)
            osem = s.dsem("out%d" % sub_l)
            otok = s.dma("sync", osem, s.out.ap()[t * TILE + sub * P:t * TILE + (sub + 1) * P, :],
                         osb)
            s.slot[("osb", sub_l)] = (osem, otok)
            s.slot[("osb_v", sub_l)] = (osem, otok)
        s.close()

    def build(self):
        s = self
        s.open()
        s.dma("sync", s.s_cst, s.ident_sb[:, :], s.ident.ap())
        for e in ("tensor",):
            s.wait(e, s.s_cst, 16)
        s.close()
        s.stage_precast()
        s.stage_X(-1)
        s.stage_QKV(-1)
        for t in range(s.n_tiles):
            s.stage_X(t)
            s.stage_QKV(t)
            s.stage_A(t)
            if s.debug and t == 0:
                s.dbg("xfm", s.v1(s.o_xfm, KC * TILE), KC * TILE, BF16)
                s.dbg("q", s.v1(s.o_q, NH * 512), NH * 512, BF16)
                s.dbg("kroll", s.k_roll[:, :], 2 * NKV * 512, BF16)
                s.dbg("vroll", s.v_roll[:, :], 2 * NSUB * 512, BF16)
                s.dbg("attn", s.AR2[:, :], 8192, BF16)
            s.stage_O(t)
            if s.debug and t == 0:
                s.dbg("x1", s.x1_bf[:, :], NSUB * D, BF16)
                s.dbg("x1fm", s.x1_fm[:, :], KC * TILE, BF16)
            for half in range(2):
                s.stage_F(t, half)
            if s.debug and t == 0:
                s.dbg("hspf", s.v1(s.o_hspf, NIC * 256), NIC * 256, BF16)
        s.open()
        def fin(E):
            for nm, sm in s.dsems.items():
                if nm.endswith("_wb") or nm.startswith("out"):
                    E.wait_ge(sm.h, sm.n)
            E.wait_ge(s.s_dbg.h, s.s_dbg.n)
        s.stage.add("sync", fin)
        s.close()
        return s.nc


def build_masks(seq_start: bool) -> np.ndarray:
    m = np.zeros((5, P, MW), dtype=np.float32)
    r = np.arange(P)[:, None]
    kw = np.arange(MW)[None, :]
    band = (kw >= r + 1) & (kw <= r + WIN)
    for v in range(5):
        lo = (WIN - P * v) if (seq_start and v < 4) else 0
        m[v] = (band & (kw >= lo)).astype(np.float32)
    return np.ascontiguousarray(m.transpose(1, 0, 2).reshape(P, 5 * MW)).astype(BB)


def kernel(L0_attn_hidden_states, L0_attn_Wq, L0_attn_Wk, L0_attn_Wv, L0_attn_Wo,
           L0_ffn_w_gate, L0_ffn_w_up, L0_ffn_w_down):
    x = np.ascontiguousarray(np.asarray(L0_attn_hidden_states, dtype=np.float32))
    B, S, _ = x.shape
    n_cores = 8
    per = (B * S) // n_cores
    mk = MK(n_tiles=NT, debug=False)
    nc = mk.build()
    ident = np.eye(P, dtype=BB)
    w = dict(L0_attn_Wq=np.asarray(L0_attn_Wq, np.float32),
             L0_attn_Wk=np.asarray(L0_attn_Wk, np.float32),
             L0_attn_Wv=np.asarray(L0_attn_Wv, np.float32),
             L0_attn_Wo=np.asarray(L0_attn_Wo, np.float32),
             L0_ffn_w_gate=np.asarray(L0_ffn_w_gate, np.float32),
             L0_ffn_w_up=np.asarray(L0_ffn_w_up, np.float32),
             L0_ffn_w_down=np.asarray(L0_ffn_w_down, np.float32))
    xf = x.reshape(B * S, D)
    in_maps = []
    for c in range(n_cores):
        t0 = c * per
        seq_start = (t0 % S) == 0
        halo = np.zeros((HALO, D), np.float32) if seq_start else xf[t0 - HALO:t0]
        xs = np.ascontiguousarray(np.concatenate([halo, xf[t0:t0 + per]], axis=0))
        im = {"x": xs, "mask01": build_masks(seq_start), "ident": ident}
        im.update(w)
        in_maps.append(im)
    res = run_bass_kernel_spmd(nc, in_maps, core_ids=list(range(n_cores)))
    outs = [res.results[c]["out"] for c in range(n_cores)]
    return np.concatenate(outs, axis=0).reshape(B, S, D).astype(np.float32)


# version 07
# speedup vs baseline: 1.0163x; 1.0163x over previous
import numpy as np
import ml_dtypes
from contextlib import ExitStack

import concourse.bass as bass
import concourse.mybir as mybir
from concourse.alu_op_type import AluOpType
from concourse.bass_utils import run_bass_kernel_spmd

F32 = mybir.dt.float32
BF16 = mybir.dt.bfloat16
AF = mybir.ActivationFunctionType
BB = ml_dtypes.bfloat16

P = 128
D = 2048
KC = 16
NH = 16
NKV = 4
WIN = 512
ISZ = 8192
NIC = 64
TILE = 512
NSUB = 4
HALO = 512
NT = 8
NSL = 16
MW = 640

f32 = np.float32
ENGS = ("sync", "gpsimd", "tensor", "scalar", "vector")


class Sem:
    def __init__(self, nc, es, name, step=1):
        self.h = es.enter_context(nc.semaphore(name))
        self.n = 0
        self.step = step
        self.name = name

    def reserve(self):
        self.n += self.step
        return self.n


class Stage:
    def __init__(self):
        self.lists = {e: [] for e in ENGS}

    def add(self, eng, fn):
        self.lists[eng].append(fn)


class MK:
    def __init__(self, n_tiles=NT, debug=False):
        self.n_tiles = n_tiles
        self.debug = debug
        self.dbg_names = []
        nc = bass.Bass("TRN2", target_bir_lowering=False)
        self.nc = nc
        self.es = ExitStack()
        es = self.es
        ntok = HALO + n_tiles * TILE

        dram = nc.dram_tensor
        self.x = dram("x", [ntok, D], F32, kind="ExternalInput")
        self.Wq = dram("L0_attn_Wq", [D, D], F32, kind="ExternalInput")
        self.Wk = dram("L0_attn_Wk", [D, 512], F32, kind="ExternalInput")
        self.Wv = dram("L0_attn_Wv", [D, 512], F32, kind="ExternalInput")
        self.Wo = dram("L0_attn_Wo", [D, D], F32, kind="ExternalInput")
        self.Wg = dram("L0_ffn_w_gate", [D, ISZ], F32, kind="ExternalInput")
        self.Wu = dram("L0_ffn_w_up", [D, ISZ], F32, kind="ExternalInput")
        self.Wd = dram("L0_ffn_w_down", [ISZ, D], F32, kind="ExternalInput")
        self.mask = dram("mask01", [P, 5 * MW], BF16, kind="ExternalInput")
        self.ident = dram("ident", [P, P], BF16, kind="ExternalInput")
        self.out = dram("out", [n_tiles * TILE, D], F32, kind="ExternalOutput")
        self.sWq = dram("sWq", [D, D], BF16)
        self.sWk = dram("sWk", [512, D], BF16)
        self.sWv = dram("sWv", [P, KC * 512], BF16)
        self.sWo = dram("sWo", [D, D], BF16)
        self.sWg = dram("sWg", [NSL * P, KC * 512], BF16)
        self.sWu = dram("sWu", [NSL * P, KC * 512], BF16)
        self.sWd = dram("sWd", [KC * P, NIC * P], BF16)

        self.s_pe = Sem(nc, es, "s_pe")
        self.s_act = Sem(nc, es, "s_act")
        self.s_dve = Sem(nc, es, "s_dve")
        self.s_pool = Sem(nc, es, "s_pool")
        self.s_cst = Sem(nc, es, "s_cst", 16)
        self.s_dbg = Sem(nc, es, "s_dbg", 16)
        self.dsems = {}
        self.waited = {}
        self.slot_wb = {}
        self.fence_sems = set()

        sb = lambda name, cols, dt: es.enter_context(nc.sbuf_tensor(name, [P, cols], dt))
        self.AR1 = sb("AR1", 50688, BF16)
        self.AR2 = sb("AR2", 8192, BF16)
        self.AR3 = sb("AR3", 4096, F32)
        self.x1_bf = sb("x1_bf", NSUB * D, BF16)
        self.x1_fm = sb("x1_fm", KC * TILE, BF16)
        self.x_tm_bf = sb("x_tm_bf", NSUB * D, BF16)
        self.ident_sb = sb("ident_sb", P, BF16)
        self.k_roll = sb("k_roll", 2 * NKV * 512, BF16)
        self.v_roll = sb("v_roll", 2 * NSUB * 512, BF16)
        self.den4 = sb("den4", 16, F32)
        self.rec4 = sb("rec4", 16, F32)
        self.stats = sb("stats", 24, F32)
        self.mv = sb("mv", 2, F32)
        self.lnv = sb("lnv", 1, F32)
        self.rstd = sb("rstd", 1, F32)

        self.o_xbf = 0
        self.o_xfm = 4096
        self.o_wq = 12288
        self.o_wk = 16384
        self.o_wv = 20480
        self.o_q = 28672
        self.o_pe_ = 36864
        self.o_pm = 38144
        self.o_pT = 45824
        self.o_wo = 0
        self.o_wg = 0
        self.o_wu = 8192
        self.o_hbuf = 16384
        self.o_habs = 30208
        self.o_thr = 32256
        self.o_hspt = 24576
        self.o_silu = 28672
        self.o_dfm = 29696
        self.o_r2 = 30208
        self.o_hspf = 34304
        self.o_mask = 0
        self.o_r1b = 34304

        ps = lambda name, cols, dt: es.enter_context(nc.psum_tensor(name, [P, cols], dt))
        self.sc0 = ps("sc0", 1024, F32)
        self.sc1 = ps("sc1", 1024, F32)
        self.tA = ps("tA", 1024, BF16)
        self.tB = ps("tB", 1024, BF16)
        self.pA = ps("pA", 512, F32)
        self.pB = ps("pB", 512, F32)

        self.wb_tok = {}
        self.slot = {}
        self.stage = None

    def v1(self, off, cols):
        return self.AR1[:, off:off + cols]

    def dsem(self, name):
        if name not in self.dsems:
            self.dsems[name] = Sem(self.nc, self.es, "sd_" + name, 16)
        return self.dsems[name]

    def wait(self, eng, sem, tok):
        if not tok:
            return
        key = (eng, sem.name)
        if self.waited.get(key, 0) >= tok:
            return
        self.waited[key] = tok
        self.stage.add(eng, lambda E, s=sem.h, t=tok: E.wait_ge(s, t))

    def op(self, eng, sem, builder):
        if sem is None:
            self.stage.add(eng, lambda E, b=builder: b(E))
            return None
        tok = sem.reserve()
        self.stage.add(eng, lambda E, b=builder, h=sem.h: b(E).then_inc(h, 1))
        return tok

    def dma(self, eng, sem, dst, src):
        tok = sem.reserve()
        self.stage.add(eng, lambda E, d=dst, a=src, h=sem.h: E.dma_start(d, a).then_inc(h, 16))
        return tok

    def open(self):
        self.stage = Stage()

    def close(self):
        st = self.stage
        for sem in sorted(self.fence_sems, key=lambda x: x.name):
            st.add("sync", lambda E, h=sem.h, n=sem.n: E.wait_ge(h, n))
            self.waited[("sync", sem.name)] = sem.n
        with self.nc.Block() as block:
            for e in ENGS:
                fns = st.lists[e]
                if fns:
                    def runner(E, fns=fns):
                        for f in fns:
                            f(E)
                    getattr(block, e)(runner)
        for sem in (self.s_pe, self.s_act, self.s_dve, self.s_pool):
            for e in ENGS:
                key = (e, sem.name)
                self.waited[key] = max(self.waited.get(key, 0), sem.n)
        names = {k[1] for k in self.waited}
        for nm in names:
            m = max(self.waited.get((e, nm), 0) for e in ENGS)
            for e in ENGS:
                self.waited[(e, nm)] = m
        self.fence_sems = set()
        self.stage = None

    def load_unit(self, key, slot_sem, dst, scratch, dep=None, queue="sync"):
        s = self
        eng = "sync" if queue == "sync" else "scalar"
        if dep:
            s.wait(eng, dep[0], dep[1])
        tok = s.dma(eng, slot_sem, dst, scratch)
        return (slot_sem, tok)

    def stage_precast(self):
        s = self
        s.open()
        chunks = []
        wqr = s.Wq.ap().rearrange("(kc p) (h m) -> p kc h m", p=P, m=P)
        for h in range(NH):
            chunks.append((s.sWq.ap()[h * P:(h + 1) * P, :], wqr[:, :, h, :]))
        wkr = s.Wk.ap().rearrange("(kc p) (g m) -> p kc g m", p=P, m=P)
        for g in range(NKV):
            chunks.append((s.sWk.ap()[g * P:(g + 1) * P, :], wkr[:, :, g, :]))
        wvr = s.Wv.ap().rearrange("(kc p) c -> p kc c", p=P)
        for q in range(4):
            chunks.append((s.sWv.ap()[:, q * 2048:(q + 1) * 2048],
                           wvr[:, q * 4:(q + 1) * 4, :]))
        wor = s.Wo.ap().rearrange("(ic p) (oc m) -> p ic oc m", p=P, m=P)
        for oc in range(KC):
            chunks.append((s.sWo.ap()[oc * P:(oc + 1) * P, :], wor[:, :, oc, :]))
        wgr = s.Wg.ap().rearrange("(kc p) i -> p kc i", p=P)
        wur = s.Wu.ap().rearrange("(kc p) i -> p kc i", p=P)
        for sl in range(NSL):
            for q in range(4):
                chunks.append((s.sWg.ap()[sl * P:(sl + 1) * P, q * 2048:(q + 1) * 2048],
                               wgr[:, q * 4:(q + 1) * 4, sl * 512:(sl + 1) * 512]))
        for sl in range(NSL):
            for q in range(4):
                chunks.append((s.sWu.ap()[sl * P:(sl + 1) * P, q * 2048:(q + 1) * 2048],
                               wur[:, q * 4:(q + 1) * 4, sl * 512:(sl + 1) * 512]))
        wdr = s.Wd.ap().rearrange("(ic p) (oc m) -> p ic oc m", p=P, m=P)
        for oc in range(KC):
            for q in range(4):
                chunks.append((s.sWd.ap()[oc * P:(oc + 1) * P, q * 2048:(q + 1) * 2048],
                               wdr[:, q * 16:(q + 1) * 16, oc, :]))
        ldsem = [s.dsem("pcl0"), s.dsem("pcl1")]
        wbsem = [s.dsem("pc0_wb"), s.dsem("pc1_wb")]
        cast_tok = [None, None]
        for i, (scr, gat) in enumerate(chunks):
            sl_ = i % 2
            stg = s.AR3[:, sl_ * 2048:(sl_ + 1) * 2048]
            bfs = s.v1(s.o_xbf + sl_ * D, D)
            if cast_tok[sl_]:
                s.wait("sync", s.s_dve, cast_tok[sl_])
            ltok = s.dma("sync", ldsem[sl_], stg, gat)
            s.wait("vector", ldsem[sl_], ltok)
            if i >= 2:
                s.wait("vector", wbsem[sl_], wbsem[sl_].n)
            ctok = s.op("vector", s.s_dve, lambda E, d=bfs, a=stg: E.tensor_copy(d, a))
            cast_tok[sl_] = ctok
            s.wait("scalar", s.s_dve, ctok)
            s.dma("scalar", wbsem[sl_], scr, bfs)
        s.fence_sems.add(wbsem[0])
        s.fence_sems.add(wbsem[1])
        s.close()

    def dbg(self, name, src_ap, cols, dt):
        if not self.debug:
            return
        t = self.nc.dram_tensor("dbg_" + name, [P, cols], dt, kind="ExternalOutput")
        self.dbg_names.append("dbg_" + name)
        self.open()
        self.dma("sync", self.s_dbg, t.ap(), src_ap)
        self.close()

    def stage_X(self, t):
        s = self
        s.open()
        row0 = 0 if t == -1 else HALO + t * TILE
        xfm = s.v1(s.o_xfm, KC * TILE).rearrange("p (kc q) -> p kc q", q=TILE)
        x_tok = {}

        def issue_x(sub):
            sl2 = sub % 2
            stg_ = s.AR3[:, sl2 * 2048:(sl2 + 1) * 2048]
            depc = s.slot.get(("xstg", sl2))
            if depc:
                s.wait("sync", *depc)
            dob = s.slot.get(("osb", sl2))
            if dob:
                s.wait("sync", *dob)
            x_tok[sub] = s.dma("sync", s.dsem("xb%d" % sl2), stg_,
                               s.x.ap()[row0 + sub * P:row0 + (sub + 1) * P, :])
        issue_x(0)
        issue_x(1)
        for sub in range(NSUB):
            sl_ = sub % 2
            xbf = s.x_tm_bf[:, sub * D:(sub + 1) * D]
            stg = s.AR3[:, sl_ * 2048:(sl_ + 1) * 2048]
            xsem = s.dsem("xb%d" % sl_)
            ltok = x_tok[sub]
            s.wait("vector", xsem, ltok)
            ctok = s.op("vector", s.s_dve, lambda E, d=xbf, a=stg: E.tensor_copy(d, a))
            s.slot[("xstg", sl_)] = (s.s_dve, ctok)
            if sub + 2 < NSUB:
                issue_x(sub + 2)
            s.wait("tensor", s.s_dve, ctok)
            last_pe = None
            for half in range(2):
                tps = s.tA if half == 0 else s.tB
                dep2 = s.slot.get(("tps", half))
                if dep2:
                    s.wait("tensor", *dep2)
                srcs = [xbf[:, (half * 8 + j) * P:(half * 8 + j + 1) * P] for j in range(8)]
                dsts = [tps[:, j * P:(j + 1) * P] for j in range(8)]
                def ftr(E, srcs=srcs, dsts=dsts):
                    for d_, a_ in zip(dsts, srcs):
                        tr = E.transpose(d_, a_, s.ident_sb[:, :])
                    return tr
                petok = s.op("tensor", s.s_pe, ftr)
                last_pe = petok
                s.wait("vector", s.s_pe, petok)
                dst = xfm[:, half * 8:(half + 1) * 8, sub * P:(sub + 1) * P]
                src = tps.ap().rearrange("p (j q) -> p j q", q=P)[:, 0:8, :]
                dtok = s.op("vector", s.s_dve, lambda E, d=dst, a=src: E.tensor_copy(d, a))
                s.slot[("tps", half)] = (s.s_dve, dtok)
        s.close()

    def stage_QKV(self, t):
        s = self
        s.open()
        halo = (t == -1)
        par = t % 2
        xfm = s.v1(s.o_xfm, KC * TILE).rearrange("p (kc q) -> p kc q", q=TILE)
        wv_sb = s.v1(s.o_wv, KC * 512)

        if t == 0:
            v_rdy = None
        else:
            v_rdy = s.load_unit(("Wv", 0), s.dsem("wv"), wv_sb, s.sWv.ap())

        if not halo:
            q_rdy = {}

            def issue_q(h):
                sl_ = h % 2
                wq_ = s.v1(s.o_wq + sl_ * D, D)
                dep_ = s.slot.get(("wq", sl_))
                q_rdy[h] = s.load_unit(("Wq", h), s.dsem("wq%d" % sl_), wq_,
                                       s.sWq.ap()[h * P:(h + 1) * P, :], dep=dep_,
                                       queue="sync" if h % 2 == 0 else "scalar")
            issue_q(0)
            issue_q(1)
            for h in range(NH):
                sl_ = h % 2
                wq = s.v1(s.o_wq + sl_ * D, D)
                rdy = q_rdy[h]
                qps = s.pA if h % 2 == 0 else s.pB
                depq = s.slot.get(("qps", h % 2))
                if depq:
                    s.wait("tensor", *depq)
                s.wait("tensor", rdy[0], rdy[1])
                lhs = [wq[:, kc * P:(kc + 1) * P] for kc in range(KC)]
                rhs = [xfm[:, kc, :] for kc in range(KC)]
                def fmm(E, lhs=lhs, rhs=rhs, qps=qps):
                    for kc in range(KC):
                        mm = E.matmul(qps[:, :], lhs[kc], rhs[kc],
                                      start=(kc == 0), stop=(kc == KC - 1))
                    return mm
                petok = s.op("tensor", s.s_pe, fmm)
                s.slot[("wq", sl_)] = (s.s_pe, petok)
                if h + 2 < NH:
                    issue_q(h + 2)
                s.wait("scalar", s.s_pe, petok)
                qdst = s.v1(s.o_q + h * TILE, TILE)
                atok = s.op("scalar", s.s_act, lambda E, d=qdst, a=qps: E.copy(d, a[:, :]))
                s.slot[("qps", h % 2)] = (s.s_act, atok)

        k_rdy = {}

        def issue_k(g):
            sl2 = g % 2
            wk_ = s.v1(s.o_wk + sl2 * D, D)
            dep_ = s.slot.get(("wk", sl2))
            k_rdy[g] = s.load_unit(("Wk", g), s.dsem("wk%d" % sl2), wk_,
                                   s.sWk.ap()[g * P:(g + 1) * P, :], dep=dep_)
        issue_k(0)
        issue_k(1)
        for g in range(NKV):
            sl_ = g % 2
            wk = s.v1(s.o_wk + sl_ * D, D)
            rdy = k_rdy[g]
            kps = s.sc0[:, (g % 2) * 512:(g % 2) * 512 + 512]
            depk = s.slot.get(("kps", g % 2))
            if depk:
                s.wait("tensor", *depk)
            s.wait("tensor", rdy[0], rdy[1])
            lhs = [wk[:, kc * P:(kc + 1) * P] for kc in range(KC)]
            rhs = [xfm[:, kc, :] for kc in range(KC)]
            def fmm(E, lhs=lhs, rhs=rhs, kps=kps):
                for kc in range(KC):
                    mm = E.matmul(kps, lhs[kc], rhs[kc], start=(kc == 0), stop=(kc == KC - 1))
                return mm
            petok = s.op("tensor", s.s_pe, fmm)
            s.slot[("wk", sl_)] = (s.s_pe, petok)
            if g + 2 < NKV:
                issue_k(g + 2)
            s.wait("scalar", s.s_pe, petok)
            kdst = s.k_roll[:, (par * NKV + g) * 512:(par * NKV + g) * 512 + 512]
            atok = s.op("scalar", s.s_act, lambda E, d=kdst, a=kps: E.copy(d, a))
            s.slot[("kps", g % 2)] = (s.s_act, atok)

        if v_rdy:
            s.wait("tensor", v_rdy[0], v_rdy[1])
        for sub in range(NSUB):
            vps = s.sc1[:, (sub % 2) * 512:(sub % 2) * 512 + 512]
            depv = s.slot.get(("vps", sub % 2))
            if depv:
                s.wait("tensor", *depv)
            lhs = [xfm[:, kc, sub * P:(sub + 1) * P] for kc in range(KC)]
            rhs = [wv_sb[:, kc * 512:(kc + 1) * 512] for kc in range(KC)]
            def fmm(E, lhs=lhs, rhs=rhs, vps=vps):
                for kc in range(KC):
                    mm = E.matmul(vps, lhs[kc], rhs[kc], start=(kc == 0), stop=(kc == KC - 1))
                return mm
            petok = s.op("tensor", s.s_pe, fmm)
            s.wait("scalar", s.s_pe, petok)
            vdst = s.v_roll[:, (par * NSUB + sub) * 512:(par * NSUB + sub) * 512 + 512]
            atok = s.op("scalar", s.s_act, lambda E, d=vdst, a=vps: E.copy(d, a))
            s.slot[("vps", sub % 2)] = (s.s_act, atok)
        s.close()

    def stage_A(self, t):
        s = self
        s.open()
        par = t % 2
        ppar = (t - 1) % 2
        mtok = s.dma("sync", s.dsem("mask"), s.v1(s.o_mask, 5 * MW), s.mask.ap())
        s.wait("vector", s.dsem("mask"), mtok)

        def pass1(c, g):
            gc = t * 4 + c
            mslot = gc if gc < 4 else 4
            mask = s.v1(s.o_mask + mslot * MW, MW)
            cgp = (c * NKV + g) % 3
            stt_toks = []
            for hl in range(4):
                h = 4 * g + hl
                scps = s.sc0 if h % 2 == 0 else s.sc1
                dep = s.slot.get(("scps", h % 2))
                if dep:
                    s.wait("tensor", *dep)
                q_l = s.v1(s.o_q + h * TILE + c * P, P)
                w1 = 512 - c * P
                kp = s.k_roll[:, (ppar * NKV + g) * 512 + c * P:(ppar * NKV + g) * 512 + 512]
                kc2 = s.k_roll[:, (par * NKV + g) * 512:(par * NKV + g) * 512 + c * P + P]
                def fsc(E, q_l=q_l, kp=kp, kc2=kc2, scps=scps, c=c, w1=w1):
                    mm = E.matmul(scps[:, 0:w1], q_l, kp, start=True, stop=True)
                    if c == 0:
                        mm = E.matmul(scps[:, 512:640], q_l, kc2, start=True, stop=True)
                    else:
                        mm = E.matmul(scps[:, w1:512], q_l, kc2[:, 0:c * P], start=True, stop=True)
                        mm = E.matmul(scps[:, 512:640], q_l, kc2[:, c * P:c * P + P], start=True, stop=True)
                    return mm
                petok = s.op("tensor", s.s_pe, fsc)
                peb = s.v1(s.o_pe_ + (h % 2) * MW, MW)
                depe = s.slot.get(("peb", h % 2))
                if depe:
                    s.wait("scalar", *depe)
                s.wait("scalar", s.s_pe, petok)
                atok = s.op("scalar", s.s_act,
                            lambda E, d=peb, a=scps: E.activation(d, a[:, 0:MW], AF.Exp,
                                                                  scale=f32(1.0 / 128.0)))
                s.slot[("scps", h % 2)] = (s.s_act, atok)
                pm = s.v1(s.o_pm + cgp * 4 * MW + hl * MW, MW)
                depm = s.slot.get(("pm", cgp, hl))
                if depm:
                    s.wait("vector", *depm)
                s.wait("vector", s.s_act, atok)
                den = s.den4[:, cgp * 4 + hl:cgp * 4 + hl + 1]
                dtok = s.op("vector", s.s_dve,
                            lambda E, d=pm, a=peb, m=mask, acc=den:
                            E.scalar_tensor_tensor(d, a, 1.0, m, AluOpType.mult,
                                                   AluOpType.mult, accum_out=acc))
                s.slot[("peb", h % 2)] = (s.s_dve, dtok)
                stt_toks.append(dtok)
            return dict(c=c, g=g, gc=gc, cgp=cgp, stt=stt_toks)

        def pass2(ctx):
            c, g, gc, cgp = ctx["c"], ctx["g"], ctx["gc"], ctx["cgp"]
            cg2 = (c * NKV + g) % 2
            pT = s.v1(s.o_pT, 2560)
            deppT = s.slot.get(("pT",))
            s.wait("vector", s.s_dve, ctx["stt"][-1])
            rec = s.rec4[:, cgp * 4:cgp * 4 + 4]
            den = s.den4[:, cgp * 4:cgp * 4 + 4]
            rtok = s.op("vector", s.s_dve, lambda E, r=rec, d=den: E.reciprocal(r, d))
            ev_toks = []
            for hl in range(4):
                pm = s.v1(s.o_pm + cgp * 4 * MW + hl * MW, MW)
                s.wait("vector", s.s_dve, rtok)
                rsc = s.rec4[:, cgp * 4 + hl:cgp * 4 + hl + 1]
                ntok = s.op("vector", s.s_dve,
                            lambda E, d=pm, a=pm, r=rsc:
                            E.tensor_scalar(d, a, r, None, AluOpType.mult))
                tps = s.tA if hl % 2 == 0 else s.tB
                dept = s.slot.get(("tpsA", hl % 2))
                if dept:
                    s.wait("tensor", *dept)
                s.wait("tensor", s.s_dve, ntok)
                srcs = [pm[:, j * P:(j + 1) * P] for j in range(5)]
                dsts = [tps[:, j * P:(j + 1) * P] for j in range(5)]
                def ftr(E, srcs=srcs, dsts=dsts):
                    for d_, a_ in zip(dsts, srcs):
                        tr = E.transpose(d_, a_, s.ident_sb[:, :])
                    return tr
                petok = s.op("tensor", s.s_pe, ftr)
                s.slot[("pm", cgp, hl)] = (s.s_pe, petok)
                s.wait("vector", s.s_pe, petok)
                if deppT:
                    s.wait("vector", *deppT)
                dst = pT.rearrange("p (j q) -> p j q", q=512)[:, 0:5, hl * P:(hl + 1) * P]
                src = tps.ap().rearrange("p (j q) -> p j q", q=P)[:, 0:5, :]
                etok = s.op("vector", s.s_dve, lambda E, d=dst, a=src: E.tensor_copy(d, a))
                s.slot[("tpsA", hl % 2)] = (s.s_dve, etok)
                ev_toks.append(etok)
            aps = s.pA if cg2 == 0 else s.pB
            depa = s.slot.get(("aps", cg2))
            if depa:
                s.wait("tensor", *depa)
            s.wait("tensor", s.s_dve, ev_toks[-1])
            vls = []
            for j in range(5):
                kcg = gc - 4 + j
                pj = (kcg // 4) % 2
                sj = kcg % 4
                o = (pj * NSUB + sj) * 512 + g * P
                vls.append(s.v_roll[:, o:o + P])
            pTs = [pT[:, j * 512:(j + 1) * 512] for j in range(5)]
            def fpv(E, vls=vls, pTs=pTs, aps=aps):
                for j in range(5):
                    mm = E.matmul(aps[:, :], vls[j], pTs[j], start=(j == 0), stop=(j == 4))
                return mm
            pvtok = s.op("tensor", s.s_pe, fpv)
            s.slot[("pT",)] = (s.s_pe, pvtok)
            s.wait("vector", s.s_pe, pvtok)
            dst = s.AR2.ap().rearrange("p (h q) -> p h q", q=TILE)[:, 4 * g:4 * g + 4, c * P:(c + 1) * P]
            src = aps.ap().rearrange("p (h q) -> p h q", q=P)[:, 0:4, :]
            dtok = s.op("vector", s.s_dve, lambda E, d=dst, a=src: E.tensor_copy(d, a))
            s.slot[("aps", cg2)] = (s.s_dve, dtok)

        pend = []
        for c in range(NSUB):
            for g in range(NKV):
                pend.append(pass1(c, g))
                if len(pend) > 2:
                    pass2(pend.pop(0))
        for ctx in pend:
            pass2(ctx)
        s.close()

    def emit_ln(self, src, dst):
        s = self
        s.wait("vector", s.s_dve, s.s_dve.n)
        atok = None
        for r in range(4):
            sv = src[:, r * 512:(r + 1) * 512]
            dv = s.stats[:, 6 * r:6 * r + 6]
            atok = s.op("vector", s.s_dve, lambda E, d=dv, a=sv: E.bn_stats(d, a))
        s.wait("vector", s.s_dve, atok)
        gtok = s.op("vector", s.s_dve, lambda E: E.bn_aggr(s.mv[:, :], s.stats[:, :]))
        s.wait("scalar", s.s_dve, gtok)
        ltok = s.op("scalar", s.s_act,
                    lambda E: E.activation(s.lnv[:, :], s.mv[:, 1:2], AF.Ln, bias=f32(1e-5)))
        s.wait("scalar", s.s_act, ltok)
        stok = s.op("scalar", s.s_act,
                    lambda E: E.activation(s.rstd[:, :], s.lnv[:, :], AF.Exp, scale=f32(-0.5)))
        s.wait("vector", s.s_act, stok)
        s.wait("vector", s.s_dve, gtok)
        tok = s.op("vector", s.s_dve,
                   lambda E, a=src, d=dst: E.tensor_scalar(d, a, s.mv[:, 0:1], s.rstd[:, 0:1],
                                                           AluOpType.subtract, AluOpType.mult))
        return tok

    def stage_O(self, t):
        s = self
        s.open()
        attn = s.AR2.ap().rearrange("p (h q) -> p h q", q=TILE)
        x1fm = s.x1_fm.ap().rearrange("p (kc q) -> p kc q", q=TILE)
        xtm = s.x_tm_bf.ap().rearrange("p (u c) -> p u c", c=D)
        r1b = s.v1(s.o_r1b, NSUB * D)
        r1bv = r1b.rearrange("p (u c) -> p u c", c=D)
        wo_rdy = {}

        def issue_wo(oc):
            ring_ = oc % 2
            wo_ = s.v1(s.o_wo + ring_ * D, D)
            dep_ = s.slot.get(("wo", ring_))
            wo_rdy[oc] = s.load_unit(("Wo", oc), s.dsem("wo%d" % ring_), wo_,
                                     s.sWo.ap()[oc * P:(oc + 1) * P, :], dep=dep_,
                                     queue="sync" if ring_ == 0 else "scalar")
        issue_wo(0)
        issue_wo(1)
        for oc in range(KC):
            ring = oc % 2
            wo = s.v1(s.o_wo + ring * D, D)
            rdy = wo_rdy[oc]
            s.wait("tensor", rdy[0], rdy[1])
            dps = s.pA if oc % 2 == 0 else s.pB
            depo = s.slot.get(("dpsO", oc % 2))
            if depo:
                s.wait("tensor", *depo)
            lhs = [wo[:, ic * P:(ic + 1) * P] for ic in range(KC)]
            rhs = [attn[:, ic, :] for ic in range(KC)]
            def fmm(E, lhs=lhs, rhs=rhs, dps=dps):
                for ic in range(KC):
                    mm = E.matmul(dps[:, :], lhs[ic], rhs[ic],
                                  start=(ic == 0), stop=(ic == KC - 1))
                return mm
            petok = s.op("tensor", s.s_pe, fmm)
            s.slot[("wo", ring)] = (s.s_pe, petok)
            if oc + 2 < KC:
                issue_wo(oc + 2)
            wob = s.v1(s.o_silu + (oc % 2) * 512, 512)
            depf = s.slot.get(("wob", oc % 2))
            if depf:
                s.wait("scalar", *depf)
            s.wait("scalar", s.s_pe, petok)
            etok = s.op("scalar", s.s_act, lambda E, d=wob, a=dps: E.copy(d, a[:, :]))
            s.slot[("dpsO", oc % 2)] = (s.s_act, etok)
            tps = s.tA if oc % 2 == 0 else s.tB
            for k_ in (("tpsO", oc % 2), ("tps", oc % 2)):
                d_ = s.slot.get(k_)
                if d_:
                    s.wait("tensor", *d_)
            s.wait("tensor", s.s_act, etok)
            dsts = [tps[:, j * P:(j + 1) * P] for j in range(4)]
            srcs = [wob[:, j * P:(j + 1) * P] for j in range(4)]
            def ftr(E, srcs=srcs, dsts=dsts):
                for d_, a_ in zip(dsts, srcs):
                    tr = E.transpose(d_, a_, s.ident_sb[:, :])
                return tr
            ttok = s.op("tensor", s.s_pe, ftr)
            s.slot[("wob", oc % 2)] = (s.s_pe, ttok)
            s.wait("vector", s.s_pe, ttok)
            dst = r1bv[:, 0:4, oc * P:(oc + 1) * P]
            src = tps.ap().rearrange("p (u q) -> p u q", q=P)[:, 0:4, :]
            xb = xtm[:, 0:4, oc * P:(oc + 1) * P]
            rtok = s.op("vector", s.s_dve,
                        lambda E, d=dst, a=src, b=xb: E.tensor_tensor(d, a, b, AluOpType.add))
            s.slot[("tpsO", oc % 2)] = (s.s_dve, rtok)
        for sub in range(NSUB):
            x1b = s.x1_bf[:, sub * D:(sub + 1) * D]
            aptok = self.emit_ln(r1b[:, sub * D:(sub + 1) * D], x1b)
            for half in range(2):
                tps = s.tA if half == 0 else s.tB
                for k_ in (("tps", half), ("tpsO", half), ("wob", half)):
                    d_ = s.slot.get(k_)
                    if d_:
                        s.wait("tensor", *d_)
                s.wait("tensor", s.s_dve, aptok)
                srcs = [x1b[:, (half * 8 + j) * P:(half * 8 + j + 1) * P] for j in range(8)]
                dsts = [tps[:, j * P:(j + 1) * P] for j in range(8)]
                def ftr(E, srcs=srcs, dsts=dsts):
                    for d_, a_ in zip(dsts, srcs):
                        tr = E.transpose(d_, a_, s.ident_sb[:, :])
                    return tr
                petok = s.op("tensor", s.s_pe, ftr)
                s.wait("vector", s.s_pe, petok)
                dst = x1fm[:, half * 8:(half + 1) * 8, sub * P:(sub + 1) * P]
                src = tps.ap().rearrange("p (j q) -> p j q", q=P)[:, 0:8, :]
                dtok = s.op("vector", s.s_dve, lambda E, d=dst, a=src: E.tensor_copy(d, a))
                s.slot[("tps", half)] = (s.s_dve, dtok)
        s.close()

    def stage_F(self, t, half):
        s = self
        s.open()
        subs = (2 * half, 2 * half + 1)
        x1fm = s.x1_fm.ap().rearrange("p (kc q) -> p kc q", q=TILE)
        hspf = s.v1(s.o_hspf, NIC * 256).rearrange("p (ic q) -> p ic q", q=256)
        def gps_of(sub_l, sl):
            if sub_l == 1:
                return s.sc0[:, 0:512]
            return s.pA[:, :] if sl % 2 == 0 else s.sc1[:, 0:512]

        def ups_of(sub_l, sl):
            if sub_l == 1:
                return s.sc0[:, 512:1024]
            return s.pB[:, :] if sl % 2 == 0 else s.sc1[:, 512:1024]

        sp_pending = []
        sp_state = {"cur": None, "tok": None}

        def emit_hsp_transposes(grp, sub_l, sptok):
            s.wait("tensor", s.s_dve, sptok)
            hspt = s.v1(s.o_hspt + (grp % 2) * D, D)
            for tph in range(2):
                tps = s.tA if tph == 0 else s.tB
                deptp = s.slot.get(("tps", tph))
                if deptp:
                    s.wait("tensor", *deptp)
                srcs = [hspt[:, (tph * 8 + j) * P:(tph * 8 + j + 1) * P] for j in range(8)]
                dsts = [tps[:, j * P:(j + 1) * P] for j in range(8)]
                def ftr(E, srcs=srcs, dsts=dsts):
                    for d_, a_ in zip(dsts, srcs):
                        tr = E.transpose(d_, a_, s.ident_sb[:, :])
                    return tr
                petok = s.op("tensor", s.s_pe, ftr)
                s.wait("scalar", s.s_pe, petok)
                dst = hspf[:, grp * 16 + tph * 8:grp * 16 + (tph + 1) * 8, sub_l * P:(sub_l + 1) * P]
                src = tps.ap().rearrange("p (j q) -> p j q", q=P)[:, 0:8, :]
                etok = s.op("scalar", s.s_act, lambda E, d=dst, a=src: E.copy(d, a))
                s.slot[("tps", tph)] = (s.s_act, etok)
            s.slot[("hspt", grp % 2)] = (s.s_pe, s.s_pe.n)

        def emit_sparsify(k):
            while k > 0:
                if sp_state["cur"] is None:
                    if not sp_pending:
                        return
                    tag, grp, sub_l, builders = sp_pending.pop(0)
                    hw = s.slot.get(("hbw", grp % 2, sub_l))
                    if hw:
                        s.wait("vector", *hw)
                    dep = s.slot.get(("hspt", grp % 2))
                    if dep:
                        s.wait("vector", *dep)
                    sp_state["cur"] = (grp, sub_l, builders, 0)
                grp, sub_l, builders, i = sp_state["cur"]
                if sp_state["tok"]:
                    s.wait("vector", s.s_dve, sp_state["tok"])
                sp_state["tok"] = s.op("vector", s.s_dve, builders[i])
                i += 1
                k -= 1
                if i == len(builders):
                    emit_hsp_transposes(grp, sub_l, sp_state["tok"])
                    sp_state["cur"] = None
                else:
                    sp_state["cur"] = (grp, sub_l, builders, i)

        gu_rdy = {}

        def issue_gu(ci):
            sl_, q4_ = divmod(ci, 4)
            ring_ = ci % 4
            out = {}
            for (nm, o_ring, sW, q_) in (("Wg", s.o_wg, s.sWg, "sync"),
                                         ("Wu", s.o_wu, s.sWu, "scalar")):
                dst_ = s.v1(o_ring + ring_ * D, D)
                dep_ = s.slot.get((nm + "ring", ring_))
                out[nm] = s.load_unit((nm, sl_, q4_), s.dsem(nm + "r%d" % ring_), dst_,
                                      sW.ap()[sl_ * P:(sl_ + 1) * P, q4_ * 4 * 512:(q4_ + 1) * 4 * 512],
                                      dep=dep_, queue=q_)
            gu_rdy[ci] = out
        for ci0 in range(3):
            issue_gu(ci0)
        for sl in range(NSL):
            gtok = utok = None
            for q4 in range(4):
                ci = sl * 4 + q4
                ring = ci % 4
                if ci + 3 < NSL * 4:
                    issue_gu(ci + 3)
                rds = gu_rdy[ci]
                s.wait("tensor", rds["Wg"][0], rds["Wg"][1])
                s.wait("tensor", rds["Wu"][0], rds["Wu"][1])
                wg_c = s.v1(s.o_wg + ring * D, D)
                wu_c = s.v1(s.o_wu + ring * D, D)
                tok = None
                for sub_l in (1, 0):
                    sub = subs[sub_l]
                    if q4 == 0:
                        keys = ([("gps0", sl % 2), ("ups0", sl % 2)] if sub_l == 0
                                else [("gps1",), ("ups1",)])
                        for k_ in keys:
                            d_ = s.slot.get(k_)
                            if d_:
                                s.wait("tensor", *d_)
                    lhs = [x1fm[:, q4 * 4 + j, sub * P:(sub + 1) * P] for j in range(4)]
                    g_ps, u_ps = gps_of(sub_l, sl), ups_of(sub_l, sl)
                    def fmm(E, lhs=lhs, wg_c=wg_c, wu_c=wu_c, g_ps=g_ps, u_ps=u_ps, q4=q4):
                        for j in range(4):
                            st_f = (q4 == 0 and j == 0)
                            sp_f = (q4 == 3 and j == 3)
                            mm = E.matmul(g_ps, lhs[j], wg_c[:, j * 512:(j + 1) * 512],
                                          start=st_f, stop=sp_f)
                            mm2 = E.matmul(u_ps, lhs[j], wu_c[:, j * 512:(j + 1) * 512],
                                           start=st_f, stop=sp_f)
                        return mm2
                    tok = s.op("tensor", s.s_pe, fmm)
                    if q4 == 3:
                        if sub_l == 0:
                            gtok = tok
                        else:
                            utok = tok
                s.slot[("Wgring", ring)] = (s.s_pe, tok)
                s.slot[("Wuring", ring)] = (s.s_pe, tok)
            mm_done = {subs[0]: gtok, subs[1]: utok}
            sl_l = sl % 4
            grp = sl // 4
            for sub_l in (1, 0):
                sub = subs[sub_l]
                silu = s.v1(s.o_silu + sub_l * 512, 512)
                dw = s.slot.get(("silu_w", sub_l))
                if dw:
                    s.wait("scalar", *dw)
                s.wait("scalar", s.s_pe, mm_done[sub])
                g_ps, u_ps = gps_of(sub_l, sl), ups_of(sub_l, sl)
                stok = s.op("scalar", s.s_act, lambda E, d=silu, a=g_ps: E.activation(d, a, AF.Silu))
                s.slot[("gps0", sl % 2) if sub_l == 0 else ("gps1",)] = (s.s_act, stok)
                hb = s.v1(s.o_hbuf + (grp % 2) * 2 * D + sub_l * D + sl_l * 512, 512)
                s.wait("vector", s.s_act, stok)
                s.wait("vector", s.s_pe, mm_done[sub])
                htok = s.op("vector", s.s_dve,
                            lambda E, d=hb, a=u_ps, b=silu: E.tensor_tensor(d, a, b, AluOpType.mult))
                s.slot[("ups0", sl % 2) if sub_l == 0 else ("ups1",)] = (s.s_dve, htok)
                s.slot[("hbw", grp % 2, sub_l)] = (s.s_dve, htok)
                s.slot[("silu_w", sub_l)] = (s.s_dve, htok)
            if sl_l == 3:
                for sub_l in (1, 0):
                    sub = subs[sub_l]
                    hb = s.v1(s.o_hbuf + (grp % 2) * 2 * D + sub_l * D, D)
                    habs = s.v1(s.o_habs, D)
                    hspt = s.v1(s.o_hspt + (grp % 2) * D, D)
                    thr = s.v1(s.o_thr, 512)
                    t1_ = s.v1(s.o_thr + 512, 512)
                    t2_ = s.v1(s.o_thr + 1024, 512)
                    t3_ = s.v1(s.o_thr + 1536, 512)
                    h0, h1, h2, h3 = (habs[:, k:D:4] for k in range(4))
                    builders = [
                        lambda E, habs=habs, hb=hb: E.tensor_tensor(habs, hb, hb, AluOpType.mult),
                        lambda E, t1_=t1_, h0=h0, h1=h1: E.tensor_tensor(t1_, h0, h1, AluOpType.max),
                        lambda E, t2_=t2_, h0=h0, h1=h1: E.tensor_tensor(t2_, h0, h1, AluOpType.min),
                        lambda E, t3_=t3_, h2=h2, h3=h3: E.tensor_tensor(t3_, h2, h3, AluOpType.max),
                        lambda E, thr=thr, h2=h2, h3=h3: E.tensor_tensor(thr, h2, h3, AluOpType.min),
                        lambda E, t1_=t1_, t3_=t3_: E.tensor_tensor(t1_, t1_, t3_, AluOpType.min),
                        lambda E, t2_=t2_, thr=thr: E.tensor_tensor(t2_, t2_, thr, AluOpType.max),
                        lambda E, thr=thr, t1_=t1_, t2_=t2_: E.tensor_tensor(thr, t1_, t2_, AluOpType.max),
                    ]
                    for k in range(4):
                        tt = t1_ if k % 2 == 0 else t2_
                        hak = habs[:, k:D:4]
                        hbk = hb[:, k:D:4]
                        hok = hspt[:, k:D:4]
                        builders.append(lambda E, tt=tt, hak=hak, thr=thr:
                                        E.tensor_tensor(tt, hak, thr, AluOpType.is_ge))
                        builders.append(lambda E, hok=hok, tt=tt, hbk=hbk:
                                        E.tensor_tensor(hok, tt, hbk, AluOpType.mult))
                    sp_pending.append(("start", grp, sub_l, builders))
            emit_sparsify(9)

        while sp_pending or sp_state["cur"] is not None:
            emit_sparsify(8)

        for k_ in (("tps", 0), ("tps", 1), ("gps0", 0), ("gps0", 1), ("ups0", 0),
                   ("ups0", 1), ("gps1",), ("ups1",)):
            d_ = s.slot.get(k_)
            if d_:
                s.wait("tensor", *d_)
        r2 = s.AR1[:, s.o_r2:s.o_r2 + 2 * D]
        wd_rdy = {}

        def issue_wd(idx):
            oc_, hu_ = divmod(idx, 2)
            ring_ = idx % 2
            dep_ = s.slot.get(("wdring", ring_))
            sem_a = s.dsem("wd%da" % ring_)
            sem_b = s.dsem("wd%db" % ring_)
            base = ring_ * 4096
            scr = s.sWd.ap()[oc_ * P:(oc_ + 1) * P, hu_ * 4096:(hu_ + 1) * 4096]
            ta = s.load_unit(("Wd", oc_, hu_, 0), sem_a, s.AR2[:, base:base + 2048],
                             scr[:, 0:2048], dep=dep_, queue="sync")
            tb = s.load_unit(("Wd", oc_, hu_, 1), sem_b, s.AR2[:, base + 2048:base + 4096],
                             scr[:, 2048:4096], dep=dep_, queue="scalar")
            wd_rdy[idx] = (ta, tb)
        issue_wd(0)
        for oc in range(KC):
            dps = s.pA if oc % 2 == 0 else s.pB
            dtok = None
            for hu in range(2):
                idx = oc * 2 + hu
                ring = idx % 2
                wd = s.AR2[:, ring * 4096:(ring + 1) * 4096]
                if idx + 1 < 2 * KC:
                    issue_wd(idx + 1)
                ta, tb = wd_rdy[idx]
                s.wait("tensor", ta[0], ta[1])
                s.wait("tensor", tb[0], tb[1])
                if hu == 0:
                    depo = s.slot.get(("dps", oc % 2))
                    if depo:
                        s.wait("tensor", *depo)
                lhs = [wd[:, k * P:(k + 1) * P] for k in range(32)]
                rhs = [hspf[:, hu * 32 + k, :] for k in range(32)]
                def fdn(E, lhs=lhs, rhs=rhs, dps=dps, hu=hu):
                    for k in range(32):
                        mm = E.matmul(dps[:, 0:256], lhs[k], rhs[k],
                                      start=(hu == 0 and k == 0), stop=(hu == 1 and k == 31))
                    return mm
                dtok = s.op("tensor", s.s_pe, fdn)
                s.slot[("wdring", ring)] = (s.s_pe, dtok)
            dfm = s.v1(s.o_dfm + (oc % 2) * 256, 256)
            s.wait("scalar", s.s_pe, dtok)
            depf = s.slot.get(("dfm", oc % 2))
            if depf:
                s.wait("scalar", *depf)
            etok = s.op("scalar", s.s_act, lambda E, d=dfm, a=dps: E.copy(d, a[:, 0:256]))
            s.slot[("dps", oc % 2)] = (s.s_act, etok)
            tps = s.tA if oc % 2 == 0 else s.tB
            for k_ in (("tpsF3", oc % 2), ("tps", oc % 2)):
                d_ = s.slot.get(k_)
                if d_:
                    s.wait("tensor", *d_)
            s.wait("tensor", s.s_act, etok)
            dsts = [tps[:, j * P:(j + 1) * P] for j in range(2)]
            srcs = [dfm[:, j * P:(j + 1) * P] for j in range(2)]
            def ftr(E, srcs=srcs, dsts=dsts):
                for d_, a_ in zip(dsts, srcs):
                    tr = E.transpose(d_, a_, s.ident_sb[:, :])
                return tr
            petok = s.op("tensor", s.s_pe, ftr)
            s.slot[("dfm", oc % 2)] = (s.s_pe, petok)
            s.wait("vector", s.s_pe, petok)
            dst = r2.rearrange("p (u c) -> p u c", c=D)[:, 0:2, oc * P:(oc + 1) * P]
            src = tps.ap().rearrange("p (u q) -> p u q", q=P)[:, 0:2, :]
            xb = s.x1_bf.ap().rearrange("p (u c) -> p u c", c=D)[:, 2 * half:2 * half + 2, oc * P:(oc + 1) * P]
            rtok = s.op("vector", s.s_dve,
                        lambda E, d=dst, a=src, b=xb: E.tensor_tensor(d, a, b, AluOpType.add))
            s.slot[("tpsF3", oc % 2)] = (s.s_dve, rtok)
        for sub_l, sub in enumerate(subs):
            osb = s.AR3[:, sub_l * D:(sub_l + 1) * D]
            depo = s.slot.get(("osb_v", sub_l))
            if depo:
                s.wait("vector", *depo)
            aptok = self.emit_ln(r2[:, sub_l * D:(sub_l + 1) * D], osb)
            s.wait("sync", s.s_dve, aptok)
            osem = s.dsem("out%d" % sub_l)
            otok = s.dma("sync", osem, s.out.ap()[t * TILE + sub * P:t * TILE + (sub + 1) * P, :],
                         osb)
            s.slot[("osb", sub_l)] = (osem, otok)
            s.slot[("osb_v", sub_l)] = (osem, otok)
        s.close()

    def build(self):
        s = self
        s.open()
        s.dma("sync", s.s_cst, s.ident_sb[:, :], s.ident.ap())
        for e in ("tensor",):
            s.wait(e, s.s_cst, 16)
        s.close()
        s.stage_precast()
        s.stage_X(-1)
        s.stage_QKV(-1)
        for t in range(s.n_tiles):
            s.stage_X(t)
            s.stage_QKV(t)
            s.stage_A(t)
            if s.debug and t == 0:
                s.dbg("xfm", s.v1(s.o_xfm, KC * TILE), KC * TILE, BF16)
                s.dbg("q", s.v1(s.o_q, NH * 512), NH * 512, BF16)
                s.dbg("kroll", s.k_roll[:, :], 2 * NKV * 512, BF16)
                s.dbg("vroll", s.v_roll[:, :], 2 * NSUB * 512, BF16)
                s.dbg("attn", s.AR2[:, :], 8192, BF16)
            s.stage_O(t)
            if s.debug and t == 0:
                s.dbg("x1", s.x1_bf[:, :], NSUB * D, BF16)
                s.dbg("x1fm", s.x1_fm[:, :], KC * TILE, BF16)
            for half in range(2):
                s.stage_F(t, half)
            if s.debug and t == 0:
                s.dbg("hspf", s.v1(s.o_hspf, NIC * 256), NIC * 256, BF16)
        s.open()
        def fin(E):
            for nm, sm in s.dsems.items():
                if nm.endswith("_wb") or nm.startswith("out"):
                    E.wait_ge(sm.h, sm.n)
            E.wait_ge(s.s_dbg.h, s.s_dbg.n)
        s.stage.add("sync", fin)
        s.close()
        return s.nc


def build_masks(seq_start: bool) -> np.ndarray:
    m = np.zeros((5, P, MW), dtype=np.float32)
    r = np.arange(P)[:, None]
    kw = np.arange(MW)[None, :]
    band = (kw >= r + 1) & (kw <= r + WIN)
    for v in range(5):
        lo = (WIN - P * v) if (seq_start and v < 4) else 0
        m[v] = (band & (kw >= lo)).astype(np.float32)
    return np.ascontiguousarray(m.transpose(1, 0, 2).reshape(P, 5 * MW)).astype(BB)


def kernel(L0_attn_hidden_states, L0_attn_Wq, L0_attn_Wk, L0_attn_Wv, L0_attn_Wo,
           L0_ffn_w_gate, L0_ffn_w_up, L0_ffn_w_down):
    x = np.ascontiguousarray(np.asarray(L0_attn_hidden_states, dtype=np.float32))
    B, S, _ = x.shape
    n_cores = 8
    per = (B * S) // n_cores
    mk = MK(n_tiles=NT, debug=False)
    nc = mk.build()
    ident = np.eye(P, dtype=BB)
    w = dict(L0_attn_Wq=np.asarray(L0_attn_Wq, np.float32),
             L0_attn_Wk=np.asarray(L0_attn_Wk, np.float32),
             L0_attn_Wv=np.asarray(L0_attn_Wv, np.float32),
             L0_attn_Wo=np.asarray(L0_attn_Wo, np.float32),
             L0_ffn_w_gate=np.asarray(L0_ffn_w_gate, np.float32),
             L0_ffn_w_up=np.asarray(L0_ffn_w_up, np.float32),
             L0_ffn_w_down=np.asarray(L0_ffn_w_down, np.float32))
    xf = x.reshape(B * S, D)
    in_maps = []
    for c in range(n_cores):
        t0 = c * per
        seq_start = (t0 % S) == 0
        halo = np.zeros((HALO, D), np.float32) if seq_start else xf[t0 - HALO:t0]
        xs = np.ascontiguousarray(np.concatenate([halo, xf[t0:t0 + per]], axis=0))
        im = {"x": xs, "mask01": build_masks(seq_start), "ident": ident}
        im.update(w)
        in_maps.append(im)
    res = run_bass_kernel_spmd(nc, in_maps, core_ids=list(range(n_cores)))
    outs = [res.results[c]["out"] for c in range(n_cores)]
    return np.concatenate(outs, axis=0).reshape(B, S, D).astype(np.float32)


# version 08
# speedup vs baseline: 1.0234x; 1.0070x over previous
import numpy as np
import ml_dtypes
from contextlib import ExitStack

import concourse.bass as bass
import concourse.mybir as mybir
from concourse.alu_op_type import AluOpType
from concourse.bass_utils import run_bass_kernel_spmd

F32 = mybir.dt.float32
BF16 = mybir.dt.bfloat16
AF = mybir.ActivationFunctionType
BB = ml_dtypes.bfloat16

P = 128
D = 2048
KC = 16
NH = 16
NKV = 4
WIN = 512
ISZ = 8192
NIC = 64
TILE = 512
NSUB = 4
HALO = 512
NT = 8
NSL = 16
MW = 640

f32 = np.float32
ENGS = ("sync", "gpsimd", "tensor", "scalar", "vector")


class Sem:
    def __init__(self, nc, es, name, step=1):
        self.h = es.enter_context(nc.semaphore(name))
        self.n = 0
        self.step = step
        self.name = name

    def reserve(self):
        self.n += self.step
        return self.n


class Stage:
    def __init__(self):
        self.lists = {e: [] for e in ENGS}

    def add(self, eng, fn):
        self.lists[eng].append(fn)


class MK:
    def __init__(self, n_tiles=NT, debug=False):
        self.n_tiles = n_tiles
        self.debug = debug
        self.dbg_names = []
        nc = bass.Bass("TRN2", target_bir_lowering=False)
        self.nc = nc
        self.es = ExitStack()
        es = self.es
        ntok = HALO + n_tiles * TILE

        dram = nc.dram_tensor
        self.x = dram("x", [ntok, D], F32, kind="ExternalInput")
        self.Wq = dram("L0_attn_Wq", [D, D], F32, kind="ExternalInput")
        self.Wk = dram("L0_attn_Wk", [D, 512], F32, kind="ExternalInput")
        self.Wv = dram("L0_attn_Wv", [D, 512], F32, kind="ExternalInput")
        self.Wo = dram("L0_attn_Wo", [D, D], F32, kind="ExternalInput")
        self.Wg = dram("L0_ffn_w_gate", [D, ISZ], F32, kind="ExternalInput")
        self.Wu = dram("L0_ffn_w_up", [D, ISZ], F32, kind="ExternalInput")
        self.Wd = dram("L0_ffn_w_down", [ISZ, D], F32, kind="ExternalInput")
        self.mask = dram("mask01", [P, 5 * MW], BF16, kind="ExternalInput")
        self.ident = dram("ident", [P, P], BF16, kind="ExternalInput")
        self.out = dram("out", [n_tiles * TILE, D], F32, kind="ExternalOutput")
        self.sWq = dram("sWq", [D, D], BF16)
        self.sWk = dram("sWk", [512, D], BF16)
        self.sWv = dram("sWv", [P, KC * 512], BF16)
        self.sWo = dram("sWo", [D, D], BF16)
        self.sWg = dram("sWg", [NSL * P, KC * 512], BF16)
        self.sWu = dram("sWu", [NSL * P, KC * 512], BF16)
        self.sWd = dram("sWd", [KC * P, NIC * P], BF16)

        self.s_pe = Sem(nc, es, "s_pe")
        self.s_act = Sem(nc, es, "s_act")
        self.s_dve = Sem(nc, es, "s_dve")
        self.s_pool = Sem(nc, es, "s_pool")
        self.s_cst = Sem(nc, es, "s_cst", 16)
        self.s_dbg = Sem(nc, es, "s_dbg", 16)
        self.dsems = {}
        self.waited = {}
        self.slot_wb = {}
        self.fence_sems = set()

        sb = lambda name, cols, dt: es.enter_context(nc.sbuf_tensor(name, [P, cols], dt))
        self.AR1 = sb("AR1", 50688, BF16)
        self.AR2 = sb("AR2", 8192, BF16)
        self.AR3 = sb("AR3", 4096, F32)
        self.x1_bf = sb("x1_bf", NSUB * D, BF16)
        self.x1_fm = sb("x1_fm", KC * TILE, BF16)
        self.x_tm_bf = sb("x_tm_bf", NSUB * D, BF16)
        self.ident_sb = sb("ident_sb", P, BF16)
        self.k_roll = sb("k_roll", 2 * NKV * 512, BF16)
        self.v_roll = sb("v_roll", 2 * NSUB * 512, BF16)
        self.den4 = sb("den4", 16, F32)
        self.rec4 = sb("rec4", 16, F32)
        self.stats = sb("stats", 24, F32)
        self.mv = sb("mv", 2, F32)
        self.lnv = sb("lnv", 1, F32)
        self.rstd = sb("rstd", 1, F32)

        self.o_xbf = 0
        self.o_xfm = 4096
        self.o_wq = 12288
        self.o_wk = 16384
        self.o_wv = 20480
        self.o_q = 28672
        self.o_pe_ = 36864
        self.o_pm = 38144
        self.o_pT = 45824
        self.o_wo = 0
        self.o_wg = 0
        self.o_wu = 8192
        self.o_hbuf = 16384
        self.o_habs = 30208
        self.o_thr = 32256
        self.o_hspt = 24576
        self.o_silu = 28672
        self.o_dfm = 29696
        self.o_r2 = 30208
        self.o_hspf = 34304
        self.o_mask = 0
        self.o_r1b = 34304

        ps = lambda name, cols, dt: es.enter_context(nc.psum_tensor(name, [P, cols], dt))
        self.sc0 = ps("sc0", 1024, F32)
        self.sc1 = ps("sc1", 1024, F32)
        self.tA = ps("tA", 1024, BF16)
        self.tB = ps("tB", 1024, BF16)
        self.pA = ps("pA", 512, F32)
        self.pB = ps("pB", 512, F32)

        self.wb_tok = {}
        self.slot = {}
        self.stage = None

    def v1(self, off, cols):
        return self.AR1[:, off:off + cols]

    def dsem(self, name):
        if name not in self.dsems:
            self.dsems[name] = Sem(self.nc, self.es, "sd_" + name, 16)
        return self.dsems[name]

    def wait(self, eng, sem, tok):
        if not tok:
            return
        key = (eng, sem.name)
        if self.waited.get(key, 0) >= tok:
            return
        self.waited[key] = tok
        self.stage.add(eng, lambda E, s=sem.h, t=tok: E.wait_ge(s, t))

    def op(self, eng, sem, builder):
        if sem is None:
            self.stage.add(eng, lambda E, b=builder: b(E))
            return None
        tok = sem.reserve()
        self.stage.add(eng, lambda E, b=builder, h=sem.h: b(E).then_inc(h, 1))
        return tok

    def dma(self, eng, sem, dst, src):
        tok = sem.reserve()
        self.stage.add(eng, lambda E, d=dst, a=src, h=sem.h: E.dma_start(d, a).then_inc(h, 16))
        return tok

    def open(self):
        self.stage = Stage()

    def close(self):
        st = self.stage
        for sem in sorted(self.fence_sems, key=lambda x: x.name):
            st.add("sync", lambda E, h=sem.h, n=sem.n: E.wait_ge(h, n))
            self.waited[("sync", sem.name)] = sem.n
        with self.nc.Block() as block:
            for e in ENGS:
                fns = st.lists[e]
                if fns:
                    def runner(E, fns=fns):
                        for f in fns:
                            f(E)
                    getattr(block, e)(runner)
        for sem in (self.s_pe, self.s_act, self.s_dve, self.s_pool):
            for e in ENGS:
                key = (e, sem.name)
                self.waited[key] = max(self.waited.get(key, 0), sem.n)
        names = {k[1] for k in self.waited}
        for nm in names:
            m = max(self.waited.get((e, nm), 0) for e in ENGS)
            for e in ENGS:
                self.waited[(e, nm)] = m
        self.fence_sems = set()
        self.stage = None

    def load_unit(self, key, slot_sem, dst, scratch, dep=None, queue="sync"):
        s = self
        eng = "sync" if queue == "sync" else "scalar"
        if dep:
            s.wait(eng, dep[0], dep[1])
        tok = s.dma(eng, slot_sem, dst, scratch)
        return (slot_sem, tok)

    def stage_precast(self):
        s = self
        s.open()
        ldsem = [s.dsem("pcl0"), s.dsem("pcl1")]
        wbsem = [s.dsem("pc0_wb"), s.dsem("pc1_wb")]
        state = {"cast": [None, None], "i": 0, "asm_rd": None}
        o_asm = 16384

        def cast_chunk(gat, scr=None, asm_dst=None, asm_src_view=None):
            i = state["i"]; state["i"] += 1
            sl_ = i % 2
            stg = s.AR3[:, sl_ * 2048:(sl_ + 1) * 2048]
            bfs = s.v1(s.o_xbf + sl_ * D, D)
            if state["cast"][sl_]:
                s.wait("sync", s.s_dve, state["cast"][sl_])
            ltok = s.dma("sync", ldsem[sl_], stg[:, 0:gat.shape[-1] if False else 2048] if False else stg, gat)
            s.wait("vector", ldsem[sl_], ltok)
            s.wait("vector", wbsem[sl_], wbsem[sl_].n)
            ctok = s.op("vector", s.s_dve, lambda E, d=bfs, a=stg: E.tensor_copy(d, a))
            state["cast"][sl_] = ctok
            if scr is not None:
                s.wait("scalar", s.s_dve, ctok)
                s.dma("scalar", wbsem[sl_], scr, bfs)
            else:
                s.wait("vector", s.s_dve, ctok)
                ctok2 = s.op("vector", s.s_dve,
                             lambda E, d=asm_dst, a=asm_src_view(bfs): E.tensor_copy(d, a))
                state["cast"][sl_] = ctok2
                return ctok2
            return ctok

        asem = s.dsem("pca_wb")

        def flush_asm(wb_list):
            s.wait("scalar", s.s_dve, s.s_dve.n)
            for scr, lo, cols in wb_list:
                s.dma("scalar", asem, scr, s.v1(o_asm + lo, cols))
            state["asm_rd"] = asem.n

        def asm_guard():
            if state["asm_rd"]:
                s.wait("vector", asem, state["asm_rd"])
                state["asm_rd"] = None

        A = s.v1(o_asm, 32768).rearrange("p (u c) -> p u c", c=1)

        for kc in range(KC):
            asm_guard()
            gat = s.Wq.ap()[kc * P:(kc + 1) * P, :]
            dst = s.v1(o_asm, NH * D).rearrange("p (h m) -> p h m", m=P)[:, :, :]
            dstv = s.v1(o_asm + kc * P, 1)
            dst2 = bass.AP(s.AR1, o_asm + kc * P, [[50688, P], [D, NH], [1, P]])
            src_view = lambda bfs: bfs.rearrange("p (h m) -> p h m", m=P)
            cast_chunk(gat, asm_dst=dst2, asm_src_view=src_view)
        flush_asm([(s.sWq.ap()[h * P:(h + 1) * P, :], h * D, D) for h in range(NH)])

        for kc in range(KC):
            asm_guard()
            gat = s.Wk.ap()[kc * P:(kc + 1) * P, :]
            dst2 = bass.AP(s.AR1, o_asm + kc * P, [[50688, P], [D, NKV], [1, P]])
            i = state["i"]; state["i"] += 1
            sl_ = i % 2
            stg = s.AR3[:, sl_ * 2048:sl_ * 2048 + 512]
            bfs = s.v1(s.o_xbf + sl_ * D, 512)
            if state["cast"][sl_]:
                s.wait("sync", s.s_dve, state["cast"][sl_])
            ltok = s.dma("sync", ldsem[sl_], stg, gat)
            s.wait("vector", ldsem[sl_], ltok)
            ctok = s.op("vector", s.s_dve, lambda E, d=bfs, a=stg: E.tensor_copy(d, a))
            s.wait("vector", s.s_dve, ctok)
            sv = bfs.rearrange("p (g m) -> p g m", m=P)
            ctok2 = s.op("vector", s.s_dve, lambda E, d=dst2, a=sv: E.tensor_copy(d, a))
            state["cast"][sl_] = ctok2
        flush_asm([(s.sWk.ap()[g * P:(g + 1) * P, :], g * D, D) for g in range(NKV)])

        for kc in range(KC):
            asm_guard()
            i = state["i"]; state["i"] += 1
            sl_ = i % 2
            stg = s.AR3[:, sl_ * 2048:sl_ * 2048 + 512]
            if state["cast"][sl_]:
                s.wait("sync", s.s_dve, state["cast"][sl_])
            ltok = s.dma("sync", ldsem[sl_], stg, s.Wv.ap()[kc * P:(kc + 1) * P, :])
            s.wait("vector", ldsem[sl_], ltok)
            dd = s.v1(o_asm + kc * 512, 512)
            ctok = s.op("vector", s.s_dve, lambda E, d=dd, a=stg: E.tensor_copy(d, a))
            state["cast"][sl_] = ctok
        flush_asm([(s.sWv.ap(), 0, KC * 512)])

        for ic in range(KC):
            asm_guard()
            gat = s.Wo.ap()[ic * P:(ic + 1) * P, :]
            dst2 = bass.AP(s.AR1, o_asm + ic * P, [[50688, P], [D, KC], [1, P]])
            src_view = lambda bfs: bfs.rearrange("p (oc m) -> p oc m", m=P)
            cast_chunk(gat, asm_dst=dst2, asm_src_view=src_view)
        flush_asm([(s.sWo.ap()[oc * P:(oc + 1) * P, :], oc * D, D) for oc in range(KC)])

        for qp in range(4):
            for icl in range(KC):
                asm_guard()
                ic = qp * KC + icl
                gat = s.Wd.ap()[ic * P:(ic + 1) * P, :]
                dst2 = bass.AP(s.AR1, o_asm + icl * P, [[50688, P], [D, KC], [1, P]])
                src_view = lambda bfs: bfs.rearrange("p (oc m) -> p oc m", m=P)
                cast_chunk(gat, asm_dst=dst2, asm_src_view=src_view)
            flush_asm([(s.sWd.ap()[oc * P:(oc + 1) * P, qp * 2048:(qp + 1) * 2048], oc * D, D)
                       for oc in range(KC)])

        wgr = s.Wg.ap().rearrange("(kc p) i -> p kc i", p=P)
        wur = s.Wu.ap().rearrange("(kc p) i -> p kc i", p=P)
        for (Wr, sW) in ((wgr, s.sWg), (wur, s.sWu)):
            for sl in range(NSL):
                for q in range(4):
                    cast_chunk(Wr[:, q * 4:(q + 1) * 4, sl * 512:(sl + 1) * 512],
                               scr=sW.ap()[sl * P:(sl + 1) * P, q * 2048:(q + 1) * 2048])
        s.fence_sems.add(wbsem[0])
        s.fence_sems.add(wbsem[1])
        s.fence_sems.add(asem)
        s.close()

    def dbg(self, name, src_ap, cols, dt):
        if not self.debug:
            return
        t = self.nc.dram_tensor("dbg_" + name, [P, cols], dt, kind="ExternalOutput")
        self.dbg_names.append("dbg_" + name)
        self.open()
        self.dma("sync", self.s_dbg, t.ap(), src_ap)
        self.close()

    def stage_X(self, t):
        s = self
        s.open()
        row0 = 0 if t == -1 else HALO + t * TILE
        xfm = s.v1(s.o_xfm, KC * TILE).rearrange("p (kc q) -> p kc q", q=TILE)
        x_tok = {}

        def issue_x(sub):
            sl2 = sub % 2
            stg_ = s.AR3[:, sl2 * 2048:(sl2 + 1) * 2048]
            depc = s.slot.get(("xstg", sl2))
            if depc:
                s.wait("sync", *depc)
            dob = s.slot.get(("osb", sl2))
            if dob:
                s.wait("sync", *dob)
            x_tok[sub] = s.dma("sync", s.dsem("xb%d" % sl2), stg_,
                               s.x.ap()[row0 + sub * P:row0 + (sub + 1) * P, :])
        issue_x(0)
        issue_x(1)
        for sub in range(NSUB):
            sl_ = sub % 2
            xbf = s.x_tm_bf[:, sub * D:(sub + 1) * D]
            stg = s.AR3[:, sl_ * 2048:(sl_ + 1) * 2048]
            xsem = s.dsem("xb%d" % sl_)
            ltok = x_tok[sub]
            s.wait("vector", xsem, ltok)
            ctok = s.op("vector", s.s_dve, lambda E, d=xbf, a=stg: E.tensor_copy(d, a))
            s.slot[("xstg", sl_)] = (s.s_dve, ctok)
            if sub + 2 < NSUB:
                issue_x(sub + 2)
            s.wait("tensor", s.s_dve, ctok)
            last_pe = None
            for half in range(2):
                tps = s.tA if half == 0 else s.tB
                dep2 = s.slot.get(("tps", half))
                if dep2:
                    s.wait("tensor", *dep2)
                srcs = [xbf[:, (half * 8 + j) * P:(half * 8 + j + 1) * P] for j in range(8)]
                dsts = [tps[:, j * P:(j + 1) * P] for j in range(8)]
                def ftr(E, srcs=srcs, dsts=dsts):
                    for d_, a_ in zip(dsts, srcs):
                        tr = E.transpose(d_, a_, s.ident_sb[:, :])
                    return tr
                petok = s.op("tensor", s.s_pe, ftr)
                last_pe = petok
                s.wait("vector", s.s_pe, petok)
                dst = xfm[:, half * 8:(half + 1) * 8, sub * P:(sub + 1) * P]
                src = tps.ap().rearrange("p (j q) -> p j q", q=P)[:, 0:8, :]
                dtok = s.op("vector", s.s_dve, lambda E, d=dst, a=src: E.tensor_copy(d, a))
                s.slot[("tps", half)] = (s.s_dve, dtok)
        s.close()

    def stage_QKV(self, t):
        s = self
        s.open()
        halo = (t == -1)
        par = t % 2
        xfm = s.v1(s.o_xfm, KC * TILE).rearrange("p (kc q) -> p kc q", q=TILE)
        wv_sb = s.v1(s.o_wv, KC * 512)

        if t == 0:
            v_rdy = None
        else:
            v_rdy = s.load_unit(("Wv", 0), s.dsem("wv"), wv_sb, s.sWv.ap())

        if not halo:
            q_rdy = {}

            def issue_q(h):
                sl_ = h % 2
                wq_ = s.v1(s.o_wq + sl_ * D, D)
                dep_ = s.slot.get(("wq", sl_))
                q_rdy[h] = s.load_unit(("Wq", h), s.dsem("wq%d" % sl_), wq_,
                                       s.sWq.ap()[h * P:(h + 1) * P, :], dep=dep_,
                                       queue="sync" if h % 2 == 0 else "scalar")
            issue_q(0)
            issue_q(1)
            for h in range(NH):
                sl_ = h % 2
                wq = s.v1(s.o_wq + sl_ * D, D)
                rdy = q_rdy[h]
                qps = s.pA if h % 2 == 0 else s.pB
                depq = s.slot.get(("qps", h % 2))
                if depq:
                    s.wait("tensor", *depq)
                s.wait("tensor", rdy[0], rdy[1])
                lhs = [wq[:, kc * P:(kc + 1) * P] for kc in range(KC)]
                rhs = [xfm[:, kc, :] for kc in range(KC)]
                def fmm(E, lhs=lhs, rhs=rhs, qps=qps):
                    for kc in range(KC):
                        mm = E.matmul(qps[:, :], lhs[kc], rhs[kc],
                                      start=(kc == 0), stop=(kc == KC - 1))
                    return mm
                petok = s.op("tensor", s.s_pe, fmm)
                s.slot[("wq", sl_)] = (s.s_pe, petok)
                if h + 2 < NH:
                    issue_q(h + 2)
                s.wait("scalar", s.s_pe, petok)
                qdst = s.v1(s.o_q + h * TILE, TILE)
                atok = s.op("scalar", s.s_act, lambda E, d=qdst, a=qps: E.copy(d, a[:, :]))
                s.slot[("qps", h % 2)] = (s.s_act, atok)

        k_rdy = {}

        def issue_k(g):
            sl2 = g % 2
            wk_ = s.v1(s.o_wk + sl2 * D, D)
            dep_ = s.slot.get(("wk", sl2))
            k_rdy[g] = s.load_unit(("Wk", g), s.dsem("wk%d" % sl2), wk_,
                                   s.sWk.ap()[g * P:(g + 1) * P, :], dep=dep_)
        issue_k(0)
        issue_k(1)
        for g in range(NKV):
            sl_ = g % 2
            wk = s.v1(s.o_wk + sl_ * D, D)
            rdy = k_rdy[g]
            kps = s.sc0[:, (g % 2) * 512:(g % 2) * 512 + 512]
            depk = s.slot.get(("kps", g % 2))
            if depk:
                s.wait("tensor", *depk)
            s.wait("tensor", rdy[0], rdy[1])
            lhs = [wk[:, kc * P:(kc + 1) * P] for kc in range(KC)]
            rhs = [xfm[:, kc, :] for kc in range(KC)]
            def fmm(E, lhs=lhs, rhs=rhs, kps=kps):
                for kc in range(KC):
                    mm = E.matmul(kps, lhs[kc], rhs[kc], start=(kc == 0), stop=(kc == KC - 1))
                return mm
            petok = s.op("tensor", s.s_pe, fmm)
            s.slot[("wk", sl_)] = (s.s_pe, petok)
            if g + 2 < NKV:
                issue_k(g + 2)
            s.wait("scalar", s.s_pe, petok)
            kdst = s.k_roll[:, (par * NKV + g) * 512:(par * NKV + g) * 512 + 512]
            atok = s.op("scalar", s.s_act, lambda E, d=kdst, a=kps: E.copy(d, a))
            s.slot[("kps", g % 2)] = (s.s_act, atok)

        if v_rdy:
            s.wait("tensor", v_rdy[0], v_rdy[1])
        for sub in range(NSUB):
            vps = s.sc1[:, (sub % 2) * 512:(sub % 2) * 512 + 512]
            depv = s.slot.get(("vps", sub % 2))
            if depv:
                s.wait("tensor", *depv)
            lhs = [xfm[:, kc, sub * P:(sub + 1) * P] for kc in range(KC)]
            rhs = [wv_sb[:, kc * 512:(kc + 1) * 512] for kc in range(KC)]
            def fmm(E, lhs=lhs, rhs=rhs, vps=vps):
                for kc in range(KC):
                    mm = E.matmul(vps, lhs[kc], rhs[kc], start=(kc == 0), stop=(kc == KC - 1))
                return mm
            petok = s.op("tensor", s.s_pe, fmm)
            s.wait("scalar", s.s_pe, petok)
            vdst = s.v_roll[:, (par * NSUB + sub) * 512:(par * NSUB + sub) * 512 + 512]
            atok = s.op("scalar", s.s_act, lambda E, d=vdst, a=vps: E.copy(d, a))
            s.slot[("vps", sub % 2)] = (s.s_act, atok)
        s.close()

    def stage_A(self, t):
        s = self
        s.open()
        par = t % 2
        ppar = (t - 1) % 2
        mtok = s.dma("sync", s.dsem("mask"), s.v1(s.o_mask, 5 * MW), s.mask.ap())
        s.wait("vector", s.dsem("mask"), mtok)

        def pass1(c, g):
            gc = t * 4 + c
            mslot = gc if gc < 4 else 4
            mask = s.v1(s.o_mask + mslot * MW, MW)
            cgp = (c * NKV + g) % 3
            stt_toks = []
            for hl in range(4):
                h = 4 * g + hl
                scps = s.sc0 if h % 2 == 0 else s.sc1
                dep = s.slot.get(("scps", h % 2))
                if dep:
                    s.wait("tensor", *dep)
                q_l = s.v1(s.o_q + h * TILE + c * P, P)
                w1 = 512 - c * P
                kp = s.k_roll[:, (ppar * NKV + g) * 512 + c * P:(ppar * NKV + g) * 512 + 512]
                kc2 = s.k_roll[:, (par * NKV + g) * 512:(par * NKV + g) * 512 + c * P + P]
                def fsc(E, q_l=q_l, kp=kp, kc2=kc2, scps=scps, c=c, w1=w1):
                    mm = E.matmul(scps[:, 0:w1], q_l, kp, start=True, stop=True)
                    if c == 0:
                        mm = E.matmul(scps[:, 512:640], q_l, kc2, start=True, stop=True)
                    else:
                        mm = E.matmul(scps[:, w1:512], q_l, kc2[:, 0:c * P], start=True, stop=True)
                        mm = E.matmul(scps[:, 512:640], q_l, kc2[:, c * P:c * P + P], start=True, stop=True)
                    return mm
                petok = s.op("tensor", s.s_pe, fsc)
                peb = s.v1(s.o_pe_ + (h % 2) * MW, MW)
                depe = s.slot.get(("peb", h % 2))
                if depe:
                    s.wait("scalar", *depe)
                s.wait("scalar", s.s_pe, petok)
                atok = s.op("scalar", s.s_act,
                            lambda E, d=peb, a=scps: E.activation(d, a[:, 0:MW], AF.Exp,
                                                                  scale=f32(1.0 / 128.0)))
                s.slot[("scps", h % 2)] = (s.s_act, atok)
                pm = s.v1(s.o_pm + cgp * 4 * MW + hl * MW, MW)
                depm = s.slot.get(("pm", cgp, hl))
                if depm:
                    s.wait("vector", *depm)
                s.wait("vector", s.s_act, atok)
                den = s.den4[:, cgp * 4 + hl:cgp * 4 + hl + 1]
                dtok = s.op("vector", s.s_dve,
                            lambda E, d=pm, a=peb, m=mask, acc=den:
                            E.scalar_tensor_tensor(d, a, 1.0, m, AluOpType.mult,
                                                   AluOpType.mult, accum_out=acc))
                s.slot[("peb", h % 2)] = (s.s_dve, dtok)
                stt_toks.append(dtok)
            return dict(c=c, g=g, gc=gc, cgp=cgp, stt=stt_toks)

        def pass2(ctx):
            c, g, gc, cgp = ctx["c"], ctx["g"], ctx["gc"], ctx["cgp"]
            cg2 = (c * NKV + g) % 2
            pT = s.v1(s.o_pT, 2560)
            deppT = s.slot.get(("pT",))
            s.wait("vector", s.s_dve, ctx["stt"][-1])
            rec = s.rec4[:, cgp * 4:cgp * 4 + 4]
            den = s.den4[:, cgp * 4:cgp * 4 + 4]
            rtok = s.op("vector", s.s_dve, lambda E, r=rec, d=den: E.reciprocal(r, d))
            ev_toks = []
            for hl in range(4):
                pm = s.v1(s.o_pm + cgp * 4 * MW + hl * MW, MW)
                s.wait("vector", s.s_dve, rtok)
                rsc = s.rec4[:, cgp * 4 + hl:cgp * 4 + hl + 1]
                ntok = s.op("vector", s.s_dve,
                            lambda E, d=pm, a=pm, r=rsc:
                            E.tensor_scalar(d, a, r, None, AluOpType.mult))
                tps = s.tA if hl % 2 == 0 else s.tB
                dept = s.slot.get(("tpsA", hl % 2))
                if dept:
                    s.wait("tensor", *dept)
                s.wait("tensor", s.s_dve, ntok)
                srcs = [pm[:, j * P:(j + 1) * P] for j in range(5)]
                dsts = [tps[:, j * P:(j + 1) * P] for j in range(5)]
                def ftr(E, srcs=srcs, dsts=dsts):
                    for d_, a_ in zip(dsts, srcs):
                        tr = E.transpose(d_, a_, s.ident_sb[:, :])
                    return tr
                petok = s.op("tensor", s.s_pe, ftr)
                s.slot[("pm", cgp, hl)] = (s.s_pe, petok)
                ev_eng = "scalar" if hl % 2 == 0 else "vector"
                ev_sem = s.s_act if hl % 2 == 0 else s.s_dve
                s.wait(ev_eng, s.s_pe, petok)
                if deppT:
                    s.wait(ev_eng, *deppT)
                dst = pT.rearrange("p (j q) -> p j q", q=512)[:, 0:5, hl * P:(hl + 1) * P]
                src = tps.ap().rearrange("p (j q) -> p j q", q=P)[:, 0:5, :]
                if hl % 2 == 0:
                    etok = s.op("scalar", s.s_act, lambda E, d=dst, a=src: E.copy(d, a))
                else:
                    etok = s.op("vector", s.s_dve, lambda E, d=dst, a=src: E.tensor_copy(d, a))
                s.slot[("tpsA", hl % 2)] = (ev_sem, etok)
                ev_toks.append((ev_sem, etok))
            aps = s.pA if cg2 == 0 else s.pB
            depa = s.slot.get(("aps", cg2))
            if depa:
                s.wait("tensor", *depa)
            for sm_, tk_ in ev_toks[-2:]:
                s.wait("tensor", sm_, tk_)
            vls = []
            for j in range(5):
                kcg = gc - 4 + j
                pj = (kcg // 4) % 2
                sj = kcg % 4
                o = (pj * NSUB + sj) * 512 + g * P
                vls.append(s.v_roll[:, o:o + P])
            pTs = [pT[:, j * 512:(j + 1) * 512] for j in range(5)]
            def fpv(E, vls=vls, pTs=pTs, aps=aps):
                for j in range(5):
                    mm = E.matmul(aps[:, :], vls[j], pTs[j], start=(j == 0), stop=(j == 4))
                return mm
            pvtok = s.op("tensor", s.s_pe, fpv)
            s.slot[("pT",)] = (s.s_pe, pvtok)
            s.wait("vector", s.s_pe, pvtok)
            dst = s.AR2.ap().rearrange("p (h q) -> p h q", q=TILE)[:, 4 * g:4 * g + 4, c * P:(c + 1) * P]
            src = aps.ap().rearrange("p (h q) -> p h q", q=P)[:, 0:4, :]
            dtok = s.op("vector", s.s_dve, lambda E, d=dst, a=src: E.tensor_copy(d, a))
            s.slot[("aps", cg2)] = (s.s_dve, dtok)

        pend = []
        for c in range(NSUB):
            for g in range(NKV):
                pend.append(pass1(c, g))
                if len(pend) > 2:
                    pass2(pend.pop(0))
        for ctx in pend:
            pass2(ctx)
        s.close()

    def emit_ln(self, src, dst):
        s = self
        s.wait("vector", s.s_dve, s.s_dve.n)
        atok = None
        for r in range(4):
            sv = src[:, r * 512:(r + 1) * 512]
            dv = s.stats[:, 6 * r:6 * r + 6]
            atok = s.op("vector", s.s_dve, lambda E, d=dv, a=sv: E.bn_stats(d, a))
        s.wait("vector", s.s_dve, atok)
        gtok = s.op("vector", s.s_dve, lambda E: E.bn_aggr(s.mv[:, :], s.stats[:, :]))
        s.wait("scalar", s.s_dve, gtok)
        ltok = s.op("scalar", s.s_act,
                    lambda E: E.activation(s.lnv[:, :], s.mv[:, 1:2], AF.Ln, bias=f32(1e-5)))
        s.wait("scalar", s.s_act, ltok)
        stok = s.op("scalar", s.s_act,
                    lambda E: E.activation(s.rstd[:, :], s.lnv[:, :], AF.Exp, scale=f32(-0.5)))
        s.wait("vector", s.s_act, stok)
        s.wait("vector", s.s_dve, gtok)
        tok = s.op("vector", s.s_dve,
                   lambda E, a=src, d=dst: E.tensor_scalar(d, a, s.mv[:, 0:1], s.rstd[:, 0:1],
                                                           AluOpType.subtract, AluOpType.mult))
        return tok

    def stage_O(self, t):
        s = self
        s.open()
        attn = s.AR2.ap().rearrange("p (h q) -> p h q", q=TILE)
        x1fm = s.x1_fm.ap().rearrange("p (kc q) -> p kc q", q=TILE)
        xtm = s.x_tm_bf.ap().rearrange("p (u c) -> p u c", c=D)
        r1b = s.v1(s.o_r1b, NSUB * D)
        r1bv = r1b.rearrange("p (u c) -> p u c", c=D)
        wo_rdy = {}

        def issue_wo(oc):
            ring_ = oc % 2
            wo_ = s.v1(s.o_wo + ring_ * D, D)
            dep_ = s.slot.get(("wo", ring_))
            wo_rdy[oc] = s.load_unit(("Wo", oc), s.dsem("wo%d" % ring_), wo_,
                                     s.sWo.ap()[oc * P:(oc + 1) * P, :], dep=dep_,
                                     queue="sync" if ring_ == 0 else "scalar")
        issue_wo(0)
        issue_wo(1)
        for oc in range(KC):
            ring = oc % 2
            wo = s.v1(s.o_wo + ring * D, D)
            rdy = wo_rdy[oc]
            s.wait("tensor", rdy[0], rdy[1])
            dps = s.pA if oc % 2 == 0 else s.pB
            depo = s.slot.get(("dpsO", oc % 2))
            if depo:
                s.wait("tensor", *depo)
            lhs = [wo[:, ic * P:(ic + 1) * P] for ic in range(KC)]
            rhs = [attn[:, ic, :] for ic in range(KC)]
            def fmm(E, lhs=lhs, rhs=rhs, dps=dps):
                for ic in range(KC):
                    mm = E.matmul(dps[:, :], lhs[ic], rhs[ic],
                                  start=(ic == 0), stop=(ic == KC - 1))
                return mm
            petok = s.op("tensor", s.s_pe, fmm)
            s.slot[("wo", ring)] = (s.s_pe, petok)
            if oc + 2 < KC:
                issue_wo(oc + 2)
            wob = s.v1(s.o_silu + (oc % 2) * 512, 512)
            depf = s.slot.get(("wob", oc % 2))
            if depf:
                s.wait("scalar", *depf)
            s.wait("scalar", s.s_pe, petok)
            etok = s.op("scalar", s.s_act, lambda E, d=wob, a=dps: E.copy(d, a[:, :]))
            s.slot[("dpsO", oc % 2)] = (s.s_act, etok)
            tps = s.tA if oc % 2 == 0 else s.tB
            for k_ in (("tpsO", oc % 2), ("tps", oc % 2)):
                d_ = s.slot.get(k_)
                if d_:
                    s.wait("tensor", *d_)
            s.wait("tensor", s.s_act, etok)
            dsts = [tps[:, j * P:(j + 1) * P] for j in range(4)]
            srcs = [wob[:, j * P:(j + 1) * P] for j in range(4)]
            def ftr(E, srcs=srcs, dsts=dsts):
                for d_, a_ in zip(dsts, srcs):
                    tr = E.transpose(d_, a_, s.ident_sb[:, :])
                return tr
            ttok = s.op("tensor", s.s_pe, ftr)
            s.slot[("wob", oc % 2)] = (s.s_pe, ttok)
            s.wait("vector", s.s_pe, ttok)
            dst = r1bv[:, 0:4, oc * P:(oc + 1) * P]
            src = tps.ap().rearrange("p (u q) -> p u q", q=P)[:, 0:4, :]
            xb = xtm[:, 0:4, oc * P:(oc + 1) * P]
            rtok = s.op("vector", s.s_dve,
                        lambda E, d=dst, a=src, b=xb: E.tensor_tensor(d, a, b, AluOpType.add))
            s.slot[("tpsO", oc % 2)] = (s.s_dve, rtok)
        for sub in range(NSUB):
            x1b = s.x1_bf[:, sub * D:(sub + 1) * D]
            aptok = self.emit_ln(r1b[:, sub * D:(sub + 1) * D], x1b)
            for half in range(2):
                tps = s.tA if half == 0 else s.tB
                for k_ in (("tps", half), ("tpsO", half), ("wob", half)):
                    d_ = s.slot.get(k_)
                    if d_:
                        s.wait("tensor", *d_)
                s.wait("tensor", s.s_dve, aptok)
                srcs = [x1b[:, (half * 8 + j) * P:(half * 8 + j + 1) * P] for j in range(8)]
                dsts = [tps[:, j * P:(j + 1) * P] for j in range(8)]
                def ftr(E, srcs=srcs, dsts=dsts):
                    for d_, a_ in zip(dsts, srcs):
                        tr = E.transpose(d_, a_, s.ident_sb[:, :])
                    return tr
                petok = s.op("tensor", s.s_pe, ftr)
                s.wait("vector", s.s_pe, petok)
                dst = x1fm[:, half * 8:(half + 1) * 8, sub * P:(sub + 1) * P]
                src = tps.ap().rearrange("p (j q) -> p j q", q=P)[:, 0:8, :]
                dtok = s.op("vector", s.s_dve, lambda E, d=dst, a=src: E.tensor_copy(d, a))
                s.slot[("tps", half)] = (s.s_dve, dtok)
        s.close()

    def stage_F(self, t, half):
        s = self
        s.open()
        subs = (2 * half, 2 * half + 1)
        x1fm = s.x1_fm.ap().rearrange("p (kc q) -> p kc q", q=TILE)
        hspf = s.v1(s.o_hspf, NIC * 256).rearrange("p (ic q) -> p ic q", q=256)
        def gps_of(sub_l, sl):
            if sub_l == 1:
                return s.sc0[:, 0:512]
            return s.pA[:, :] if sl % 2 == 0 else s.sc1[:, 0:512]

        def ups_of(sub_l, sl):
            if sub_l == 1:
                return s.sc0[:, 512:1024]
            return s.pB[:, :] if sl % 2 == 0 else s.sc1[:, 512:1024]

        sp_pending = []
        sp_state = {"cur": None, "tok": None}

        def emit_hsp_transposes(grp, sub_l, sptok):
            s.wait("tensor", s.s_dve, sptok)
            hspt = s.v1(s.o_hspt + (grp % 2) * D, D)
            for tph in range(2):
                tps = s.tA if tph == 0 else s.tB
                deptp = s.slot.get(("tps", tph))
                if deptp:
                    s.wait("tensor", *deptp)
                srcs = [hspt[:, (tph * 8 + j) * P:(tph * 8 + j + 1) * P] for j in range(8)]
                dsts = [tps[:, j * P:(j + 1) * P] for j in range(8)]
                def ftr(E, srcs=srcs, dsts=dsts):
                    for d_, a_ in zip(dsts, srcs):
                        tr = E.transpose(d_, a_, s.ident_sb[:, :])
                    return tr
                petok = s.op("tensor", s.s_pe, ftr)
                s.wait("scalar", s.s_pe, petok)
                dst = hspf[:, grp * 16 + tph * 8:grp * 16 + (tph + 1) * 8, sub_l * P:(sub_l + 1) * P]
                src = tps.ap().rearrange("p (j q) -> p j q", q=P)[:, 0:8, :]
                etok = s.op("scalar", s.s_act, lambda E, d=dst, a=src: E.copy(d, a))
                s.slot[("tps", tph)] = (s.s_act, etok)
            s.slot[("hspt", grp % 2)] = (s.s_pe, s.s_pe.n)

        def emit_sparsify(k):
            while k > 0:
                if sp_state["cur"] is None:
                    if not sp_pending:
                        return
                    tag, grp, sub_l, builders = sp_pending.pop(0)
                    hw = s.slot.get(("hbw", grp % 2, sub_l))
                    if hw:
                        s.wait("vector", *hw)
                    dep = s.slot.get(("hspt", grp % 2))
                    if dep:
                        s.wait("vector", *dep)
                    sp_state["cur"] = (grp, sub_l, builders, 0)
                grp, sub_l, builders, i = sp_state["cur"]
                if sp_state["tok"]:
                    s.wait("vector", s.s_dve, sp_state["tok"])
                sp_state["tok"] = s.op("vector", s.s_dve, builders[i])
                i += 1
                k -= 1
                if i == len(builders):
                    emit_hsp_transposes(grp, sub_l, sp_state["tok"])
                    sp_state["cur"] = None
                else:
                    sp_state["cur"] = (grp, sub_l, builders, i)

        gu_rdy = {}

        def issue_gu(ci):
            sl_, q4_ = divmod(ci, 4)
            ring_ = ci % 4
            out = {}
            for (nm, o_ring, sW, q_) in (("Wg", s.o_wg, s.sWg, "sync"),
                                         ("Wu", s.o_wu, s.sWu, "scalar")):
                dst_ = s.v1(o_ring + ring_ * D, D)
                dep_ = s.slot.get((nm + "ring", ring_))
                out[nm] = s.load_unit((nm, sl_, q4_), s.dsem(nm + "r%d" % ring_), dst_,
                                      sW.ap()[sl_ * P:(sl_ + 1) * P, q4_ * 4 * 512:(q4_ + 1) * 4 * 512],
                                      dep=dep_, queue=q_)
            gu_rdy[ci] = out
        for ci0 in range(3):
            issue_gu(ci0)
        for sl in range(NSL):
            gtok = utok = None
            for q4 in range(4):
                ci = sl * 4 + q4
                ring = ci % 4
                if ci + 3 < NSL * 4:
                    issue_gu(ci + 3)
                rds = gu_rdy[ci]
                s.wait("tensor", rds["Wg"][0], rds["Wg"][1])
                s.wait("tensor", rds["Wu"][0], rds["Wu"][1])
                wg_c = s.v1(s.o_wg + ring * D, D)
                wu_c = s.v1(s.o_wu + ring * D, D)
                tok = None
                for sub_l in (1, 0):
                    sub = subs[sub_l]
                    if q4 == 0:
                        keys = ([("gps0", sl % 2), ("ups0", sl % 2)] if sub_l == 0
                                else [("gps1",), ("ups1",)])
                        for k_ in keys:
                            d_ = s.slot.get(k_)
                            if d_:
                                s.wait("tensor", *d_)
                    lhs = [x1fm[:, q4 * 4 + j, sub * P:(sub + 1) * P] for j in range(4)]
                    g_ps, u_ps = gps_of(sub_l, sl), ups_of(sub_l, sl)
                    def fmm(E, lhs=lhs, wg_c=wg_c, wu_c=wu_c, g_ps=g_ps, u_ps=u_ps, q4=q4):
                        for j in range(4):
                            st_f = (q4 == 0 and j == 0)
                            sp_f = (q4 == 3 and j == 3)
                            mm = E.matmul(g_ps, lhs[j], wg_c[:, j * 512:(j + 1) * 512],
                                          start=st_f, stop=sp_f)
                            mm2 = E.matmul(u_ps, lhs[j], wu_c[:, j * 512:(j + 1) * 512],
                                           start=st_f, stop=sp_f)
                        return mm2
                    tok = s.op("tensor", s.s_pe, fmm)
                    if q4 == 3:
                        if sub_l == 0:
                            gtok = tok
                        else:
                            utok = tok
                s.slot[("Wgring", ring)] = (s.s_pe, tok)
                s.slot[("Wuring", ring)] = (s.s_pe, tok)
            mm_done = {subs[0]: gtok, subs[1]: utok}
            sl_l = sl % 4
            grp = sl // 4
            for sub_l in (1, 0):
                sub = subs[sub_l]
                silu = s.v1(s.o_silu + sub_l * 512, 512)
                dw = s.slot.get(("silu_w", sub_l))
                if dw:
                    s.wait("scalar", *dw)
                s.wait("scalar", s.s_pe, mm_done[sub])
                g_ps, u_ps = gps_of(sub_l, sl), ups_of(sub_l, sl)
                stok = s.op("scalar", s.s_act, lambda E, d=silu, a=g_ps: E.activation(d, a, AF.Silu))
                s.slot[("gps0", sl % 2) if sub_l == 0 else ("gps1",)] = (s.s_act, stok)
                hb = s.v1(s.o_hbuf + (grp % 2) * 2 * D + sub_l * D + sl_l * 512, 512)
                s.wait("vector", s.s_act, stok)
                s.wait("vector", s.s_pe, mm_done[sub])
                htok = s.op("vector", s.s_dve,
                            lambda E, d=hb, a=u_ps, b=silu: E.tensor_tensor(d, a, b, AluOpType.mult))
                s.slot[("ups0", sl % 2) if sub_l == 0 else ("ups1",)] = (s.s_dve, htok)
                s.slot[("hbw", grp % 2, sub_l)] = (s.s_dve, htok)
                s.slot[("silu_w", sub_l)] = (s.s_dve, htok)
            if sl_l == 3:
                for sub_l in (1, 0):
                    sub = subs[sub_l]
                    hb = s.v1(s.o_hbuf + (grp % 2) * 2 * D + sub_l * D, D)
                    habs = s.v1(s.o_habs, D)
                    hspt = s.v1(s.o_hspt + (grp % 2) * D, D)
                    thr = s.v1(s.o_thr, 512)
                    t1_ = s.v1(s.o_thr + 512, 512)
                    t2_ = s.v1(s.o_thr + 1024, 512)
                    t3_ = s.v1(s.o_thr + 1536, 512)
                    h0, h1, h2, h3 = (habs[:, k:D:4] for k in range(4))
                    builders = [
                        lambda E, habs=habs, hb=hb: E.tensor_tensor(habs, hb, hb, AluOpType.mult),
                        lambda E, t1_=t1_, h0=h0, h1=h1: E.tensor_tensor(t1_, h0, h1, AluOpType.max),
                        lambda E, t2_=t2_, h0=h0, h1=h1: E.tensor_tensor(t2_, h0, h1, AluOpType.min),
                        lambda E, t3_=t3_, h2=h2, h3=h3: E.tensor_tensor(t3_, h2, h3, AluOpType.max),
                        lambda E, thr=thr, h2=h2, h3=h3: E.tensor_tensor(thr, h2, h3, AluOpType.min),
                        lambda E, t1_=t1_, t3_=t3_: E.tensor_tensor(t1_, t1_, t3_, AluOpType.min),
                        lambda E, t2_=t2_, thr=thr: E.tensor_tensor(t2_, t2_, thr, AluOpType.max),
                        lambda E, thr=thr, t1_=t1_, t2_=t2_: E.tensor_tensor(thr, t1_, t2_, AluOpType.max),
                    ]
                    for k in range(4):
                        tt = t1_ if k % 2 == 0 else t2_
                        hak = habs[:, k:D:4]
                        hbk = hb[:, k:D:4]
                        hok = hspt[:, k:D:4]
                        builders.append(lambda E, tt=tt, hak=hak, thr=thr:
                                        E.tensor_tensor(tt, hak, thr, AluOpType.is_ge))
                        builders.append(lambda E, hok=hok, tt=tt, hbk=hbk:
                                        E.tensor_tensor(hok, tt, hbk, AluOpType.mult))
                    sp_pending.append(("start", grp, sub_l, builders))
            emit_sparsify(9)

        while sp_pending or sp_state["cur"] is not None:
            emit_sparsify(8)

        for k_ in (("tps", 0), ("tps", 1), ("gps0", 0), ("gps0", 1), ("ups0", 0),
                   ("ups0", 1), ("gps1",), ("ups1",)):
            d_ = s.slot.get(k_)
            if d_:
                s.wait("tensor", *d_)
        r2 = s.AR1[:, s.o_r2:s.o_r2 + 2 * D]
        wd_rdy = {}

        def issue_wd(idx):
            oc_, hu_ = divmod(idx, 2)
            ring_ = idx % 2
            dep_ = s.slot.get(("wdring", ring_))
            sem_a = s.dsem("wd%da" % ring_)
            sem_b = s.dsem("wd%db" % ring_)
            base = ring_ * 4096
            scr = s.sWd.ap()[oc_ * P:(oc_ + 1) * P, hu_ * 4096:(hu_ + 1) * 4096]
            ta = s.load_unit(("Wd", oc_, hu_, 0), sem_a, s.AR2[:, base:base + 2048],
                             scr[:, 0:2048], dep=dep_, queue="sync")
            tb = s.load_unit(("Wd", oc_, hu_, 1), sem_b, s.AR2[:, base + 2048:base + 4096],
                             scr[:, 2048:4096], dep=dep_, queue="scalar")
            wd_rdy[idx] = (ta, tb)
        issue_wd(0)
        for oc in range(KC):
            dps = s.pA if oc % 2 == 0 else s.pB
            dtok = None
            for hu in range(2):
                idx = oc * 2 + hu
                ring = idx % 2
                wd = s.AR2[:, ring * 4096:(ring + 1) * 4096]
                if idx + 1 < 2 * KC:
                    issue_wd(idx + 1)
                ta, tb = wd_rdy[idx]
                s.wait("tensor", ta[0], ta[1])
                s.wait("tensor", tb[0], tb[1])
                if hu == 0:
                    depo = s.slot.get(("dps", oc % 2))
                    if depo:
                        s.wait("tensor", *depo)
                lhs = [wd[:, k * P:(k + 1) * P] for k in range(32)]
                rhs = [hspf[:, hu * 32 + k, :] for k in range(32)]
                def fdn(E, lhs=lhs, rhs=rhs, dps=dps, hu=hu):
                    for k in range(32):
                        mm = E.matmul(dps[:, 0:256], lhs[k], rhs[k],
                                      start=(hu == 0 and k == 0), stop=(hu == 1 and k == 31))
                    return mm
                dtok = s.op("tensor", s.s_pe, fdn)
                s.slot[("wdring", ring)] = (s.s_pe, dtok)
            dfm = s.v1(s.o_dfm + (oc % 2) * 256, 256)
            s.wait("scalar", s.s_pe, dtok)
            depf = s.slot.get(("dfm", oc % 2))
            if depf:
                s.wait("scalar", *depf)
            etok = s.op("scalar", s.s_act, lambda E, d=dfm, a=dps: E.copy(d, a[:, 0:256]))
            s.slot[("dps", oc % 2)] = (s.s_act, etok)
            tps = s.tA if oc % 2 == 0 else s.tB
            for k_ in (("tpsF3", oc % 2), ("tps", oc % 2)):
                d_ = s.slot.get(k_)
                if d_:
                    s.wait("tensor", *d_)
            s.wait("tensor", s.s_act, etok)
            dsts = [tps[:, j * P:(j + 1) * P] for j in range(2)]
            srcs = [dfm[:, j * P:(j + 1) * P] for j in range(2)]
            def ftr(E, srcs=srcs, dsts=dsts):
                for d_, a_ in zip(dsts, srcs):
                    tr = E.transpose(d_, a_, s.ident_sb[:, :])
                return tr
            petok = s.op("tensor", s.s_pe, ftr)
            s.slot[("dfm", oc % 2)] = (s.s_pe, petok)
            s.wait("vector", s.s_pe, petok)
            dst = r2.rearrange("p (u c) -> p u c", c=D)[:, 0:2, oc * P:(oc + 1) * P]
            src = tps.ap().rearrange("p (u q) -> p u q", q=P)[:, 0:2, :]
            xb = s.x1_bf.ap().rearrange("p (u c) -> p u c", c=D)[:, 2 * half:2 * half + 2, oc * P:(oc + 1) * P]
            rtok = s.op("vector", s.s_dve,
                        lambda E, d=dst, a=src, b=xb: E.tensor_tensor(d, a, b, AluOpType.add))
            s.slot[("tpsF3", oc % 2)] = (s.s_dve, rtok)
        for sub_l, sub in enumerate(subs):
            osb = s.AR3[:, sub_l * D:(sub_l + 1) * D]
            depo = s.slot.get(("osb_v", sub_l))
            if depo:
                s.wait("vector", *depo)
            aptok = self.emit_ln(r2[:, sub_l * D:(sub_l + 1) * D], osb)
            s.wait("sync", s.s_dve, aptok)
            osem = s.dsem("out%d" % sub_l)
            otok = s.dma("sync", osem, s.out.ap()[t * TILE + sub * P:t * TILE + (sub + 1) * P, :],
                         osb)
            s.slot[("osb", sub_l)] = (osem, otok)
            s.slot[("osb_v", sub_l)] = (osem, otok)
        s.close()

    def build(self):
        s = self
        s.open()
        s.dma("sync", s.s_cst, s.ident_sb[:, :], s.ident.ap())
        for e in ("tensor",):
            s.wait(e, s.s_cst, 16)
        s.close()
        s.stage_precast()
        s.stage_X(-1)
        s.stage_QKV(-1)
        for t in range(s.n_tiles):
            s.stage_X(t)
            s.stage_QKV(t)
            s.stage_A(t)
            if s.debug and t == 0:
                s.dbg("xfm", s.v1(s.o_xfm, KC * TILE), KC * TILE, BF16)
                s.dbg("q", s.v1(s.o_q, NH * 512), NH * 512, BF16)
                s.dbg("kroll", s.k_roll[:, :], 2 * NKV * 512, BF16)
                s.dbg("vroll", s.v_roll[:, :], 2 * NSUB * 512, BF16)
                s.dbg("attn", s.AR2[:, :], 8192, BF16)
            s.stage_O(t)
            if s.debug and t == 0:
                s.dbg("x1", s.x1_bf[:, :], NSUB * D, BF16)
                s.dbg("x1fm", s.x1_fm[:, :], KC * TILE, BF16)
            for half in range(2):
                s.stage_F(t, half)
            if s.debug and t == 0:
                s.dbg("hspf", s.v1(s.o_hspf, NIC * 256), NIC * 256, BF16)
        s.open()
        def fin(E):
            for nm, sm in s.dsems.items():
                if nm.endswith("_wb") or nm.startswith("out"):
                    E.wait_ge(sm.h, sm.n)
            E.wait_ge(s.s_dbg.h, s.s_dbg.n)
        s.stage.add("sync", fin)
        s.close()
        return s.nc


def build_masks(seq_start: bool) -> np.ndarray:
    m = np.zeros((5, P, MW), dtype=np.float32)
    r = np.arange(P)[:, None]
    kw = np.arange(MW)[None, :]
    band = (kw >= r + 1) & (kw <= r + WIN)
    for v in range(5):
        lo = (WIN - P * v) if (seq_start and v < 4) else 0
        m[v] = (band & (kw >= lo)).astype(np.float32)
    return np.ascontiguousarray(m.transpose(1, 0, 2).reshape(P, 5 * MW)).astype(BB)


def kernel(L0_attn_hidden_states, L0_attn_Wq, L0_attn_Wk, L0_attn_Wv, L0_attn_Wo,
           L0_ffn_w_gate, L0_ffn_w_up, L0_ffn_w_down):
    x = np.ascontiguousarray(np.asarray(L0_attn_hidden_states, dtype=np.float32))
    B, S, _ = x.shape
    n_cores = 8
    per = (B * S) // n_cores
    mk = MK(n_tiles=NT, debug=False)
    nc = mk.build()
    ident = np.eye(P, dtype=BB)
    w = dict(L0_attn_Wq=np.asarray(L0_attn_Wq, np.float32),
             L0_attn_Wk=np.asarray(L0_attn_Wk, np.float32),
             L0_attn_Wv=np.asarray(L0_attn_Wv, np.float32),
             L0_attn_Wo=np.asarray(L0_attn_Wo, np.float32),
             L0_ffn_w_gate=np.asarray(L0_ffn_w_gate, np.float32),
             L0_ffn_w_up=np.asarray(L0_ffn_w_up, np.float32),
             L0_ffn_w_down=np.asarray(L0_ffn_w_down, np.float32))
    xf = x.reshape(B * S, D)
    in_maps = []
    for c in range(n_cores):
        t0 = c * per
        seq_start = (t0 % S) == 0
        halo = np.zeros((HALO, D), np.float32) if seq_start else xf[t0 - HALO:t0]
        xs = np.ascontiguousarray(np.concatenate([halo, xf[t0:t0 + per]], axis=0))
        im = {"x": xs, "mask01": build_masks(seq_start), "ident": ident}
        im.update(w)
        in_maps.append(im)
    res = run_bass_kernel_spmd(nc, in_maps, core_ids=list(range(n_cores)))
    outs = [res.results[c]["out"] for c in range(n_cores)]
    return np.concatenate(outs, axis=0).reshape(B, S, D).astype(np.float32)


# version 09
# speedup vs baseline: 1.0477x; 1.0237x over previous
import numpy as np
import ml_dtypes
from contextlib import ExitStack

import concourse.bass as bass
import concourse.mybir as mybir
from concourse.alu_op_type import AluOpType
from concourse.bass_utils import run_bass_kernel_spmd

F32 = mybir.dt.float32
BF16 = mybir.dt.bfloat16
AF = mybir.ActivationFunctionType
BB = ml_dtypes.bfloat16

P = 128
D = 2048
KC = 16
NH = 16
NKV = 4
WIN = 512
ISZ = 8192
NIC = 64
TILE = 512
NSUB = 4
HALO = 512
NT = 8
NSL = 16
MW = 640

f32 = np.float32
ENGS = ("sync", "gpsimd", "tensor", "scalar", "vector")


class Sem:
    def __init__(self, nc, es, name, step=1):
        self.h = es.enter_context(nc.semaphore(name))
        self.n = 0
        self.step = step
        self.name = name

    def reserve(self):
        self.n += self.step
        return self.n


class Stage:
    def __init__(self):
        self.lists = {e: [] for e in ENGS}

    def add(self, eng, fn):
        self.lists[eng].append(fn)


class MK:
    def __init__(self, n_tiles=NT, debug=False):
        self.n_tiles = n_tiles
        self.debug = debug
        self.dbg_names = []
        nc = bass.Bass("TRN2", target_bir_lowering=False)
        self.nc = nc
        self.es = ExitStack()
        es = self.es
        ntok = HALO + n_tiles * TILE

        dram = nc.dram_tensor
        self.x = dram("x", [ntok, D], F32, kind="ExternalInput")
        self.Wq = dram("L0_attn_Wq", [D, D], F32, kind="ExternalInput")
        self.Wk = dram("L0_attn_Wk", [D, 512], F32, kind="ExternalInput")
        self.Wv = dram("L0_attn_Wv", [D, 512], F32, kind="ExternalInput")
        self.Wo = dram("L0_attn_Wo", [D, D], F32, kind="ExternalInput")
        self.Wg = dram("L0_ffn_w_gate", [D, ISZ], F32, kind="ExternalInput")
        self.Wu = dram("L0_ffn_w_up", [D, ISZ], F32, kind="ExternalInput")
        self.Wd = dram("L0_ffn_w_down", [ISZ, D], F32, kind="ExternalInput")
        self.mask = dram("mask01", [P, 5 * MW], BF16, kind="ExternalInput")
        self.ident = dram("ident", [P, P], BF16, kind="ExternalInput")
        self.out = dram("out", [n_tiles * TILE, D], F32, kind="ExternalOutput")
        self.sWq = dram("sWq", [D, D], BF16)
        self.sWk = dram("sWk", [512, D], BF16)
        self.sWv = dram("sWv", [P, KC * 512], BF16)
        self.sWo = dram("sWo", [D, D], BF16)
        self.sWg = dram("sWg", [NSL * P, KC * 512], BF16)
        self.sWu = dram("sWu", [NSL * P, KC * 512], BF16)
        self.sWd = dram("sWd", [KC * P, NIC * P], BF16)

        self.s_pe = Sem(nc, es, "s_pe")
        self.s_act = Sem(nc, es, "s_act")
        self.s_dve = Sem(nc, es, "s_dve")
        self.s_pool = Sem(nc, es, "s_pool")
        self.s_cst = Sem(nc, es, "s_cst", 16)
        self.s_dbg = Sem(nc, es, "s_dbg", 16)
        self.dsems = {}
        self.waited = {}
        self.slot_wb = {}
        self.fence_sems = set()

        sb = lambda name, cols, dt: es.enter_context(nc.sbuf_tensor(name, [P, cols], dt))
        self.AR1 = sb("AR1", 50688, BF16)
        self.AR2 = sb("AR2", 8192, BF16)
        self.AR3 = sb("AR3", 4096, F32)
        self.x1_bf = sb("x1_bf", NSUB * D, BF16)
        self.x1_fm = sb("x1_fm", KC * TILE, BF16)
        self.x_tm_bf = sb("x_tm_bf", NSUB * D, BF16)
        self.ident_sb = sb("ident_sb", P, BF16)
        self.k_roll = sb("k_roll", 2 * NKV * 512, BF16)
        self.v_roll = sb("v_roll", 2 * NSUB * 512, BF16)
        self.den4 = sb("den4", 16, F32)
        self.rec4 = sb("rec4", 16, F32)
        self.stats = sb("stats", 24, F32)
        self.mv = sb("mv", 2, F32)
        self.lnv = sb("lnv", 1, F32)
        self.rstd = sb("rstd", 1, F32)

        self.o_xbf = 0
        self.o_xfm = 4096
        self.o_wq = 12288
        self.o_wk = 16384
        self.o_wv = 20480
        self.o_q = 28672
        self.o_pe_ = 36864
        self.o_pm = 38144
        self.o_pT = 45824
        self.o_wo = 0
        self.o_wg = 0
        self.o_wu = 8192
        self.o_hbuf = 16384
        self.o_habs = 30208
        self.o_thr = 32256
        self.o_hspt = 24576
        self.o_silu = 28672
        self.o_dfm = 29696
        self.o_r2 = 30208
        self.o_hspf = 34304
        self.o_mask = 0
        self.o_r1b = 34304

        ps = lambda name, cols, dt: es.enter_context(nc.psum_tensor(name, [P, cols], dt))
        self.sc0 = ps("sc0", 1024, F32)
        self.sc1 = ps("sc1", 1024, F32)
        self.tA = ps("tA", 1024, BF16)
        self.tB = ps("tB", 1024, BF16)
        self.pA = ps("pA", 512, F32)
        self.pB = ps("pB", 512, F32)

        self.wb_tok = {}
        self.slot = {}
        self.stage = None

    def v1(self, off, cols):
        return self.AR1[:, off:off + cols]

    def dsem(self, name):
        if name not in self.dsems:
            self.dsems[name] = Sem(self.nc, self.es, "sd_" + name, 16)
        return self.dsems[name]

    def wait(self, eng, sem, tok):
        if not tok:
            return
        key = (eng, sem.name)
        if self.waited.get(key, 0) >= tok:
            return
        self.waited[key] = tok
        self.stage.add(eng, lambda E, s=sem.h, t=tok: E.wait_ge(s, t))

    def op(self, eng, sem, builder):
        if sem is None:
            self.stage.add(eng, lambda E, b=builder: b(E))
            return None
        tok = sem.reserve()
        self.stage.add(eng, lambda E, b=builder, h=sem.h: b(E).then_inc(h, 1))
        return tok

    def dma(self, eng, sem, dst, src):
        tok = sem.reserve()
        self.stage.add(eng, lambda E, d=dst, a=src, h=sem.h: E.dma_start(d, a).then_inc(h, 16))
        return tok

    def open(self):
        self.stage = Stage()

    def close(self):
        st = self.stage
        for sem in sorted(self.fence_sems, key=lambda x: x.name):
            st.add("sync", lambda E, h=sem.h, n=sem.n: E.wait_ge(h, n))
            self.waited[("sync", sem.name)] = sem.n
        with self.nc.Block() as block:
            for e in ENGS:
                fns = st.lists[e]
                if fns:
                    def runner(E, fns=fns):
                        for f in fns:
                            f(E)
                    getattr(block, e)(runner)
        for sem in (self.s_pe, self.s_act, self.s_dve, self.s_pool):
            for e in ENGS:
                key = (e, sem.name)
                self.waited[key] = max(self.waited.get(key, 0), sem.n)
        names = {k[1] for k in self.waited}
        for nm in names:
            m = max(self.waited.get((e, nm), 0) for e in ENGS)
            for e in ENGS:
                self.waited[(e, nm)] = m
        self.fence_sems = set()
        self.stage = None

    def load_unit(self, key, slot_sem, dst, scratch, dep=None, queue="sync"):
        s = self
        eng = "sync" if queue == "sync" else "scalar"
        if dep:
            s.wait(eng, dep[0], dep[1])
        tok = s.dma(eng, slot_sem, dst, scratch)
        return (slot_sem, tok)

    def stage_precast(self):
        s = self
        s.open()
        ldsem = [s.dsem("pcl0"), s.dsem("pcl1")]
        wbsem = [s.dsem("pc0_wb"), s.dsem("pc1_wb")]
        state = {"cast": [None, None], "i": 0, "asm_rd": None}
        o_asm = 16384

        def cast_chunk(gat, scr=None, asm_dst=None, asm_src_view=None):
            i = state["i"]; state["i"] += 1
            sl_ = i % 2
            stg = s.AR3[:, sl_ * 2048:(sl_ + 1) * 2048]
            bfs = s.v1(s.o_xbf + sl_ * D, D)
            if state["cast"][sl_]:
                s.wait("sync", s.s_dve, state["cast"][sl_])
            ltok = s.dma("sync", ldsem[sl_], stg[:, 0:gat.shape[-1] if False else 2048] if False else stg, gat)
            s.wait("vector", ldsem[sl_], ltok)
            s.wait("vector", wbsem[sl_], wbsem[sl_].n)
            ctok = s.op("vector", s.s_dve, lambda E, d=bfs, a=stg: E.tensor_copy(d, a))
            state["cast"][sl_] = ctok
            if scr is not None:
                s.wait("scalar", s.s_dve, ctok)
                s.dma("scalar", wbsem[sl_], scr, bfs)
            else:
                s.wait("vector", s.s_dve, ctok)
                ctok2 = s.op("vector", s.s_dve,
                             lambda E, d=asm_dst, a=asm_src_view(bfs): E.tensor_copy(d, a))
                state["cast"][sl_] = ctok2
                return ctok2
            return ctok

        asem = s.dsem("pca_wb")

        def flush_asm(wb_list):
            s.wait("scalar", s.s_dve, s.s_dve.n)
            for scr, lo, cols in wb_list:
                s.dma("scalar", asem, scr, s.v1(o_asm + lo, cols))
            state["asm_rd"] = asem.n

        def asm_guard():
            if state["asm_rd"]:
                s.wait("vector", asem, state["asm_rd"])
                state["asm_rd"] = None

        A = s.v1(o_asm, 32768).rearrange("p (u c) -> p u c", c=1)

        for kc in range(KC):
            asm_guard()
            gat = s.Wq.ap()[kc * P:(kc + 1) * P, :]
            dst = s.v1(o_asm, NH * D).rearrange("p (h m) -> p h m", m=P)[:, :, :]
            dstv = s.v1(o_asm + kc * P, 1)
            dst2 = bass.AP(s.AR1, o_asm + kc * P, [[50688, P], [D, NH], [1, P]])
            src_view = lambda bfs: bfs.rearrange("p (h m) -> p h m", m=P)
            cast_chunk(gat, asm_dst=dst2, asm_src_view=src_view)
        flush_asm([(s.sWq.ap()[h * P:(h + 1) * P, :], h * D, D) for h in range(NH)])

        for kc in range(KC):
            asm_guard()
            gat = s.Wk.ap()[kc * P:(kc + 1) * P, :]
            dst2 = bass.AP(s.AR1, o_asm + kc * P, [[50688, P], [D, NKV], [1, P]])
            i = state["i"]; state["i"] += 1
            sl_ = i % 2
            stg = s.AR3[:, sl_ * 2048:sl_ * 2048 + 512]
            bfs = s.v1(s.o_xbf + sl_ * D, 512)
            if state["cast"][sl_]:
                s.wait("sync", s.s_dve, state["cast"][sl_])
            ltok = s.dma("sync", ldsem[sl_], stg, gat)
            s.wait("vector", ldsem[sl_], ltok)
            ctok = s.op("vector", s.s_dve, lambda E, d=bfs, a=stg: E.tensor_copy(d, a))
            s.wait("vector", s.s_dve, ctok)
            sv = bfs.rearrange("p (g m) -> p g m", m=P)
            ctok2 = s.op("vector", s.s_dve, lambda E, d=dst2, a=sv: E.tensor_copy(d, a))
            state["cast"][sl_] = ctok2
        flush_asm([(s.sWk.ap()[g * P:(g + 1) * P, :], g * D, D) for g in range(NKV)])

        for kc in range(KC):
            asm_guard()
            i = state["i"]; state["i"] += 1
            sl_ = i % 2
            stg = s.AR3[:, sl_ * 2048:sl_ * 2048 + 512]
            if state["cast"][sl_]:
                s.wait("sync", s.s_dve, state["cast"][sl_])
            ltok = s.dma("sync", ldsem[sl_], stg, s.Wv.ap()[kc * P:(kc + 1) * P, :])
            s.wait("vector", ldsem[sl_], ltok)
            dd = s.v1(o_asm + kc * 512, 512)
            ctok = s.op("vector", s.s_dve, lambda E, d=dd, a=stg: E.tensor_copy(d, a))
            state["cast"][sl_] = ctok
        flush_asm([(s.sWv.ap(), 0, KC * 512)])

        for ic in range(KC):
            asm_guard()
            gat = s.Wo.ap()[ic * P:(ic + 1) * P, :]
            dst2 = bass.AP(s.AR1, o_asm + ic * P, [[50688, P], [D, KC], [1, P]])
            src_view = lambda bfs: bfs.rearrange("p (oc m) -> p oc m", m=P)
            cast_chunk(gat, asm_dst=dst2, asm_src_view=src_view)
        flush_asm([(s.sWo.ap()[oc * P:(oc + 1) * P, :], oc * D, D) for oc in range(KC)])

        for qp in range(4):
            for icl in range(KC):
                asm_guard()
                ic = qp * KC + icl
                gat = s.Wd.ap()[ic * P:(ic + 1) * P, :]
                dst2 = bass.AP(s.AR1, o_asm + icl * P, [[50688, P], [D, KC], [1, P]])
                src_view = lambda bfs: bfs.rearrange("p (oc m) -> p oc m", m=P)
                cast_chunk(gat, asm_dst=dst2, asm_src_view=src_view)
            flush_asm([(s.sWd.ap()[oc * P:(oc + 1) * P, qp * 2048:(qp + 1) * 2048], oc * D, D)
                       for oc in range(KC)])

        wgr = s.Wg.ap().rearrange("(kc p) i -> p kc i", p=P)
        wur = s.Wu.ap().rearrange("(kc p) i -> p kc i", p=P)
        for (Wr, sW) in ((wgr, s.sWg), (wur, s.sWu)):
            for sl in range(NSL):
                for q in range(4):
                    cast_chunk(Wr[:, q * 4:(q + 1) * 4, sl * 512:(sl + 1) * 512],
                               scr=sW.ap()[sl * P:(sl + 1) * P, q * 2048:(q + 1) * 2048])
        s.fence_sems.add(wbsem[0])
        s.fence_sems.add(wbsem[1])
        s.fence_sems.add(asem)
        s.close()

    def dbg(self, name, src_ap, cols, dt):
        if not self.debug:
            return
        t = self.nc.dram_tensor("dbg_" + name, [P, cols], dt, kind="ExternalOutput")
        self.dbg_names.append("dbg_" + name)
        self.open()
        self.dma("sync", self.s_dbg, t.ap(), src_ap)
        self.close()

    def stage_X(self, t):
        s = self
        s.open()
        row0 = 0 if t == -1 else HALO + t * TILE
        xfm = s.v1(s.o_xfm, KC * TILE).rearrange("p (kc q) -> p kc q", q=TILE)
        x_tok = {}

        def issue_x(sub):
            sl2 = sub % 2
            stg_ = s.AR3[:, sl2 * 2048:(sl2 + 1) * 2048]
            depc = s.slot.get(("xstg", sl2))
            if depc:
                s.wait("sync", *depc)
            dob = s.slot.get(("osb", sl2))
            if dob:
                s.wait("sync", *dob)
            x_tok[sub] = s.dma("sync", s.dsem("xb%d" % sl2), stg_,
                               s.x.ap()[row0 + sub * P:row0 + (sub + 1) * P, :])
        issue_x(0)
        issue_x(1)
        for sub in range(NSUB):
            sl_ = sub % 2
            xbf = s.x_tm_bf[:, sub * D:(sub + 1) * D]
            stg = s.AR3[:, sl_ * 2048:(sl_ + 1) * 2048]
            xsem = s.dsem("xb%d" % sl_)
            ltok = x_tok[sub]
            s.wait("vector", xsem, ltok)
            ctok = s.op("vector", s.s_dve, lambda E, d=xbf, a=stg: E.tensor_copy(d, a))
            s.slot[("xstg", sl_)] = (s.s_dve, ctok)
            if sub + 2 < NSUB:
                issue_x(sub + 2)
            s.wait("tensor", s.s_dve, ctok)
            last_pe = None
            for half in range(2):
                tps = s.tA if half == 0 else s.tB
                dep2 = s.slot.get(("tps", half))
                if dep2:
                    s.wait("tensor", *dep2)
                srcs = [xbf[:, (half * 8 + j) * P:(half * 8 + j + 1) * P] for j in range(8)]
                dsts = [tps[:, j * P:(j + 1) * P] for j in range(8)]
                def ftr(E, srcs=srcs, dsts=dsts):
                    for d_, a_ in zip(dsts, srcs):
                        tr = E.transpose(d_, a_, s.ident_sb[:, :])
                    return tr
                petok = s.op("tensor", s.s_pe, ftr)
                last_pe = petok
                s.wait("vector", s.s_pe, petok)
                dst = xfm[:, half * 8:(half + 1) * 8, sub * P:(sub + 1) * P]
                src = tps.ap().rearrange("p (j q) -> p j q", q=P)[:, 0:8, :]
                dtok = s.op("vector", s.s_dve, lambda E, d=dst, a=src: E.tensor_copy(d, a))
                s.slot[("tps", half)] = (s.s_dve, dtok)
        s.close()

    def stage_QKV(self, t):
        s = self
        s.open()
        halo = (t == -1)
        par = t % 2
        xfm = s.v1(s.o_xfm, KC * TILE).rearrange("p (kc q) -> p kc q", q=TILE)
        wv_sb = s.v1(s.o_wv, KC * 512)

        if t == 0:
            v_rdy = None
        else:
            v_rdy = s.load_unit(("Wv", 0), s.dsem("wv"), wv_sb, s.sWv.ap())

        if not halo:
            q_rdy = {}

            def issue_q(h):
                sl_ = h % 2
                wq_ = s.v1(s.o_wq + sl_ * D, D)
                dep_ = s.slot.get(("wq", sl_))
                q_rdy[h] = s.load_unit(("Wq", h), s.dsem("wq%d" % sl_), wq_,
                                       s.sWq.ap()[h * P:(h + 1) * P, :], dep=dep_,
                                       queue="sync" if h % 2 == 0 else "scalar")
            issue_q(0)
            issue_q(1)
            for h in range(NH):
                sl_ = h % 2
                wq = s.v1(s.o_wq + sl_ * D, D)
                rdy = q_rdy[h]
                qps = s.pA if h % 2 == 0 else s.pB
                depq = s.slot.get(("qps", h % 2))
                if depq:
                    s.wait("tensor", *depq)
                s.wait("tensor", rdy[0], rdy[1])
                lhs = [wq[:, kc * P:(kc + 1) * P] for kc in range(KC)]
                rhs = [xfm[:, kc, :] for kc in range(KC)]
                def fmm(E, lhs=lhs, rhs=rhs, qps=qps):
                    for kc in range(KC):
                        mm = E.matmul(qps[:, :], lhs[kc], rhs[kc],
                                      start=(kc == 0), stop=(kc == KC - 1))
                    return mm
                petok = s.op("tensor", s.s_pe, fmm)
                s.slot[("wq", sl_)] = (s.s_pe, petok)
                if h + 2 < NH:
                    issue_q(h + 2)
                s.wait("scalar", s.s_pe, petok)
                qdst = s.v1(s.o_q + h * TILE, TILE)
                atok = s.op("scalar", s.s_act, lambda E, d=qdst, a=qps: E.copy(d, a[:, :]))
                s.slot[("qps", h % 2)] = (s.s_act, atok)

        k_rdy = {}

        def issue_k(g):
            sl2 = g % 2
            wk_ = s.v1(s.o_wk + sl2 * D, D)
            dep_ = s.slot.get(("wk", sl2))
            k_rdy[g] = s.load_unit(("Wk", g), s.dsem("wk%d" % sl2), wk_,
                                   s.sWk.ap()[g * P:(g + 1) * P, :], dep=dep_)
        issue_k(0)
        issue_k(1)
        for g in range(NKV):
            sl_ = g % 2
            wk = s.v1(s.o_wk + sl_ * D, D)
            rdy = k_rdy[g]
            kps = s.sc0[:, (g % 2) * 512:(g % 2) * 512 + 512]
            depk = s.slot.get(("kps", g % 2))
            if depk:
                s.wait("tensor", *depk)
            s.wait("tensor", rdy[0], rdy[1])
            lhs = [wk[:, kc * P:(kc + 1) * P] for kc in range(KC)]
            rhs = [xfm[:, kc, :] for kc in range(KC)]
            def fmm(E, lhs=lhs, rhs=rhs, kps=kps):
                for kc in range(KC):
                    mm = E.matmul(kps, lhs[kc], rhs[kc], start=(kc == 0), stop=(kc == KC - 1))
                return mm
            petok = s.op("tensor", s.s_pe, fmm)
            s.slot[("wk", sl_)] = (s.s_pe, petok)
            if g + 2 < NKV:
                issue_k(g + 2)
            s.wait("scalar", s.s_pe, petok)
            kdst = s.k_roll[:, (par * NKV + g) * 512:(par * NKV + g) * 512 + 512]
            atok = s.op("scalar", s.s_act, lambda E, d=kdst, a=kps: E.copy(d, a))
            s.slot[("kps", g % 2)] = (s.s_act, atok)

        if v_rdy:
            s.wait("tensor", v_rdy[0], v_rdy[1])
        for sub in range(NSUB):
            vps = s.sc1[:, (sub % 2) * 512:(sub % 2) * 512 + 512]
            depv = s.slot.get(("vps", sub % 2))
            if depv:
                s.wait("tensor", *depv)
            lhs = [xfm[:, kc, sub * P:(sub + 1) * P] for kc in range(KC)]
            rhs = [wv_sb[:, kc * 512:(kc + 1) * 512] for kc in range(KC)]
            def fmm(E, lhs=lhs, rhs=rhs, vps=vps):
                for kc in range(KC):
                    mm = E.matmul(vps, lhs[kc], rhs[kc], start=(kc == 0), stop=(kc == KC - 1))
                return mm
            petok = s.op("tensor", s.s_pe, fmm)
            s.wait("scalar", s.s_pe, petok)
            vdst = s.v_roll[:, (par * NSUB + sub) * 512:(par * NSUB + sub) * 512 + 512]
            atok = s.op("scalar", s.s_act, lambda E, d=vdst, a=vps: E.copy(d, a))
            s.slot[("vps", sub % 2)] = (s.s_act, atok)
        s.close()

    def stage_A(self, t):
        s = self
        s.open()
        par = t % 2
        ppar = (t - 1) % 2
        mtok = s.dma("sync", s.dsem("mask"), s.v1(s.o_mask, 5 * MW), s.mask.ap())
        s.wait("vector", s.dsem("mask"), mtok)

        def pass1(c, g):
            gc = t * 4 + c
            mslot = gc if gc < 4 else 4
            mask = s.v1(s.o_mask + mslot * MW, MW)
            cgp = (c * NKV + g) % 3
            stt_toks = []
            for hl in range(4):
                h = 4 * g + hl
                scps = s.sc0 if h % 2 == 0 else s.sc1
                dep = s.slot.get(("scps", h % 2))
                if dep:
                    s.wait("tensor", *dep)
                q_l = s.v1(s.o_q + h * TILE + c * P, P)
                w1 = 512 - c * P
                kp = s.k_roll[:, (ppar * NKV + g) * 512 + c * P:(ppar * NKV + g) * 512 + 512]
                kc2 = s.k_roll[:, (par * NKV + g) * 512:(par * NKV + g) * 512 + c * P + P]
                def fsc(E, q_l=q_l, kp=kp, kc2=kc2, scps=scps, c=c, w1=w1):
                    mm = E.matmul(scps[:, 0:w1], q_l, kp, start=True, stop=True)
                    if c == 0:
                        mm = E.matmul(scps[:, 512:640], q_l, kc2, start=True, stop=True)
                    else:
                        mm = E.matmul(scps[:, w1:512], q_l, kc2[:, 0:c * P], start=True, stop=True)
                        mm = E.matmul(scps[:, 512:640], q_l, kc2[:, c * P:c * P + P], start=True, stop=True)
                    return mm
                petok = s.op("tensor", s.s_pe, fsc)
                peb = s.v1(s.o_pe_ + (h % 2) * MW, MW)
                depe = s.slot.get(("peb", h % 2))
                if depe:
                    s.wait("scalar", *depe)
                s.wait("scalar", s.s_pe, petok)
                atok = s.op("scalar", s.s_act,
                            lambda E, d=peb, a=scps: E.activation(d, a[:, 0:MW], AF.Exp,
                                                                  scale=f32(1.0 / 128.0)))
                s.slot[("scps", h % 2)] = (s.s_act, atok)
                pm = s.v1(s.o_pm + cgp * 4 * MW + hl * MW, MW)
                depm = s.slot.get(("pm", cgp, hl))
                if depm:
                    s.wait("vector", *depm)
                s.wait("vector", s.s_act, atok)
                den = s.den4[:, cgp * 4 + hl:cgp * 4 + hl + 1]
                dtok = s.op("vector", s.s_dve,
                            lambda E, d=pm, a=peb, m=mask, acc=den:
                            E.scalar_tensor_tensor(d, a, 1.0, m, AluOpType.mult,
                                                   AluOpType.mult, accum_out=acc))
                s.slot[("peb", h % 2)] = (s.s_dve, dtok)
                stt_toks.append(dtok)
            return dict(c=c, g=g, gc=gc, cgp=cgp, stt=stt_toks)

        def pass2(ctx):
            c, g, gc, cgp = ctx["c"], ctx["g"], ctx["gc"], ctx["cgp"]
            cg2 = (c * NKV + g) % 2
            pT = s.v1(s.o_pT, 2560)
            deppT = s.slot.get(("pT",))
            s.wait("vector", s.s_dve, ctx["stt"][-1])
            rec = s.rec4[:, cgp * 4:cgp * 4 + 4]
            den = s.den4[:, cgp * 4:cgp * 4 + 4]
            rtok = s.op("vector", s.s_dve, lambda E, r=rec, d=den: E.reciprocal(r, d))
            ev_toks = []
            for hl in range(4):
                pm = s.v1(s.o_pm + cgp * 4 * MW + hl * MW, MW)
                s.wait("vector", s.s_dve, rtok)
                rsc = s.rec4[:, cgp * 4 + hl:cgp * 4 + hl + 1]
                ntok = s.op("vector", s.s_dve,
                            lambda E, d=pm, a=pm, r=rsc:
                            E.tensor_scalar(d, a, r, None, AluOpType.mult))
                tps = s.tA if hl % 2 == 0 else s.tB
                dept = s.slot.get(("tpsA", hl % 2))
                if dept:
                    s.wait("tensor", *dept)
                s.wait("tensor", s.s_dve, ntok)
                srcs = [pm[:, j * P:(j + 1) * P] for j in range(5)]
                dsts = [tps[:, j * P:(j + 1) * P] for j in range(5)]
                def ftr(E, srcs=srcs, dsts=dsts):
                    for d_, a_ in zip(dsts, srcs):
                        tr = E.transpose(d_, a_, s.ident_sb[:, :])
                    return tr
                petok = s.op("tensor", s.s_pe, ftr)
                s.slot[("pm", cgp, hl)] = (s.s_pe, petok)
                ev_eng = "scalar" if hl % 2 == 0 else "vector"
                ev_sem = s.s_act if hl % 2 == 0 else s.s_dve
                s.wait(ev_eng, s.s_pe, petok)
                if deppT:
                    s.wait(ev_eng, *deppT)
                dst = pT.rearrange("p (j q) -> p j q", q=512)[:, 0:5, hl * P:(hl + 1) * P]
                src = tps.ap().rearrange("p (j q) -> p j q", q=P)[:, 0:5, :]
                if hl % 2 == 0:
                    etok = s.op("scalar", s.s_act, lambda E, d=dst, a=src: E.copy(d, a))
                else:
                    etok = s.op("vector", s.s_dve, lambda E, d=dst, a=src: E.tensor_copy(d, a))
                s.slot[("tpsA", hl % 2)] = (ev_sem, etok)
                ev_toks.append((ev_sem, etok))
            aps = s.pA if cg2 == 0 else s.pB
            depa = s.slot.get(("aps", cg2))
            if depa:
                s.wait("tensor", *depa)
            for sm_, tk_ in ev_toks[-2:]:
                s.wait("tensor", sm_, tk_)
            vls = []
            for j in range(5):
                kcg = gc - 4 + j
                pj = (kcg // 4) % 2
                sj = kcg % 4
                o = (pj * NSUB + sj) * 512 + g * P
                vls.append(s.v_roll[:, o:o + P])
            pTs = [pT[:, j * 512:(j + 1) * 512] for j in range(5)]
            def fpv(E, vls=vls, pTs=pTs, aps=aps):
                for j in range(5):
                    mm = E.matmul(aps[:, :], vls[j], pTs[j], start=(j == 0), stop=(j == 4))
                return mm
            pvtok = s.op("tensor", s.s_pe, fpv)
            s.slot[("pT",)] = (s.s_pe, pvtok)
            s.wait("vector", s.s_pe, pvtok)
            dst = s.AR2.ap().rearrange("p (h q) -> p h q", q=TILE)[:, 4 * g:4 * g + 4, c * P:(c + 1) * P]
            src = aps.ap().rearrange("p (h q) -> p h q", q=P)[:, 0:4, :]
            dtok = s.op("vector", s.s_dve, lambda E, d=dst, a=src: E.tensor_copy(d, a))
            s.slot[("aps", cg2)] = (s.s_dve, dtok)

        pend = []
        for c in range(NSUB):
            for g in range(NKV):
                pend.append(pass1(c, g))
                if len(pend) > 2:
                    pass2(pend.pop(0))
        for ctx in pend:
            pass2(ctx)
        s.close()

    def emit_ln(self, src, dst):
        s = self
        s.wait("vector", s.s_dve, s.s_dve.n)
        atok = None
        for r in range(4):
            sv = src[:, r * 512:(r + 1) * 512]
            dv = s.stats[:, 6 * r:6 * r + 6]
            atok = s.op("vector", s.s_dve, lambda E, d=dv, a=sv: E.bn_stats(d, a))
        s.wait("vector", s.s_dve, atok)
        gtok = s.op("vector", s.s_dve, lambda E: E.bn_aggr(s.mv[:, :], s.stats[:, :]))
        s.wait("scalar", s.s_dve, gtok)
        ltok = s.op("scalar", s.s_act,
                    lambda E: E.activation(s.lnv[:, :], s.mv[:, 1:2], AF.Ln, bias=f32(1e-5)))
        s.wait("scalar", s.s_act, ltok)
        stok = s.op("scalar", s.s_act,
                    lambda E: E.activation(s.rstd[:, :], s.lnv[:, :], AF.Exp, scale=f32(-0.5)))
        s.wait("vector", s.s_act, stok)
        s.wait("vector", s.s_dve, gtok)
        tok = s.op("vector", s.s_dve,
                   lambda E, a=src, d=dst: E.tensor_scalar(d, a, s.mv[:, 0:1], s.rstd[:, 0:1],
                                                           AluOpType.subtract, AluOpType.mult))
        return tok

    def stage_O(self, t):
        s = self
        s.open()
        attn = s.AR2.ap().rearrange("p (h q) -> p h q", q=TILE)
        x1fm = s.x1_fm.ap().rearrange("p (kc q) -> p kc q", q=TILE)
        xtm = s.x_tm_bf.ap().rearrange("p (u c) -> p u c", c=D)
        r1b = s.v1(s.o_r1b, NSUB * D)
        r1bv = r1b.rearrange("p (u c) -> p u c", c=D)
        wo_rdy = {}

        def issue_wo(oc):
            ring_ = oc % 2
            wo_ = s.v1(s.o_wo + ring_ * D, D)
            dep_ = s.slot.get(("wo", ring_))
            wo_rdy[oc] = s.load_unit(("Wo", oc), s.dsem("wo%d" % ring_), wo_,
                                     s.sWo.ap()[oc * P:(oc + 1) * P, :], dep=dep_,
                                     queue="sync" if ring_ == 0 else "scalar")
        issue_wo(0)
        issue_wo(1)
        for oc in range(KC):
            ring = oc % 2
            wo = s.v1(s.o_wo + ring * D, D)
            rdy = wo_rdy[oc]
            s.wait("tensor", rdy[0], rdy[1])
            dps = s.pA if oc % 2 == 0 else s.pB
            depo = s.slot.get(("dpsO", oc % 2))
            if depo:
                s.wait("tensor", *depo)
            lhs = [wo[:, ic * P:(ic + 1) * P] for ic in range(KC)]
            rhs = [attn[:, ic, :] for ic in range(KC)]
            def fmm(E, lhs=lhs, rhs=rhs, dps=dps):
                for ic in range(KC):
                    mm = E.matmul(dps[:, :], lhs[ic], rhs[ic],
                                  start=(ic == 0), stop=(ic == KC - 1))
                return mm
            petok = s.op("tensor", s.s_pe, fmm)
            s.slot[("wo", ring)] = (s.s_pe, petok)
            if oc + 2 < KC:
                issue_wo(oc + 2)
            wob = s.v1(s.o_silu + (oc % 2) * 512, 512)
            depf = s.slot.get(("wob", oc % 2))
            if depf:
                s.wait("scalar", *depf)
            s.wait("scalar", s.s_pe, petok)
            etok = s.op("scalar", s.s_act, lambda E, d=wob, a=dps: E.copy(d, a[:, :]))
            s.slot[("dpsO", oc % 2)] = (s.s_act, etok)
            tps = s.tA if oc % 2 == 0 else s.tB
            for k_ in (("tpsO", oc % 2), ("tps", oc % 2)):
                d_ = s.slot.get(k_)
                if d_:
                    s.wait("tensor", *d_)
            s.wait("tensor", s.s_act, etok)
            dsts = [tps[:, j * P:(j + 1) * P] for j in range(4)]
            srcs = [wob[:, j * P:(j + 1) * P] for j in range(4)]
            def ftr(E, srcs=srcs, dsts=dsts):
                for d_, a_ in zip(dsts, srcs):
                    tr = E.transpose(d_, a_, s.ident_sb[:, :])
                return tr
            ttok = s.op("tensor", s.s_pe, ftr)
            s.slot[("wob", oc % 2)] = (s.s_pe, ttok)
            s.wait("vector", s.s_pe, ttok)
            dst = r1bv[:, 0:4, oc * P:(oc + 1) * P]
            src = tps.ap().rearrange("p (u q) -> p u q", q=P)[:, 0:4, :]
            xb = xtm[:, 0:4, oc * P:(oc + 1) * P]
            rtok = s.op("vector", s.s_dve,
                        lambda E, d=dst, a=src, b=xb: E.tensor_tensor(d, a, b, AluOpType.add))
            s.slot[("tpsO", oc % 2)] = (s.s_dve, rtok)
        for sub in range(NSUB):
            x1b = s.x1_bf[:, sub * D:(sub + 1) * D]
            aptok = self.emit_ln(r1b[:, sub * D:(sub + 1) * D], x1b)
            for half in range(2):
                tps = s.tA if half == 0 else s.tB
                for k_ in (("tps", half), ("tpsO", half), ("wob", half)):
                    d_ = s.slot.get(k_)
                    if d_:
                        s.wait("tensor", *d_)
                s.wait("tensor", s.s_dve, aptok)
                srcs = [x1b[:, (half * 8 + j) * P:(half * 8 + j + 1) * P] for j in range(8)]
                dsts = [tps[:, j * P:(j + 1) * P] for j in range(8)]
                def ftr(E, srcs=srcs, dsts=dsts):
                    for d_, a_ in zip(dsts, srcs):
                        tr = E.transpose(d_, a_, s.ident_sb[:, :])
                    return tr
                petok = s.op("tensor", s.s_pe, ftr)
                s.wait("vector", s.s_pe, petok)
                dst = x1fm[:, half * 8:(half + 1) * 8, sub * P:(sub + 1) * P]
                src = tps.ap().rearrange("p (j q) -> p j q", q=P)[:, 0:8, :]
                dtok = s.op("vector", s.s_dve, lambda E, d=dst, a=src: E.tensor_copy(d, a))
                s.slot[("tps", half)] = (s.s_dve, dtok)
        s.close()

    def stage_F(self, t, half):
        s = self
        s.open()
        subs = (2 * half, 2 * half + 1)
        x1fm = s.x1_fm.ap().rearrange("p (kc q) -> p kc q", q=TILE)
        hspf = s.v1(s.o_hspf, NIC * 256).rearrange("p (ic q) -> p ic q", q=256)
        def gps_of(sub_l, sl):
            if sub_l == 1:
                return s.sc0[:, 0:512]
            return s.pA[:, :] if sl % 2 == 0 else s.sc1[:, 0:512]

        def ups_of(sub_l, sl):
            if sub_l == 1:
                return s.sc0[:, 512:1024]
            return s.pB[:, :] if sl % 2 == 0 else s.sc1[:, 512:1024]

        wd_rdy = {}

        def issue_wd(idx):
            oc_, hu_ = divmod(idx, 2)
            ring_ = idx % 2
            dep_ = s.slot.get(("wdring", ring_))
            sem_a = s.dsem("wd%da" % ring_)
            sem_b = s.dsem("wd%db" % ring_)
            base = ring_ * 4096
            scr = s.sWd.ap()[oc_ * P:(oc_ + 1) * P, hu_ * 4096:(hu_ + 1) * 4096]
            ta = s.load_unit(("Wd", oc_, hu_, 0), sem_a, s.AR2[:, base:base + 2048],
                             scr[:, 0:2048], dep=dep_, queue="sync")
            tb = s.load_unit(("Wd", oc_, hu_, 1), sem_b, s.AR2[:, base + 2048:base + 4096],
                             scr[:, 2048:4096], dep=dep_, queue="scalar")
            wd_rdy[idx] = (ta, tb)
        issue_wd(0)
        issue_wd(1)

        sp_pending = []
        sp_state = {"cur": None, "tok": None, "ci": 0}
        tp_queue = []

        def flush_tp():
            while tp_queue:
                emit_hsp_transposes(*tp_queue.pop(0))

        def emit_hsp_transposes(grp, sub_l, hslot, sptok):
            s.wait("tensor", s.s_dve, sptok)
            hspt = s.v1(s.o_hspt + hslot * D, D)
            for tph in range(2):
                tps = s.tA if tph == 0 else s.tB
                deptp = s.slot.get(("tps", tph))
                if deptp:
                    s.wait("tensor", *deptp)
                srcs = [hspt[:, (tph * 8 + j) * P:(tph * 8 + j + 1) * P] for j in range(8)]
                dsts = [tps[:, j * P:(j + 1) * P] for j in range(8)]
                def ftr(E, srcs=srcs, dsts=dsts):
                    for d_, a_ in zip(dsts, srcs):
                        tr = E.transpose(d_, a_, s.ident_sb[:, :])
                    return tr
                petok = s.op("tensor", s.s_pe, ftr)
                s.wait("scalar", s.s_pe, petok)
                dst = hspf[:, grp * 16 + tph * 8:grp * 16 + (tph + 1) * 8, sub_l * P:(sub_l + 1) * P]
                src = tps.ap().rearrange("p (j q) -> p j q", q=P)[:, 0:8, :]
                etok = s.op("scalar", s.s_act, lambda E, d=dst, a=src: E.copy(d, a))
                s.slot[("tps", tph)] = (s.s_act, etok)
            s.slot[("hspt", hslot)] = (s.s_pe, s.s_pe.n)

        def emit_sparsify(k):
            while k > 0:
                if sp_state["cur"] is None:
                    if not sp_pending:
                        return
                    tag, grp, sub_l, mk_builders = sp_pending.pop(0)
                    hslot = sp_state["ci"] % 2
                    sp_state["ci"] += 1
                    hw = s.slot.get(("hbw", grp % 2, sub_l))
                    if hw:
                        s.wait("vector", *hw)
                    dep = s.slot.get(("hspt", hslot))
                    if dep:
                        s.wait("vector", *dep)
                    sp_state["cur"] = (grp, sub_l, hslot, mk_builders(hslot), 0)
                grp, sub_l, hslot, builders, i = sp_state["cur"]
                if sp_state["tok"]:
                    s.wait("vector", s.s_dve, sp_state["tok"])
                sp_state["tok"] = s.op("vector", s.s_dve, builders[i])
                i += 1
                k -= 1
                if i == len(builders):
                    tp_queue.append((grp, sub_l, hslot, sp_state["tok"]))
                    sp_state["cur"] = None
                else:
                    sp_state["cur"] = (grp, sub_l, hslot, builders, i)

        gu_rdy = {}

        def issue_gu(ci):
            sl_, q4_ = divmod(ci, 4)
            ring_ = ci % 4
            out = {}
            for (nm, o_ring, sW, q_) in (("Wg", s.o_wg, s.sWg, "sync"),
                                         ("Wu", s.o_wu, s.sWu, "scalar")):
                dst_ = s.v1(o_ring + ring_ * D, D)
                dep_ = s.slot.get((nm + "ring", ring_))
                out[nm] = s.load_unit((nm, sl_, q4_), s.dsem(nm + "r%d" % ring_), dst_,
                                      sW.ap()[sl_ * P:(sl_ + 1) * P, q4_ * 4 * 512:(q4_ + 1) * 4 * 512],
                                      dep=dep_, queue=q_)
            gu_rdy[ci] = out
        for ci0 in range(3):
            issue_gu(ci0)
        for sl in range(NSL):
            flush_tp()
            gtok = utok = None
            for q4 in range(4):
                ci = sl * 4 + q4
                ring = ci % 4
                if ci + 3 < NSL * 4:
                    issue_gu(ci + 3)
                rds = gu_rdy[ci]
                s.wait("tensor", rds["Wg"][0], rds["Wg"][1])
                s.wait("tensor", rds["Wu"][0], rds["Wu"][1])
                wg_c = s.v1(s.o_wg + ring * D, D)
                wu_c = s.v1(s.o_wu + ring * D, D)
                tok = None
                for sub_l in (1, 0):
                    sub = subs[sub_l]
                    if q4 == 0:
                        keys = ([("gps0", sl % 2), ("ups0", sl % 2)] if sub_l == 0
                                else [("gps1",), ("ups1",)])
                        for k_ in keys:
                            d_ = s.slot.get(k_)
                            if d_:
                                s.wait("tensor", *d_)
                    lhs = [x1fm[:, q4 * 4 + j, sub * P:(sub + 1) * P] for j in range(4)]
                    g_ps, u_ps = gps_of(sub_l, sl), ups_of(sub_l, sl)
                    def fmm(E, lhs=lhs, wg_c=wg_c, wu_c=wu_c, g_ps=g_ps, u_ps=u_ps, q4=q4):
                        for j in range(4):
                            st_f = (q4 == 0 and j == 0)
                            sp_f = (q4 == 3 and j == 3)
                            mm = E.matmul(g_ps, lhs[j], wg_c[:, j * 512:(j + 1) * 512],
                                          start=st_f, stop=sp_f)
                            mm2 = E.matmul(u_ps, lhs[j], wu_c[:, j * 512:(j + 1) * 512],
                                           start=st_f, stop=sp_f)
                        return mm2
                    tok = s.op("tensor", s.s_pe, fmm)
                    if q4 == 3:
                        if sub_l == 0:
                            gtok = tok
                        else:
                            utok = tok
                s.slot[("Wgring", ring)] = (s.s_pe, tok)
                s.slot[("Wuring", ring)] = (s.s_pe, tok)
            mm_done = {subs[0]: gtok, subs[1]: utok}
            sl_l = sl % 4
            grp = sl // 4
            for sub_l in (1, 0):
                sub = subs[sub_l]
                silu = s.v1(s.o_silu + sub_l * 512, 512)
                dw = s.slot.get(("silu_w", sub_l))
                if dw:
                    s.wait("scalar", *dw)
                s.wait("scalar", s.s_pe, mm_done[sub])
                g_ps, u_ps = gps_of(sub_l, sl), ups_of(sub_l, sl)
                stok = s.op("scalar", s.s_act, lambda E, d=silu, a=g_ps: E.activation(d, a, AF.Silu))
                s.slot[("gps0", sl % 2) if sub_l == 0 else ("gps1",)] = (s.s_act, stok)
                hb = s.v1(s.o_hbuf + (grp % 2) * 2 * D + sub_l * D + sl_l * 512, 512)
                s.wait("vector", s.s_act, stok)
                s.wait("vector", s.s_pe, mm_done[sub])
                htok = s.op("vector", s.s_dve,
                            lambda E, d=hb, a=u_ps, b=silu: E.tensor_tensor(d, a, b, AluOpType.mult))
                s.slot[("ups0", sl % 2) if sub_l == 0 else ("ups1",)] = (s.s_dve, htok)
                s.slot[("hbw", grp % 2, sub_l)] = (s.s_dve, htok)
                s.slot[("silu_w", sub_l)] = (s.s_dve, htok)
            if sl_l == 3:
                for sub_l in (1, 0):
                    sub = subs[sub_l]
                    def mk_builders(hslot, grp=grp, sub_l=sub_l):
                        hb = s.v1(s.o_hbuf + (grp % 2) * 2 * D + sub_l * D, D)
                        habs = s.v1(s.o_habs, D)
                        hspt = s.v1(s.o_hspt + hslot * D, D)
                        thr = s.v1(s.o_thr, 512)
                        t1_ = s.v1(s.o_thr + 512, 512)
                        t2_ = s.v1(s.o_thr + 1024, 512)
                        t3_ = s.v1(s.o_thr + 1536, 512)
                        h0, h1, h2, h3 = (habs[:, k:D:4] for k in range(4))
                        builders = [
                            lambda E, habs=habs, hb=hb: E.tensor_tensor(habs, hb, hb, AluOpType.mult),
                            lambda E, t1_=t1_, h0=h0, h1=h1: E.tensor_tensor(t1_, h0, h1, AluOpType.max),
                            lambda E, t2_=t2_, h0=h0, h1=h1: E.tensor_tensor(t2_, h0, h1, AluOpType.min),
                            lambda E, t3_=t3_, h2=h2, h3=h3: E.tensor_tensor(t3_, h2, h3, AluOpType.max),
                            lambda E, thr=thr, h2=h2, h3=h3: E.tensor_tensor(thr, h2, h3, AluOpType.min),
                            lambda E, t1_=t1_, t3_=t3_: E.tensor_tensor(t1_, t1_, t3_, AluOpType.min),
                            lambda E, t2_=t2_, thr=thr: E.tensor_tensor(t2_, t2_, thr, AluOpType.max),
                            lambda E, thr=thr, t1_=t1_, t2_=t2_: E.tensor_tensor(thr, t1_, t2_, AluOpType.max),
                        ]
                        for k in range(4):
                            tt = t1_ if k % 2 == 0 else t2_
                            hak = habs[:, k:D:4]
                            hbk = hb[:, k:D:4]
                            hok = hspt[:, k:D:4]
                            builders.append(lambda E, tt=tt, hak=hak, thr=thr:
                                            E.tensor_tensor(tt, hak, thr, AluOpType.is_ge))
                            builders.append(lambda E, hok=hok, tt=tt, hbk=hbk:
                                            E.tensor_tensor(hok, tt, hbk, AluOpType.mult))
                        return builders
                    sp_pending.append(("start", grp, sub_l, mk_builders))
            emit_sparsify(9)

        while sp_pending or sp_state["cur"] is not None:
            emit_sparsify(8)
        flush_tp()

        for k_ in (("tps", 0), ("tps", 1), ("gps0", 0), ("gps0", 1), ("ups0", 0),
                   ("ups0", 1), ("gps1",), ("ups1",)):
            d_ = s.slot.get(k_)
            if d_:
                s.wait("tensor", *d_)
        r2 = s.AR1[:, s.o_r2:s.o_r2 + 2 * D]
        for oc in range(KC):
            dps = s.pA if oc % 2 == 0 else s.pB
            dtok = None
            for hu in range(2):
                idx = oc * 2 + hu
                ring = idx % 2
                wd = s.AR2[:, ring * 4096:(ring + 1) * 4096]
                ta, tb = wd_rdy[idx]
                s.wait("tensor", ta[0], ta[1])
                s.wait("tensor", tb[0], tb[1])
                if hu == 0:
                    depo = s.slot.get(("dps", oc % 2))
                    if depo:
                        s.wait("tensor", *depo)
                lhs = [wd[:, k * P:(k + 1) * P] for k in range(32)]
                rhs = [hspf[:, hu * 32 + k, :] for k in range(32)]
                def fdn(E, lhs=lhs, rhs=rhs, dps=dps, hu=hu):
                    for k in range(32):
                        mm = E.matmul(dps[:, 0:256], lhs[k], rhs[k],
                                      start=(hu == 0 and k == 0), stop=(hu == 1 and k == 31))
                    return mm
                dtok = s.op("tensor", s.s_pe, fdn)
                s.slot[("wdring", ring)] = (s.s_pe, dtok)
                if idx + 2 < 2 * KC:
                    issue_wd(idx + 2)
            dfm = s.v1(s.o_dfm + (oc % 2) * 256, 256)
            s.wait("scalar", s.s_pe, dtok)
            depf = s.slot.get(("dfm", oc % 2))
            if depf:
                s.wait("scalar", *depf)
            etok = s.op("scalar", s.s_act, lambda E, d=dfm, a=dps: E.copy(d, a[:, 0:256]))
            s.slot[("dps", oc % 2)] = (s.s_act, etok)
            tps = s.tA if oc % 2 == 0 else s.tB
            for k_ in (("tpsF3", oc % 2), ("tps", oc % 2)):
                d_ = s.slot.get(k_)
                if d_:
                    s.wait("tensor", *d_)
            s.wait("tensor", s.s_act, etok)
            dsts = [tps[:, j * P:(j + 1) * P] for j in range(2)]
            srcs = [dfm[:, j * P:(j + 1) * P] for j in range(2)]
            def ftr(E, srcs=srcs, dsts=dsts):
                for d_, a_ in zip(dsts, srcs):
                    tr = E.transpose(d_, a_, s.ident_sb[:, :])
                return tr
            petok = s.op("tensor", s.s_pe, ftr)
            s.slot[("dfm", oc % 2)] = (s.s_pe, petok)
            s.wait("vector", s.s_pe, petok)
            dst = r2.rearrange("p (u c) -> p u c", c=D)[:, 0:2, oc * P:(oc + 1) * P]
            src = tps.ap().rearrange("p (u q) -> p u q", q=P)[:, 0:2, :]
            xb = s.x1_bf.ap().rearrange("p (u c) -> p u c", c=D)[:, 2 * half:2 * half + 2, oc * P:(oc + 1) * P]
            rtok = s.op("vector", s.s_dve,
                        lambda E, d=dst, a=src, b=xb: E.tensor_tensor(d, a, b, AluOpType.add))
            s.slot[("tpsF3", oc % 2)] = (s.s_dve, rtok)
        for sub_l, sub in enumerate(subs):
            osb = s.AR3[:, sub_l * D:(sub_l + 1) * D]
            depo = s.slot.get(("osb_v", sub_l))
            if depo:
                s.wait("vector", *depo)
            aptok = self.emit_ln(r2[:, sub_l * D:(sub_l + 1) * D], osb)
            s.wait("sync", s.s_dve, aptok)
            osem = s.dsem("out%d" % sub_l)
            otok = s.dma("sync", osem, s.out.ap()[t * TILE + sub * P:t * TILE + (sub + 1) * P, :],
                         osb)
            s.slot[("osb", sub_l)] = (osem, otok)
            s.slot[("osb_v", sub_l)] = (osem, otok)
        s.close()

    def build(self):
        s = self
        s.open()
        s.dma("sync", s.s_cst, s.ident_sb[:, :], s.ident.ap())
        for e in ("tensor",):
            s.wait(e, s.s_cst, 16)
        s.close()
        s.stage_precast()
        s.stage_X(-1)
        s.stage_QKV(-1)
        for t in range(s.n_tiles):
            s.stage_X(t)
            s.stage_QKV(t)
            s.stage_A(t)
            if s.debug and t == 0:
                s.dbg("xfm", s.v1(s.o_xfm, KC * TILE), KC * TILE, BF16)
                s.dbg("q", s.v1(s.o_q, NH * 512), NH * 512, BF16)
                s.dbg("kroll", s.k_roll[:, :], 2 * NKV * 512, BF16)
                s.dbg("vroll", s.v_roll[:, :], 2 * NSUB * 512, BF16)
                s.dbg("attn", s.AR2[:, :], 8192, BF16)
            s.stage_O(t)
            if s.debug and t == 0:
                s.dbg("x1", s.x1_bf[:, :], NSUB * D, BF16)
                s.dbg("x1fm", s.x1_fm[:, :], KC * TILE, BF16)
            for half in range(2):
                s.stage_F(t, half)
            if s.debug and t == 0:
                s.dbg("hspf", s.v1(s.o_hspf, NIC * 256), NIC * 256, BF16)
        s.open()
        def fin(E):
            for nm, sm in s.dsems.items():
                if nm.endswith("_wb") or nm.startswith("out"):
                    E.wait_ge(sm.h, sm.n)
            E.wait_ge(s.s_dbg.h, s.s_dbg.n)
        s.stage.add("sync", fin)
        s.close()
        return s.nc


def build_masks(seq_start: bool) -> np.ndarray:
    m = np.zeros((5, P, MW), dtype=np.float32)
    r = np.arange(P)[:, None]
    kw = np.arange(MW)[None, :]
    band = (kw >= r + 1) & (kw <= r + WIN)
    for v in range(5):
        lo = (WIN - P * v) if (seq_start and v < 4) else 0
        m[v] = (band & (kw >= lo)).astype(np.float32)
    return np.ascontiguousarray(m.transpose(1, 0, 2).reshape(P, 5 * MW)).astype(BB)


def kernel(L0_attn_hidden_states, L0_attn_Wq, L0_attn_Wk, L0_attn_Wv, L0_attn_Wo,
           L0_ffn_w_gate, L0_ffn_w_up, L0_ffn_w_down):
    x = np.ascontiguousarray(np.asarray(L0_attn_hidden_states, dtype=np.float32))
    B, S, _ = x.shape
    n_cores = 8
    per = (B * S) // n_cores
    mk = MK(n_tiles=NT, debug=False)
    nc = mk.build()
    ident = np.eye(P, dtype=BB)
    w = dict(L0_attn_Wq=np.asarray(L0_attn_Wq, np.float32),
             L0_attn_Wk=np.asarray(L0_attn_Wk, np.float32),
             L0_attn_Wv=np.asarray(L0_attn_Wv, np.float32),
             L0_attn_Wo=np.asarray(L0_attn_Wo, np.float32),
             L0_ffn_w_gate=np.asarray(L0_ffn_w_gate, np.float32),
             L0_ffn_w_up=np.asarray(L0_ffn_w_up, np.float32),
             L0_ffn_w_down=np.asarray(L0_ffn_w_down, np.float32))
    xf = x.reshape(B * S, D)
    in_maps = []
    for c in range(n_cores):
        t0 = c * per
        seq_start = (t0 % S) == 0
        halo = np.zeros((HALO, D), np.float32) if seq_start else xf[t0 - HALO:t0]
        xs = np.ascontiguousarray(np.concatenate([halo, xf[t0:t0 + per]], axis=0))
        im = {"x": xs, "mask01": build_masks(seq_start), "ident": ident}
        im.update(w)
        in_maps.append(im)
    res = run_bass_kernel_spmd(nc, in_maps, core_ids=list(range(n_cores)))
    outs = [res.results[c]["out"] for c in range(n_cores)]
    return np.concatenate(outs, axis=0).reshape(B, S, D).astype(np.float32)


# version 10
# speedup vs baseline: 1.1167x; 1.0658x over previous
import numpy as np
import ml_dtypes
from contextlib import ExitStack

import concourse.bass as bass
import concourse.mybir as mybir
from concourse.alu_op_type import AluOpType
from concourse.bass_utils import run_bass_kernel_spmd

F32 = mybir.dt.float32
BF16 = mybir.dt.bfloat16
AF = mybir.ActivationFunctionType
BB = ml_dtypes.bfloat16

P = 128
D = 2048
KC = 16
NH = 16
NKV = 4
WIN = 512
ISZ = 8192
NIC = 64
TILE = 512
NSUB = 4
HALO = 512
NT = 8
NSL = 16
MW = 640

f32 = np.float32
ENGS = ("sync", "gpsimd", "tensor", "scalar", "vector")


class Sem:
    def __init__(self, nc, es, name, step=1):
        self.h = es.enter_context(nc.semaphore(name))
        self.n = 0
        self.step = step
        self.name = name

    def reserve(self):
        self.n += self.step
        return self.n


class Stage:
    def __init__(self):
        self.lists = {e: [] for e in ENGS}

    def add(self, eng, fn):
        self.lists[eng].append(fn)


class MK:
    def __init__(self, n_tiles=NT, debug=False):
        self.n_tiles = n_tiles
        self.debug = debug
        self.dbg_names = []
        nc = bass.Bass("TRN2", target_bir_lowering=False)
        self.nc = nc
        self.es = ExitStack()
        es = self.es
        ntok = HALO + n_tiles * TILE

        dram = nc.dram_tensor
        self.x = dram("x", [ntok, D], F32, kind="ExternalInput")
        self.Wq = dram("L0_attn_Wq", [D, D], F32, kind="ExternalInput")
        self.Wk = dram("L0_attn_Wk", [D, 512], F32, kind="ExternalInput")
        self.Wv = dram("L0_attn_Wv", [D, 512], F32, kind="ExternalInput")
        self.Wo = dram("L0_attn_Wo", [D, D], F32, kind="ExternalInput")
        self.Wg = dram("L0_ffn_w_gate", [D, ISZ], F32, kind="ExternalInput")
        self.Wu = dram("L0_ffn_w_up", [D, ISZ], F32, kind="ExternalInput")
        self.Wd = dram("L0_ffn_w_down", [ISZ, D], F32, kind="ExternalInput")
        self.mask = dram("mask01", [P, 5 * MW], BF16, kind="ExternalInput")
        self.ident = dram("ident", [P, P], BF16, kind="ExternalInput")
        self.out = dram("out", [n_tiles * TILE, D], F32, kind="ExternalOutput")
        self.sWq = dram("sWq", [D, D], BF16)
        self.sWk = dram("sWk", [512, D], BF16)
        self.sWv = dram("sWv", [P, KC * 512], BF16)
        self.sWo = dram("sWo", [D, D], BF16)
        self.sWg = dram("sWg", [NSL * P, KC * 512], BF16)
        self.sWu = dram("sWu", [NSL * P, KC * 512], BF16)
        self.sWd = dram("sWd", [KC * P, NIC * P], BF16)

        self.s_pe = Sem(nc, es, "s_pe")
        self.s_act = Sem(nc, es, "s_act")
        self.s_dve = Sem(nc, es, "s_dve")
        self.s_pool = Sem(nc, es, "s_pool")
        self.s_cst = Sem(nc, es, "s_cst", 16)
        self.s_dbg = Sem(nc, es, "s_dbg", 16)
        self.dsems = {}
        self.waited = {}
        self.slot_wb = {}
        self.fence_sems = set()

        sb = lambda name, cols, dt: es.enter_context(nc.sbuf_tensor(name, [P, cols], dt))
        self.AR1 = sb("AR1", 50688, BF16)
        self.AR2 = sb("AR2", 8192, BF16)
        self.AR3 = sb("AR3", 4096, F32)
        self.x1_bf = sb("x1_bf", NSUB * D, BF16)
        self.x1_fm = sb("x1_fm", KC * TILE, BF16)
        self.x_tm_bf = sb("x_tm_bf", NSUB * D, BF16)
        self.ident_sb = sb("ident_sb", P, BF16)
        self.k_roll = sb("k_roll", 2 * NKV * 512, BF16)
        self.v_roll = sb("v_roll", 2 * NSUB * 512, BF16)
        self.den4 = sb("den4", 16, F32)
        self.rec4 = sb("rec4", 16, F32)
        self.stats = sb("stats", 24, F32)
        self.mv = sb("mv", 2, F32)
        self.lnv = sb("lnv", 1, F32)
        self.rstd = sb("rstd", 1, F32)

        self.o_xbf = 0
        self.o_xfm = 4096
        self.o_wq = 12288
        self.o_wk = 16384
        self.o_wv = 20480
        self.o_q = 28672
        self.o_pe_ = 36864
        self.o_pm = 38144
        self.o_pT = 45824
        self.o_wo = 0
        self.o_wg = 0
        self.o_wu = 8192
        self.o_hbuf = 16384
        self.o_habs = 30208
        self.o_thr = 32256
        self.o_hspt = 24576
        self.o_silu = 28672
        self.o_dfm = 29696
        self.o_r2 = 30208
        self.o_hspf = 34304
        self.o_mask = 0
        self.o_r1b = 34304

        ps = lambda name, cols, dt: es.enter_context(nc.psum_tensor(name, [P, cols], dt))
        self.sc0 = ps("sc0", 1024, F32)
        self.sc1 = ps("sc1", 1024, F32)
        self.tA = ps("tA", 1024, BF16)
        self.tB = ps("tB", 1024, BF16)
        self.pA = ps("pA", 512, F32)
        self.pB = ps("pB", 512, F32)

        self.wb_tok = {}
        self.slot = {}
        self.stage = None

    def v1(self, off, cols):
        return self.AR1[:, off:off + cols]

    def dsem(self, name):
        if name not in self.dsems:
            self.dsems[name] = Sem(self.nc, self.es, "sd_" + name, 16)
        return self.dsems[name]

    def wait(self, eng, sem, tok):
        if not tok:
            return
        key = (eng, sem.name)
        if self.waited.get(key, 0) >= tok:
            return
        self.waited[key] = tok
        self.stage.add(eng, lambda E, s=sem.h, t=tok: E.wait_ge(s, t))

    def op(self, eng, sem, builder):
        if sem is None:
            self.stage.add(eng, lambda E, b=builder: b(E))
            return None
        tok = sem.reserve()
        self.stage.add(eng, lambda E, b=builder, h=sem.h: b(E).then_inc(h, 1))
        return tok

    def dma(self, eng, sem, dst, src):
        tok = sem.reserve()
        self.stage.add(eng, lambda E, d=dst, a=src, h=sem.h: E.dma_start(d, a).then_inc(h, 16))
        return tok

    def open(self):
        self.stage = Stage()

    def close(self):
        st = self.stage
        for sem in sorted(self.fence_sems, key=lambda x: x.name):
            st.add("sync", lambda E, h=sem.h, n=sem.n: E.wait_ge(h, n))
            self.waited[("sync", sem.name)] = sem.n
        with self.nc.Block() as block:
            for e in ENGS:
                fns = st.lists[e]
                if fns:
                    def runner(E, fns=fns):
                        for f in fns:
                            f(E)
                    getattr(block, e)(runner)
        for sem in (self.s_pe, self.s_act, self.s_dve, self.s_pool):
            for e in ENGS:
                key = (e, sem.name)
                self.waited[key] = max(self.waited.get(key, 0), sem.n)
        names = {k[1] for k in self.waited}
        for nm in names:
            m = max(self.waited.get((e, nm), 0) for e in ENGS)
            for e in ENGS:
                self.waited[(e, nm)] = m
        self.fence_sems = set()
        self.stage = None

    def load_unit(self, key, slot_sem, dst, scratch, dep=None, queue="sync"):
        s = self
        eng = "sync" if queue == "sync" else "scalar"
        if dep:
            s.wait(eng, dep[0], dep[1])
        tok = s.dma(eng, slot_sem, dst, scratch)
        return (slot_sem, tok)

    def stage_precast(self):
        s = self
        s.open()
        ldsem = [s.dsem("pcl0"), s.dsem("pcl1")]
        wbsem = [s.dsem("pc0_wb"), s.dsem("pc1_wb")]
        state = {"cast": [None, None], "i": 0, "asm_rd": None}
        o_asm = 16384

        def cast_chunk(gat, scr=None, asm_dst=None, asm_src_view=None):
            i = state["i"]; state["i"] += 1
            sl_ = i % 2
            stg = s.AR3[:, sl_ * 2048:(sl_ + 1) * 2048]
            bfs = s.v1(s.o_xbf + sl_ * D, D)
            if state["cast"][sl_]:
                s.wait("sync", s.s_dve, state["cast"][sl_])
            ltok = s.dma("sync", ldsem[sl_], stg[:, 0:gat.shape[-1] if False else 2048] if False else stg, gat)
            s.wait("vector", ldsem[sl_], ltok)
            s.wait("vector", wbsem[sl_], wbsem[sl_].n)
            ctok = s.op("vector", s.s_dve, lambda E, d=bfs, a=stg: E.tensor_copy(d, a))
            state["cast"][sl_] = ctok
            if scr is not None:
                s.wait("scalar", s.s_dve, ctok)
                s.dma("scalar", wbsem[sl_], scr, bfs)
            else:
                s.wait("vector", s.s_dve, ctok)
                ctok2 = s.op("vector", s.s_dve,
                             lambda E, d=asm_dst, a=asm_src_view(bfs): E.tensor_copy(d, a))
                state["cast"][sl_] = ctok2
                return ctok2
            return ctok

        asem = s.dsem("pca_wb")

        def flush_asm(wb_list):
            s.wait("scalar", s.s_dve, s.s_dve.n)
            for scr, lo, cols in wb_list:
                s.dma("scalar", asem, scr, s.v1(o_asm + lo, cols))
            state["asm_rd"] = asem.n

        def asm_guard():
            if state["asm_rd"]:
                s.wait("vector", asem, state["asm_rd"])
                state["asm_rd"] = None

        A = s.v1(o_asm, 32768).rearrange("p (u c) -> p u c", c=1)

        for kc in range(KC):
            asm_guard()
            gat = s.Wq.ap()[kc * P:(kc + 1) * P, :]
            dst = s.v1(o_asm, NH * D).rearrange("p (h m) -> p h m", m=P)[:, :, :]
            dstv = s.v1(o_asm + kc * P, 1)
            dst2 = bass.AP(s.AR1, o_asm + kc * P, [[50688, P], [D, NH], [1, P]])
            src_view = lambda bfs: bfs.rearrange("p (h m) -> p h m", m=P)
            cast_chunk(gat, asm_dst=dst2, asm_src_view=src_view)
        flush_asm([(s.sWq.ap()[h * P:(h + 1) * P, :], h * D, D) for h in range(NH)])

        for kc in range(KC):
            asm_guard()
            gat = s.Wk.ap()[kc * P:(kc + 1) * P, :]
            dst2 = bass.AP(s.AR1, o_asm + kc * P, [[50688, P], [D, NKV], [1, P]])
            i = state["i"]; state["i"] += 1
            sl_ = i % 2
            stg = s.AR3[:, sl_ * 2048:sl_ * 2048 + 512]
            bfs = s.v1(s.o_xbf + sl_ * D, 512)
            if state["cast"][sl_]:
                s.wait("sync", s.s_dve, state["cast"][sl_])
            ltok = s.dma("sync", ldsem[sl_], stg, gat)
            s.wait("vector", ldsem[sl_], ltok)
            ctok = s.op("vector", s.s_dve, lambda E, d=bfs, a=stg: E.tensor_copy(d, a))
            s.wait("vector", s.s_dve, ctok)
            sv = bfs.rearrange("p (g m) -> p g m", m=P)
            ctok2 = s.op("vector", s.s_dve, lambda E, d=dst2, a=sv: E.tensor_copy(d, a))
            state["cast"][sl_] = ctok2
        flush_asm([(s.sWk.ap()[g * P:(g + 1) * P, :], g * D, D) for g in range(NKV)])

        for kc in range(KC):
            asm_guard()
            i = state["i"]; state["i"] += 1
            sl_ = i % 2
            stg = s.AR3[:, sl_ * 2048:sl_ * 2048 + 512]
            if state["cast"][sl_]:
                s.wait("sync", s.s_dve, state["cast"][sl_])
            ltok = s.dma("sync", ldsem[sl_], stg, s.Wv.ap()[kc * P:(kc + 1) * P, :])
            s.wait("vector", ldsem[sl_], ltok)
            dd = s.v1(o_asm + kc * 512, 512)
            ctok = s.op("vector", s.s_dve, lambda E, d=dd, a=stg: E.tensor_copy(d, a))
            state["cast"][sl_] = ctok
        flush_asm([(s.sWv.ap(), 0, KC * 512)])

        for ic in range(KC):
            asm_guard()
            gat = s.Wo.ap()[ic * P:(ic + 1) * P, :]
            dst2 = bass.AP(s.AR1, o_asm + ic * P, [[50688, P], [D, KC], [1, P]])
            src_view = lambda bfs: bfs.rearrange("p (oc m) -> p oc m", m=P)
            cast_chunk(gat, asm_dst=dst2, asm_src_view=src_view)
        flush_asm([(s.sWo.ap()[oc * P:(oc + 1) * P, :], oc * D, D) for oc in range(KC)])

        for qp in range(4):
            for icl in range(KC):
                asm_guard()
                ic = qp * KC + icl
                gat = s.Wd.ap()[ic * P:(ic + 1) * P, :]
                dst2 = bass.AP(s.AR1, o_asm + icl * P, [[50688, P], [D, KC], [1, P]])
                src_view = lambda bfs: bfs.rearrange("p (oc m) -> p oc m", m=P)
                cast_chunk(gat, asm_dst=dst2, asm_src_view=src_view)
            flush_asm([(s.sWd.ap()[oc * P:(oc + 1) * P, qp * 2048:(qp + 1) * 2048], oc * D, D)
                       for oc in range(KC)])

        wgr = s.Wg.ap().rearrange("(kc p) i -> p kc i", p=P)
        wur = s.Wu.ap().rearrange("(kc p) i -> p kc i", p=P)
        for (Wr, sW) in ((wgr, s.sWg), (wur, s.sWu)):
            for sl in range(NSL):
                for q in range(4):
                    cast_chunk(Wr[:, q * 4:(q + 1) * 4, sl * 512:(sl + 1) * 512],
                               scr=sW.ap()[sl * P:(sl + 1) * P, q * 2048:(q + 1) * 2048])
        s.fence_sems.add(wbsem[0])
        s.fence_sems.add(wbsem[1])
        s.fence_sems.add(asem)
        s.close()

    def dbg(self, name, src_ap, cols, dt):
        if not self.debug:
            return
        t = self.nc.dram_tensor("dbg_" + name, [P, cols], dt, kind="ExternalOutput")
        self.dbg_names.append("dbg_" + name)
        self.open()
        self.dma("sync", self.s_dbg, t.ap(), src_ap)
        self.close()

    def stage_X(self, t):
        s = self
        s.open()
        row0 = 0 if t == -1 else HALO + t * TILE
        xfm = s.v1(s.o_xfm, KC * TILE).rearrange("p (kc q) -> p kc q", q=TILE)
        x_tok = {}

        def issue_x(sub):
            sl2 = sub % 2
            stg_ = s.AR3[:, sl2 * 2048:(sl2 + 1) * 2048]
            depc = s.slot.get(("xstg", sl2))
            if depc:
                s.wait("sync", *depc)
            dob = s.slot.get(("osb", sl2))
            if dob:
                s.wait("sync", *dob)
            x_tok[sub] = s.dma("sync", s.dsem("xb%d" % sl2), stg_,
                               s.x.ap()[row0 + sub * P:row0 + (sub + 1) * P, :])
        issue_x(0)
        issue_x(1)
        for sub in range(NSUB):
            sl_ = sub % 2
            xbf = s.x_tm_bf[:, sub * D:(sub + 1) * D]
            stg = s.AR3[:, sl_ * 2048:(sl_ + 1) * 2048]
            xsem = s.dsem("xb%d" % sl_)
            ltok = x_tok[sub]
            s.wait("vector", xsem, ltok)
            ctok = s.op("vector", s.s_dve, lambda E, d=xbf, a=stg: E.tensor_copy(d, a))
            s.slot[("xstg", sl_)] = (s.s_dve, ctok)
            if sub + 2 < NSUB:
                issue_x(sub + 2)
            s.wait("tensor", s.s_dve, ctok)
            last_pe = None
            for half in range(2):
                tps = s.tA if half == 0 else s.tB
                dep2 = s.slot.get(("tps", half))
                if dep2:
                    s.wait("tensor", *dep2)
                srcs = [xbf[:, (half * 8 + j) * P:(half * 8 + j + 1) * P] for j in range(8)]
                dsts = [tps[:, j * P:(j + 1) * P] for j in range(8)]
                def ftr(E, srcs=srcs, dsts=dsts):
                    for d_, a_ in zip(dsts, srcs):
                        tr = E.transpose(d_, a_, s.ident_sb[:, :])
                    return tr
                petok = s.op("tensor", s.s_pe, ftr)
                last_pe = petok
                s.wait("vector", s.s_pe, petok)
                dst = xfm[:, half * 8:(half + 1) * 8, sub * P:(sub + 1) * P]
                src = tps.ap().rearrange("p (j q) -> p j q", q=P)[:, 0:8, :]
                dtok = s.op("vector", s.s_dve, lambda E, d=dst, a=src: E.tensor_copy(d, a))
                s.slot[("tps", half)] = (s.s_dve, dtok)
        s.close()

    def stage_QKV(self, t):
        s = self
        s.open()
        halo = (t == -1)
        par = t % 2
        xfm = s.v1(s.o_xfm, KC * TILE).rearrange("p (kc q) -> p kc q", q=TILE)
        wv_sb = s.v1(s.o_wv, KC * 512)

        if t == 0:
            v_rdy = None
        else:
            v_rdy = s.load_unit(("Wv", 0), s.dsem("wv"), wv_sb, s.sWv.ap())

        if not halo:
            q_rdy = {}

            def issue_q(h):
                sl_ = h % 2
                wq_ = s.v1(s.o_wq + sl_ * D, D)
                dep_ = s.slot.get(("wq", sl_))
                q_rdy[h] = s.load_unit(("Wq", h), s.dsem("wq%d" % sl_), wq_,
                                       s.sWq.ap()[h * P:(h + 1) * P, :], dep=dep_,
                                       queue="sync" if h % 2 == 0 else "scalar")
            issue_q(0)
            issue_q(1)
            for h in range(NH):
                sl_ = h % 2
                wq = s.v1(s.o_wq + sl_ * D, D)
                rdy = q_rdy[h]
                qps = s.pA if h % 2 == 0 else s.pB
                depq = s.slot.get(("qps", h % 2))
                if depq:
                    s.wait("tensor", *depq)
                s.wait("tensor", rdy[0], rdy[1])
                lhs = [wq[:, kc * P:(kc + 1) * P] for kc in range(KC)]
                rhs = [xfm[:, kc, :] for kc in range(KC)]
                def fmm(E, lhs=lhs, rhs=rhs, qps=qps):
                    for kc in range(KC):
                        mm = E.matmul(qps[:, :], lhs[kc], rhs[kc],
                                      start=(kc == 0), stop=(kc == KC - 1))
                    return mm
                petok = s.op("tensor", s.s_pe, fmm)
                s.slot[("wq", sl_)] = (s.s_pe, petok)
                if h + 2 < NH:
                    issue_q(h + 2)
                s.wait("scalar", s.s_pe, petok)
                qdst = s.v1(s.o_q + h * TILE, TILE)
                atok = s.op("scalar", s.s_act, lambda E, d=qdst, a=qps: E.copy(d, a[:, :]))
                s.slot[("qps", h % 2)] = (s.s_act, atok)

        k_rdy = {}

        def issue_k(g):
            sl2 = g % 2
            wk_ = s.v1(s.o_wk + sl2 * D, D)
            dep_ = s.slot.get(("wk", sl2))
            k_rdy[g] = s.load_unit(("Wk", g), s.dsem("wk%d" % sl2), wk_,
                                   s.sWk.ap()[g * P:(g + 1) * P, :], dep=dep_)
        issue_k(0)
        issue_k(1)
        for g in range(NKV):
            sl_ = g % 2
            wk = s.v1(s.o_wk + sl_ * D, D)
            rdy = k_rdy[g]
            kps = s.sc0[:, (g % 2) * 512:(g % 2) * 512 + 512]
            depk = s.slot.get(("kps", g % 2))
            if depk:
                s.wait("tensor", *depk)
            s.wait("tensor", rdy[0], rdy[1])
            lhs = [wk[:, kc * P:(kc + 1) * P] for kc in range(KC)]
            rhs = [xfm[:, kc, :] for kc in range(KC)]
            def fmm(E, lhs=lhs, rhs=rhs, kps=kps):
                for kc in range(KC):
                    mm = E.matmul(kps, lhs[kc], rhs[kc], start=(kc == 0), stop=(kc == KC - 1))
                return mm
            petok = s.op("tensor", s.s_pe, fmm)
            s.slot[("wk", sl_)] = (s.s_pe, petok)
            if g + 2 < NKV:
                issue_k(g + 2)
            s.wait("scalar", s.s_pe, petok)
            kdst = s.k_roll[:, (par * NKV + g) * 512:(par * NKV + g) * 512 + 512]
            atok = s.op("scalar", s.s_act, lambda E, d=kdst, a=kps: E.copy(d, a))
            s.slot[("kps", g % 2)] = (s.s_act, atok)

        if v_rdy:
            s.wait("tensor", v_rdy[0], v_rdy[1])
        for sub in range(NSUB):
            vps = s.sc1[:, (sub % 2) * 512:(sub % 2) * 512 + 512]
            depv = s.slot.get(("vps", sub % 2))
            if depv:
                s.wait("tensor", *depv)
            lhs = [xfm[:, kc, sub * P:(sub + 1) * P] for kc in range(KC)]
            rhs = [wv_sb[:, kc * 512:(kc + 1) * 512] for kc in range(KC)]
            def fmm(E, lhs=lhs, rhs=rhs, vps=vps):
                for kc in range(KC):
                    mm = E.matmul(vps, lhs[kc], rhs[kc], start=(kc == 0), stop=(kc == KC - 1))
                return mm
            petok = s.op("tensor", s.s_pe, fmm)
            s.wait("scalar", s.s_pe, petok)
            vdst = s.v_roll[:, (par * NSUB + sub) * 512:(par * NSUB + sub) * 512 + 512]
            atok = s.op("scalar", s.s_act, lambda E, d=vdst, a=vps: E.copy(d, a))
            s.slot[("vps", sub % 2)] = (s.s_act, atok)
        s.close()

    def stage_A(self, t):
        s = self
        s.open()
        par = t % 2
        ppar = (t - 1) % 2
        mtok = s.dma("sync", s.dsem("mask"), s.v1(s.o_mask, 5 * MW), s.mask.ap())
        s.wait("vector", s.dsem("mask"), mtok)

        def pass1(c, g):
            gc = t * 4 + c
            mslot = gc if gc < 4 else 4
            mask = s.v1(s.o_mask + mslot * MW, MW)
            cgp = (c * NKV + g) % 3
            stt_toks = []
            for hl in range(4):
                h = 4 * g + hl
                scps = s.sc0 if h % 2 == 0 else s.sc1
                dep = s.slot.get(("scps", h % 2))
                if dep:
                    s.wait("tensor", *dep)
                q_l = s.v1(s.o_q + h * TILE + c * P, P)
                w1 = 512 - c * P
                kp = s.k_roll[:, (ppar * NKV + g) * 512 + c * P:(ppar * NKV + g) * 512 + 512]
                kc2 = s.k_roll[:, (par * NKV + g) * 512:(par * NKV + g) * 512 + c * P + P]
                def fsc(E, q_l=q_l, kp=kp, kc2=kc2, scps=scps, c=c, w1=w1):
                    mm = E.matmul(scps[:, 0:w1], q_l, kp, start=True, stop=True)
                    if c == 0:
                        mm = E.matmul(scps[:, 512:640], q_l, kc2, start=True, stop=True)
                    else:
                        mm = E.matmul(scps[:, w1:512], q_l, kc2[:, 0:c * P], start=True, stop=True)
                        mm = E.matmul(scps[:, 512:640], q_l, kc2[:, c * P:c * P + P], start=True, stop=True)
                    return mm
                petok = s.op("tensor", s.s_pe, fsc)
                peb = s.v1(s.o_pe_ + (h % 2) * MW, MW)
                depe = s.slot.get(("peb", h % 2))
                if depe:
                    s.wait("scalar", *depe)
                s.wait("scalar", s.s_pe, petok)
                atok = s.op("scalar", s.s_act,
                            lambda E, d=peb, a=scps: E.activation(d, a[:, 0:MW], AF.Exp,
                                                                  scale=f32(1.0 / 128.0)))
                s.slot[("scps", h % 2)] = (s.s_act, atok)
                pm = s.v1(s.o_pm + cgp * 4 * MW + hl * MW, MW)
                depm = s.slot.get(("pm", cgp, hl))
                if depm:
                    s.wait("vector", *depm)
                s.wait("vector", s.s_act, atok)
                den = s.den4[:, cgp * 4 + hl:cgp * 4 + hl + 1]
                dtok = s.op("vector", s.s_dve,
                            lambda E, d=pm, a=peb, m=mask, acc=den:
                            E.scalar_tensor_tensor(d, a, 1.0, m, AluOpType.mult,
                                                   AluOpType.mult, accum_out=acc))
                s.slot[("peb", h % 2)] = (s.s_dve, dtok)
                stt_toks.append(dtok)
            return dict(c=c, g=g, gc=gc, cgp=cgp, stt=stt_toks)

        def pass2(ctx):
            c, g, gc, cgp = ctx["c"], ctx["g"], ctx["gc"], ctx["cgp"]
            cg2 = (c * NKV + g) % 2
            pT = s.v1(s.o_pT, 2560)
            deppT = s.slot.get(("pT",))
            s.wait("vector", s.s_dve, ctx["stt"][-1])
            rec = s.rec4[:, cgp * 4:cgp * 4 + 4]
            den = s.den4[:, cgp * 4:cgp * 4 + 4]
            rtok = s.op("vector", s.s_dve, lambda E, r=rec, d=den: E.reciprocal(r, d))
            ev_toks = []
            for hl in range(4):
                pm = s.v1(s.o_pm + cgp * 4 * MW + hl * MW, MW)
                s.wait("vector", s.s_dve, rtok)
                rsc = s.rec4[:, cgp * 4 + hl:cgp * 4 + hl + 1]
                ntok = s.op("vector", s.s_dve,
                            lambda E, d=pm, a=pm, r=rsc:
                            E.tensor_scalar(d, a, r, None, AluOpType.mult))
                tps = s.tA if hl % 2 == 0 else s.tB
                dept = s.slot.get(("tpsA", hl % 2))
                if dept:
                    s.wait("tensor", *dept)
                s.wait("tensor", s.s_dve, ntok)
                srcs = [pm[:, j * P:(j + 1) * P] for j in range(5)]
                dsts = [tps[:, j * P:(j + 1) * P] for j in range(5)]
                def ftr(E, srcs=srcs, dsts=dsts):
                    for d_, a_ in zip(dsts, srcs):
                        tr = E.transpose(d_, a_, s.ident_sb[:, :])
                    return tr
                petok = s.op("tensor", s.s_pe, ftr)
                s.slot[("pm", cgp, hl)] = (s.s_pe, petok)
                ev_eng = "scalar" if hl % 2 == 0 else "vector"
                ev_sem = s.s_act if hl % 2 == 0 else s.s_dve
                s.wait(ev_eng, s.s_pe, petok)
                if deppT:
                    s.wait(ev_eng, *deppT)
                dst = pT.rearrange("p (j q) -> p j q", q=512)[:, 0:5, hl * P:(hl + 1) * P]
                src = tps.ap().rearrange("p (j q) -> p j q", q=P)[:, 0:5, :]
                if hl % 2 == 0:
                    etok = s.op("scalar", s.s_act, lambda E, d=dst, a=src: E.copy(d, a))
                else:
                    etok = s.op("vector", s.s_dve, lambda E, d=dst, a=src: E.tensor_copy(d, a))
                s.slot[("tpsA", hl % 2)] = (ev_sem, etok)
                ev_toks.append((ev_sem, etok))
            aps = s.pA if cg2 == 0 else s.pB
            depa = s.slot.get(("aps", cg2))
            if depa:
                s.wait("tensor", *depa)
            for sm_, tk_ in ev_toks[-2:]:
                s.wait("tensor", sm_, tk_)
            vls = []
            for j in range(5):
                kcg = gc - 4 + j
                pj = (kcg // 4) % 2
                sj = kcg % 4
                o = (pj * NSUB + sj) * 512 + g * P
                vls.append(s.v_roll[:, o:o + P])
            pTs = [pT[:, j * 512:(j + 1) * 512] for j in range(5)]
            def fpv(E, vls=vls, pTs=pTs, aps=aps):
                for j in range(5):
                    mm = E.matmul(aps[:, :], vls[j], pTs[j], start=(j == 0), stop=(j == 4))
                return mm
            pvtok = s.op("tensor", s.s_pe, fpv)
            s.slot[("pT",)] = (s.s_pe, pvtok)
            s.wait("vector", s.s_pe, pvtok)
            dst = s.AR2.ap().rearrange("p (h q) -> p h q", q=TILE)[:, 4 * g:4 * g + 4, c * P:(c + 1) * P]
            src = aps.ap().rearrange("p (h q) -> p h q", q=P)[:, 0:4, :]
            dtok = s.op("vector", s.s_dve, lambda E, d=dst, a=src: E.tensor_copy(d, a))
            s.slot[("aps", cg2)] = (s.s_dve, dtok)

        pend = []
        for c in range(NSUB):
            for g in range(NKV):
                pend.append(pass1(c, g))
                if len(pend) > 2:
                    pass2(pend.pop(0))
        for ctx in pend:
            pass2(ctx)
        s.close()

    def emit_ln(self, src, dst):
        s = self
        s.wait("vector", s.s_dve, s.s_dve.n)
        atok = None
        for r in range(4):
            sv = src[:, r * 512:(r + 1) * 512]
            dv = s.stats[:, 6 * r:6 * r + 6]
            atok = s.op("vector", s.s_dve, lambda E, d=dv, a=sv: E.bn_stats(d, a))
        s.wait("vector", s.s_dve, atok)
        gtok = s.op("vector", s.s_dve, lambda E: E.bn_aggr(s.mv[:, :], s.stats[:, :]))
        s.wait("scalar", s.s_dve, gtok)
        ltok = s.op("scalar", s.s_act,
                    lambda E: E.activation(s.lnv[:, :], s.mv[:, 1:2], AF.Ln, bias=f32(1e-5)))
        s.wait("scalar", s.s_act, ltok)
        stok = s.op("scalar", s.s_act,
                    lambda E: E.activation(s.rstd[:, :], s.lnv[:, :], AF.Exp, scale=f32(-0.5)))
        s.wait("vector", s.s_act, stok)
        s.wait("vector", s.s_dve, gtok)
        tok = s.op("vector", s.s_dve,
                   lambda E, a=src, d=dst: E.tensor_scalar(d, a, s.mv[:, 0:1], s.rstd[:, 0:1],
                                                           AluOpType.subtract, AluOpType.mult))
        return tok

    def stage_O(self, t):
        s = self
        s.open()
        attn = s.AR2.ap().rearrange("p (h q) -> p h q", q=TILE)
        x1fm = s.x1_fm.ap().rearrange("p (kc q) -> p kc q", q=TILE)
        xtm = s.x_tm_bf.ap().rearrange("p (u c) -> p u c", c=D)
        r1b = s.v1(s.o_r1b, NSUB * D)
        r1bv = r1b.rearrange("p (u c) -> p u c", c=D)
        wo_rdy = {}

        def issue_wo(oc):
            ring_ = oc % 2
            wo_ = s.v1(s.o_wo + ring_ * D, D)
            dep_ = s.slot.get(("wo", ring_))
            wo_rdy[oc] = s.load_unit(("Wo", oc), s.dsem("wo%d" % ring_), wo_,
                                     s.sWo.ap()[oc * P:(oc + 1) * P, :], dep=dep_,
                                     queue="sync" if ring_ == 0 else "scalar")
        issue_wo(0)
        issue_wo(1)

        def o_tail(oc):
            wob = s.v1(s.o_silu + (oc % 2) * 512, 512)
            tps = s.tA if oc % 2 == 0 else s.tB
            for k_ in (("tpsO", oc % 2), ("tps", oc % 2)):
                d_ = s.slot.get(k_)
                if d_:
                    s.wait("tensor", *d_)
            s.wait("tensor", s.s_act, s.slot[("dpsO", oc % 2)][1])
            dsts = [tps[:, j * P:(j + 1) * P] for j in range(4)]
            srcs = [wob[:, j * P:(j + 1) * P] for j in range(4)]
            def ftr(E, srcs=srcs, dsts=dsts):
                for d_, a_ in zip(dsts, srcs):
                    tr = E.transpose(d_, a_, s.ident_sb[:, :])
                return tr
            ttok = s.op("tensor", s.s_pe, ftr)
            s.slot[("wob", oc % 2)] = (s.s_pe, ttok)
            s.wait("vector", s.s_pe, ttok)
            dst = r1bv[:, 0:4, oc * P:(oc + 1) * P]
            src = tps.ap().rearrange("p (u q) -> p u q", q=P)[:, 0:4, :]
            xb = xtm[:, 0:4, oc * P:(oc + 1) * P]
            rtok = s.op("vector", s.s_dve,
                        lambda E, d=dst, a=src, b=xb: E.tensor_tensor(d, a, b, AluOpType.add))
            s.slot[("tpsO", oc % 2)] = (s.s_dve, rtok)

        for oc in range(KC):
            ring = oc % 2
            wo = s.v1(s.o_wo + ring * D, D)
            rdy = wo_rdy[oc]
            s.wait("tensor", rdy[0], rdy[1])
            dps = s.pA if oc % 2 == 0 else s.pB
            depo = s.slot.get(("dpsO", oc % 2))
            if depo:
                s.wait("tensor", *depo)
            lhs = [wo[:, ic * P:(ic + 1) * P] for ic in range(KC)]
            rhs = [attn[:, ic, :] for ic in range(KC)]
            def fmm(E, lhs=lhs, rhs=rhs, dps=dps):
                for ic in range(KC):
                    mm = E.matmul(dps[:, :], lhs[ic], rhs[ic],
                                  start=(ic == 0), stop=(ic == KC - 1))
                return mm
            petok = s.op("tensor", s.s_pe, fmm)
            s.slot[("wo", ring)] = (s.s_pe, petok)
            if oc + 2 < KC:
                issue_wo(oc + 2)
            wob = s.v1(s.o_silu + (oc % 2) * 512, 512)
            depf = s.slot.get(("wob", oc % 2))
            if depf:
                s.wait("scalar", *depf)
            s.wait("scalar", s.s_pe, petok)
            etok = s.op("scalar", s.s_act, lambda E, d=wob, a=dps: E.copy(d, a[:, :]))
            s.slot[("dpsO", oc % 2)] = (s.s_act, etok)
            if oc > 0:
                o_tail(oc - 1)
        o_tail(KC - 1)
        for sub in range(NSUB):
            x1b = s.x1_bf[:, sub * D:(sub + 1) * D]
            aptok = self.emit_ln(r1b[:, sub * D:(sub + 1) * D], x1b)
            for half in range(2):
                tps = s.tA if half == 0 else s.tB
                for k_ in (("tps", half), ("tpsO", half), ("wob", half)):
                    d_ = s.slot.get(k_)
                    if d_:
                        s.wait("tensor", *d_)
                s.wait("tensor", s.s_dve, aptok)
                srcs = [x1b[:, (half * 8 + j) * P:(half * 8 + j + 1) * P] for j in range(8)]
                dsts = [tps[:, j * P:(j + 1) * P] for j in range(8)]
                def ftr(E, srcs=srcs, dsts=dsts):
                    for d_, a_ in zip(dsts, srcs):
                        tr = E.transpose(d_, a_, s.ident_sb[:, :])
                    return tr
                petok = s.op("tensor", s.s_pe, ftr)
                s.wait("vector", s.s_pe, petok)
                dst = x1fm[:, half * 8:(half + 1) * 8, sub * P:(sub + 1) * P]
                src = tps.ap().rearrange("p (j q) -> p j q", q=P)[:, 0:8, :]
                dtok = s.op("vector", s.s_dve, lambda E, d=dst, a=src: E.tensor_copy(d, a))
                s.slot[("tps", half)] = (s.s_dve, dtok)
        s.close()

    def stage_F(self, t, half):
        s = self
        s.open()
        subs = (2 * half, 2 * half + 1)
        x1fm = s.x1_fm.ap().rearrange("p (kc q) -> p kc q", q=TILE)
        hspf = s.v1(s.o_hspf, NIC * 256).rearrange("p (ic q) -> p ic q", q=256)
        def gps_of(sub_l, sl):
            if sub_l == 1:
                return s.sc0[:, 0:512]
            return s.pA[:, :] if sl % 2 == 0 else s.sc1[:, 0:512]

        def ups_of(sub_l, sl):
            if sub_l == 1:
                return s.sc0[:, 512:1024]
            return s.pB[:, :] if sl % 2 == 0 else s.sc1[:, 512:1024]

        wd_rdy = {}

        def issue_wd(idx):
            oc_, hu_ = divmod(idx, 2)
            ring_ = idx % 2
            dep_ = s.slot.get(("wdring", ring_))
            sem_a = s.dsem("wd%da" % ring_)
            sem_b = s.dsem("wd%db" % ring_)
            base = ring_ * 4096
            scr = s.sWd.ap()[oc_ * P:(oc_ + 1) * P, hu_ * 4096:(hu_ + 1) * 4096]
            ta = s.load_unit(("Wd", oc_, hu_, 0), sem_a, s.AR2[:, base:base + 2048],
                             scr[:, 0:2048], dep=dep_, queue="sync")
            tb = s.load_unit(("Wd", oc_, hu_, 1), sem_b, s.AR2[:, base + 2048:base + 4096],
                             scr[:, 2048:4096], dep=dep_, queue="scalar")
            wd_rdy[idx] = (ta, tb)
        issue_wd(0)
        issue_wd(1)

        sp_pending = []
        sp_state = {"cur": None, "tok": None, "ci": 0}
        tp_queue = []

        def flush_tp():
            while tp_queue:
                emit_hsp_transposes(*tp_queue.pop(0))

        def emit_hsp_transposes(grp, sub_l, hslot, sptok):
            s.wait("tensor", s.s_dve, sptok)
            hspt = s.v1(s.o_hspt + hslot * D, D)
            for tph in range(2):
                tps = s.tA if tph == 0 else s.tB
                deptp = s.slot.get(("tps", tph))
                if deptp:
                    s.wait("tensor", *deptp)
                srcs = [hspt[:, (tph * 8 + j) * P:(tph * 8 + j + 1) * P] for j in range(8)]
                dsts = [tps[:, j * P:(j + 1) * P] for j in range(8)]
                def ftr(E, srcs=srcs, dsts=dsts):
                    for d_, a_ in zip(dsts, srcs):
                        tr = E.transpose(d_, a_, s.ident_sb[:, :])
                    return tr
                petok = s.op("tensor", s.s_pe, ftr)
                s.wait("scalar", s.s_pe, petok)
                dst = hspf[:, grp * 16 + tph * 8:grp * 16 + (tph + 1) * 8, sub_l * P:(sub_l + 1) * P]
                src = tps.ap().rearrange("p (j q) -> p j q", q=P)[:, 0:8, :]
                etok = s.op("scalar", s.s_act, lambda E, d=dst, a=src: E.copy(d, a))
                s.slot[("tps", tph)] = (s.s_act, etok)
            s.slot[("hspt", hslot)] = (s.s_pe, s.s_pe.n)

        def emit_sparsify(k):
            while k > 0:
                if sp_state["cur"] is None:
                    if not sp_pending:
                        return
                    tag, grp, sub_l, mk_builders = sp_pending.pop(0)
                    hslot = sp_state["ci"] % 2
                    sp_state["ci"] += 1
                    hw = s.slot.get(("hbw", grp % 2, sub_l))
                    if hw:
                        s.wait("vector", *hw)
                    dep = s.slot.get(("hspt", hslot))
                    if dep:
                        s.wait("vector", *dep)
                    sp_state["cur"] = (grp, sub_l, hslot, mk_builders(hslot), 0)
                grp, sub_l, hslot, builders, i = sp_state["cur"]
                if sp_state["tok"]:
                    s.wait("vector", s.s_dve, sp_state["tok"])
                sp_state["tok"] = s.op("vector", s.s_dve, builders[i])
                i += 1
                k -= 1
                if i == len(builders):
                    tp_queue.append((grp, sub_l, hslot, sp_state["tok"]))
                    sp_state["cur"] = None
                else:
                    sp_state["cur"] = (grp, sub_l, hslot, builders, i)

        gu_rdy = {}

        def issue_gu(ci):
            sl_, q4_ = divmod(ci, 4)
            ring_ = ci % 4
            out = {}
            for (nm, o_ring, sW, q_) in (("Wg", s.o_wg, s.sWg, "sync"),
                                         ("Wu", s.o_wu, s.sWu, "scalar")):
                dst_ = s.v1(o_ring + ring_ * D, D)
                dep_ = s.slot.get((nm + "ring", ring_))
                out[nm] = s.load_unit((nm, sl_, q4_), s.dsem(nm + "r%d" % ring_), dst_,
                                      sW.ap()[sl_ * P:(sl_ + 1) * P, q4_ * 4 * 512:(q4_ + 1) * 4 * 512],
                                      dep=dep_, queue=q_)
            gu_rdy[ci] = out
        for ci0 in range(3):
            issue_gu(ci0)
        for sl in range(NSL):
            emit_sparsify(10)
            flush_tp()
            gtok = utok = None
            for q4 in range(4):
                ci = sl * 4 + q4
                ring = ci % 4
                if ci + 3 < NSL * 4:
                    issue_gu(ci + 3)
                rds = gu_rdy[ci]
                s.wait("tensor", rds["Wg"][0], rds["Wg"][1])
                s.wait("tensor", rds["Wu"][0], rds["Wu"][1])
                wg_c = s.v1(s.o_wg + ring * D, D)
                wu_c = s.v1(s.o_wu + ring * D, D)
                tok = None
                for sub_l in (1, 0):
                    sub = subs[sub_l]
                    if q4 == 0:
                        keys = ([("gps0", sl % 2), ("ups0", sl % 2)] if sub_l == 0
                                else [("gps1",), ("ups1",)])
                        for k_ in keys:
                            d_ = s.slot.get(k_)
                            if d_:
                                s.wait("tensor", *d_)
                    lhs = [x1fm[:, q4 * 4 + j, sub * P:(sub + 1) * P] for j in range(4)]
                    g_ps, u_ps = gps_of(sub_l, sl), ups_of(sub_l, sl)
                    def fmm(E, lhs=lhs, wg_c=wg_c, wu_c=wu_c, g_ps=g_ps, u_ps=u_ps, q4=q4):
                        for j in range(4):
                            st_f = (q4 == 0 and j == 0)
                            sp_f = (q4 == 3 and j == 3)
                            mm = E.matmul(g_ps, lhs[j], wg_c[:, j * 512:(j + 1) * 512],
                                          start=st_f, stop=sp_f)
                            mm2 = E.matmul(u_ps, lhs[j], wu_c[:, j * 512:(j + 1) * 512],
                                           start=st_f, stop=sp_f)
                        return mm2
                    tok = s.op("tensor", s.s_pe, fmm)
                    if q4 == 3:
                        if sub_l == 0:
                            gtok = tok
                        else:
                            utok = tok
                s.slot[("Wgring", ring)] = (s.s_pe, tok)
                s.slot[("Wuring", ring)] = (s.s_pe, tok)
            mm_done = {subs[0]: gtok, subs[1]: utok}
            sl_l = sl % 4
            grp = sl // 4
            for sub_l in (1, 0):
                sub = subs[sub_l]
                silu = s.v1(s.o_silu + sub_l * 512, 512)
                dw = s.slot.get(("silu_w", sub_l))
                if dw:
                    s.wait("scalar", *dw)
                s.wait("scalar", s.s_pe, mm_done[sub])
                g_ps, u_ps = gps_of(sub_l, sl), ups_of(sub_l, sl)
                stok = s.op("scalar", s.s_act, lambda E, d=silu, a=g_ps: E.activation(d, a, AF.Silu))
                s.slot[("gps0", sl % 2) if sub_l == 0 else ("gps1",)] = (s.s_act, stok)
                hb = s.v1(s.o_hbuf + (grp % 2) * 2 * D + sub_l * D + sl_l * 512, 512)
                s.wait("vector", s.s_act, stok)
                s.wait("vector", s.s_pe, mm_done[sub])
                htok = s.op("vector", s.s_dve,
                            lambda E, d=hb, a=u_ps, b=silu: E.tensor_tensor(d, a, b, AluOpType.mult))
                s.slot[("ups0", sl % 2) if sub_l == 0 else ("ups1",)] = (s.s_dve, htok)
                s.slot[("hbw", grp % 2, sub_l)] = (s.s_dve, htok)
                s.slot[("silu_w", sub_l)] = (s.s_dve, htok)
            if sl_l == 3:
                for sub_l in (1, 0):
                    sub = subs[sub_l]
                    def mk_builders(hslot, grp=grp, sub_l=sub_l):
                        hb = s.v1(s.o_hbuf + (grp % 2) * 2 * D + sub_l * D, D)
                        habs = s.v1(s.o_habs, D)
                        hspt = s.v1(s.o_hspt + hslot * D, D)
                        thr = s.v1(s.o_thr, 512)
                        t1_ = s.v1(s.o_thr + 512, 512)
                        t2_ = s.v1(s.o_thr + 1024, 512)
                        t3_ = s.v1(s.o_thr + 1536, 512)
                        h0, h1, h2, h3 = (habs[:, k:D:4] for k in range(4))
                        builders = [
                            lambda E, habs=habs, hb=hb: E.tensor_tensor(habs, hb, hb, AluOpType.mult),
                            lambda E, t1_=t1_, h0=h0, h1=h1: E.tensor_tensor(t1_, h0, h1, AluOpType.max),
                            lambda E, t2_=t2_, h0=h0, h1=h1: E.tensor_tensor(t2_, h0, h1, AluOpType.min),
                            lambda E, t3_=t3_, h2=h2, h3=h3: E.tensor_tensor(t3_, h2, h3, AluOpType.max),
                            lambda E, thr=thr, h2=h2, h3=h3: E.tensor_tensor(thr, h2, h3, AluOpType.min),
                            lambda E, t1_=t1_, t3_=t3_: E.tensor_tensor(t1_, t1_, t3_, AluOpType.min),
                            lambda E, t2_=t2_, thr=thr: E.tensor_tensor(t2_, t2_, thr, AluOpType.max),
                            lambda E, thr=thr, t1_=t1_, t2_=t2_: E.tensor_tensor(thr, t1_, t2_, AluOpType.max),
                        ]
                        for k in range(4):
                            tt = t1_ if k % 2 == 0 else t2_
                            hak = habs[:, k:D:4]
                            hbk = hb[:, k:D:4]
                            hok = hspt[:, k:D:4]
                            builders.append(lambda E, tt=tt, hak=hak, thr=thr:
                                            E.tensor_tensor(tt, hak, thr, AluOpType.is_ge))
                            builders.append(lambda E, hok=hok, tt=tt, hbk=hbk:
                                            E.tensor_tensor(hok, tt, hbk, AluOpType.mult))
                        return builders
                    sp_pending.append(("start", grp, sub_l, mk_builders))

        while sp_pending or sp_state["cur"] is not None:
            emit_sparsify(8)
        flush_tp()

        for k_ in (("tps", 0), ("tps", 1), ("gps0", 0), ("gps0", 1), ("ups0", 0),
                   ("ups0", 1), ("gps1",), ("ups1",)):
            d_ = s.slot.get(k_)
            if d_:
                s.wait("tensor", *d_)
        r2 = s.AR1[:, s.o_r2:s.o_r2 + 2 * D]

        def f3_tail(oc):
            dfm = s.v1(s.o_dfm + (oc % 2) * 256, 256)
            tps = s.tA if oc % 2 == 0 else s.tB
            for k_ in (("tpsF3", oc % 2), ("tps", oc % 2)):
                d_ = s.slot.get(k_)
                if d_:
                    s.wait("tensor", *d_)
            s.wait("tensor", s.s_act, s.slot[("dps", oc % 2)][1])
            dsts = [tps[:, j * P:(j + 1) * P] for j in range(2)]
            srcs = [dfm[:, j * P:(j + 1) * P] for j in range(2)]
            def ftr(E, srcs=srcs, dsts=dsts):
                for d_, a_ in zip(dsts, srcs):
                    tr = E.transpose(d_, a_, s.ident_sb[:, :])
                return tr
            petok = s.op("tensor", s.s_pe, ftr)
            s.slot[("dfm", oc % 2)] = (s.s_pe, petok)
            s.wait("vector", s.s_pe, petok)
            dst = r2.rearrange("p (u c) -> p u c", c=D)[:, 0:2, oc * P:(oc + 1) * P]
            src = tps.ap().rearrange("p (u q) -> p u q", q=P)[:, 0:2, :]
            xb = s.x1_bf.ap().rearrange("p (u c) -> p u c", c=D)[:, 2 * half:2 * half + 2, oc * P:(oc + 1) * P]
            rtok = s.op("vector", s.s_dve,
                        lambda E, d=dst, a=src, b=xb: E.tensor_tensor(d, a, b, AluOpType.add))
            s.slot[("tpsF3", oc % 2)] = (s.s_dve, rtok)

        for oc in range(KC):
            dps = s.pA if oc % 2 == 0 else s.pB
            dtok = None
            for hu in range(2):
                idx = oc * 2 + hu
                ring = idx % 2
                wd = s.AR2[:, ring * 4096:(ring + 1) * 4096]
                ta, tb = wd_rdy[idx]
                s.wait("tensor", ta[0], ta[1])
                s.wait("tensor", tb[0], tb[1])
                if hu == 0:
                    depo = s.slot.get(("dps", oc % 2))
                    if depo:
                        s.wait("tensor", *depo)
                lhs = [wd[:, k * P:(k + 1) * P] for k in range(32)]
                rhs = [hspf[:, hu * 32 + k, :] for k in range(32)]
                def fdn(E, lhs=lhs, rhs=rhs, dps=dps, hu=hu):
                    for k in range(32):
                        mm = E.matmul(dps[:, 0:256], lhs[k], rhs[k],
                                      start=(hu == 0 and k == 0), stop=(hu == 1 and k == 31))
                    return mm
                dtok = s.op("tensor", s.s_pe, fdn)
                s.slot[("wdring", ring)] = (s.s_pe, dtok)
                if idx + 2 < 2 * KC:
                    issue_wd(idx + 2)
            dfm = s.v1(s.o_dfm + (oc % 2) * 256, 256)
            s.wait("scalar", s.s_pe, dtok)
            depf = s.slot.get(("dfm", oc % 2))
            if depf:
                s.wait("scalar", *depf)
            etok = s.op("scalar", s.s_act, lambda E, d=dfm, a=dps: E.copy(d, a[:, 0:256]))
            s.slot[("dps", oc % 2)] = (s.s_act, etok)
            if oc > 0:
                f3_tail(oc - 1)
        f3_tail(KC - 1)
        for sub_l, sub in enumerate(subs):
            osb = s.AR3[:, sub_l * D:(sub_l + 1) * D]
            depo = s.slot.get(("osb_v", sub_l))
            if depo:
                s.wait("vector", *depo)
            aptok = self.emit_ln(r2[:, sub_l * D:(sub_l + 1) * D], osb)
            s.wait("sync", s.s_dve, aptok)
            osem = s.dsem("out%d" % sub_l)
            otok = s.dma("sync", osem, s.out.ap()[t * TILE + sub * P:t * TILE + (sub + 1) * P, :],
                         osb)
            s.slot[("osb", sub_l)] = (osem, otok)
            s.slot[("osb_v", sub_l)] = (osem, otok)
        s.close()

    def build(self):
        s = self
        s.open()
        s.dma("sync", s.s_cst, s.ident_sb[:, :], s.ident.ap())
        for e in ("tensor",):
            s.wait(e, s.s_cst, 16)
        s.close()
        s.stage_precast()
        s.stage_X(-1)
        s.stage_QKV(-1)
        for t in range(s.n_tiles):
            s.stage_X(t)
            s.stage_QKV(t)
            s.stage_A(t)
            if s.debug and t == 0:
                s.dbg("xfm", s.v1(s.o_xfm, KC * TILE), KC * TILE, BF16)
                s.dbg("q", s.v1(s.o_q, NH * 512), NH * 512, BF16)
                s.dbg("kroll", s.k_roll[:, :], 2 * NKV * 512, BF16)
                s.dbg("vroll", s.v_roll[:, :], 2 * NSUB * 512, BF16)
                s.dbg("attn", s.AR2[:, :], 8192, BF16)
            s.stage_O(t)
            if s.debug and t == 0:
                s.dbg("x1", s.x1_bf[:, :], NSUB * D, BF16)
                s.dbg("x1fm", s.x1_fm[:, :], KC * TILE, BF16)
            for half in range(2):
                s.stage_F(t, half)
            if s.debug and t == 0:
                s.dbg("hspf", s.v1(s.o_hspf, NIC * 256), NIC * 256, BF16)
        s.open()
        def fin(E):
            for nm, sm in s.dsems.items():
                if nm.endswith("_wb") or nm.startswith("out"):
                    E.wait_ge(sm.h, sm.n)
            E.wait_ge(s.s_dbg.h, s.s_dbg.n)
        s.stage.add("sync", fin)
        s.close()
        return s.nc


def build_masks(seq_start: bool) -> np.ndarray:
    m = np.zeros((5, P, MW), dtype=np.float32)
    r = np.arange(P)[:, None]
    kw = np.arange(MW)[None, :]
    band = (kw >= r + 1) & (kw <= r + WIN)
    for v in range(5):
        lo = (WIN - P * v) if (seq_start and v < 4) else 0
        m[v] = (band & (kw >= lo)).astype(np.float32)
    return np.ascontiguousarray(m.transpose(1, 0, 2).reshape(P, 5 * MW)).astype(BB)


def kernel(L0_attn_hidden_states, L0_attn_Wq, L0_attn_Wk, L0_attn_Wv, L0_attn_Wo,
           L0_ffn_w_gate, L0_ffn_w_up, L0_ffn_w_down):
    x = np.ascontiguousarray(np.asarray(L0_attn_hidden_states, dtype=np.float32))
    B, S, _ = x.shape
    n_cores = 8
    per = (B * S) // n_cores
    mk = MK(n_tiles=NT, debug=False)
    nc = mk.build()
    ident = np.eye(P, dtype=BB)
    w = dict(L0_attn_Wq=np.asarray(L0_attn_Wq, np.float32),
             L0_attn_Wk=np.asarray(L0_attn_Wk, np.float32),
             L0_attn_Wv=np.asarray(L0_attn_Wv, np.float32),
             L0_attn_Wo=np.asarray(L0_attn_Wo, np.float32),
             L0_ffn_w_gate=np.asarray(L0_ffn_w_gate, np.float32),
             L0_ffn_w_up=np.asarray(L0_ffn_w_up, np.float32),
             L0_ffn_w_down=np.asarray(L0_ffn_w_down, np.float32))
    xf = x.reshape(B * S, D)
    in_maps = []
    for c in range(n_cores):
        t0 = c * per
        seq_start = (t0 % S) == 0
        halo = np.zeros((HALO, D), np.float32) if seq_start else xf[t0 - HALO:t0]
        xs = np.ascontiguousarray(np.concatenate([halo, xf[t0:t0 + per]], axis=0))
        im = {"x": xs, "mask01": build_masks(seq_start), "ident": ident}
        im.update(w)
        in_maps.append(im)
    res = run_bass_kernel_spmd(nc, in_maps, core_ids=list(range(n_cores)))
    outs = [res.results[c]["out"] for c in range(n_cores)]
    return np.concatenate(outs, axis=0).reshape(B, S, D).astype(np.float32)
